# version 03
# baseline (speedup 1.0000x reference)
import numpy as np
import ml_dtypes
from contextlib import ExitStack

import concourse.bass as bass
import concourse.mybir as mybir
from concourse.bass_utils import run_bass_kernel_spmd

F32 = mybir.dt.float32
BF16 = mybir.dt.bfloat16
I32 = mybir.dt.int32
AF = mybir.ActivationFunctionType
ALU = mybir.AluOpType
AX = mybir.AxisListType

NTOK = 4096
D = 1024
H = 16
HD = 64
FF = 4096
HQ = 3072
TT = NTOK // 128
RING = 16
CH = 256
NCH = NTOK // CH
QS = 1.0 / 127.0

WINSET = set(range(0, 33)) | {48, 64, 96, 128, 192, 256}
SPARSE = [384, 512, 768, 1024, 1536]
SP_M = [d // 128 for d in SPARSE]
OFFSETS = tuple(sorted(set(range(0, 33)) | {48, 64, 96, 128, 192, 256, 384, 512, 768, 1024, 1536}))
CLAMP = 480.0

DEBUG = ()


def host_pb_win(pos_bias):
    idx_of = {d: i for i, d in enumerate(OFFSETS)}
    pb = np.asarray(pos_bias, dtype=np.float32)
    out = np.full((16, 128, 3 * 128), -1e9, dtype=np.float32)
    tq = np.arange(128)
    for wt in range(3):
        for j in range(128):
            dl = tq - j + 128 * wt
            for d in WINSET:
                m = dl == d
                if m.any():
                    out[:, j, wt * 128 + np.nonzero(m)[0]] = pb[idx_of[d], :][:, None]
    return out


def host_pb_sparse(pos_bias):
    idx_of = {d: i for i, d in enumerate(OFFSETS)}
    pb = np.asarray(pos_bias, dtype=np.float32)
    out = np.full((13, 128, 128), -1e9, dtype=np.float32)
    for cv in range(13):
        for bi, dlt in enumerate(SPARSE):
            lo = dlt - 128 * cv
            if lo >= 128:
                continue
            lo = max(0, lo)
            for h in range(16):
                out[cv, lo:, 8 * h + bi] = pb[idx_of[dlt], h]
    return out


class Sem:
    __slots__ = ("h", "val")


class Builder:
    def __init__(self, nc, ctx):
        self.nc = nc
        self.ctx = ctx
        self.streams = {e: [] for e in ("tensor", "vector", "scalar", "gpsimd", "sync")}
        self.self_sems = {}
        self.all_ev = []
        self.all_dma = []
        self._n = 0

    def sem(self, name=None):
        s = Sem()
        s.h = self.ctx.enter_context(self.nc.semaphore(name or f"s{self._n}"))
        s.val = 0
        self._n += 1
        return s

    def op(self, eng, fn, waits=(), inc=None, inc_n=1, self_ser=True):
        wl = [(s.h, v) for (s, v) in waits]
        ss = self.self_sems.get(eng) if self_ser else None
        if ss is not None:
            wl.append((ss.h, ss.val))
            if inc is None:
                inc = ss
                inc_n = 1
        if inc is not None:
            inc.val += inc_n
        self.streams[eng].append((wl, fn, None if inc is None else (inc.h, inc_n)))

    def wait(self, eng, waits):
        self.streams[eng].append(([(s.h, v) for (s, v) in waits], None, None))

    def dma(self, eng, out, in_, waits=(), inc=None, **kw):
        self.op(eng, lambda e: e.dma_start(out=out, in_=in_, **kw), waits, inc, 16, self_ser=False)

    def trans(self, eng, out, in_, waits=(), inc=None):
        waits = list(waits) + [(s, s.val) for s in self.all_ev] + [(s, s.val) for s in self.all_dma]
        self.op(eng, lambda e: e.dma_start_transpose(out=out, in_=in_), waits, inc, 16, self_ser=False)

    def barrier(self, dma_sems):
        for eng, s in dma_sems.items():
            self.wait(eng, [(s, s.val)])
        bsem = self.sem()
        bsem.val = 5
        for e in self.streams:
            self.streams[e].append(([], lambda eng: eng.nop(), (bsem.h, 1)))
            self.streams[e].append(([(bsem.h, 5)], None, None))

    def emit(self, block):
        def make(eng_name):
            def body(e):
                for wl, fn, inc in self.streams[eng_name]:
                    for (h, v) in wl:
                        e.wait_ge(h, v)
                    if fn is None:
                        continue
                    r = fn(e)
                    if inc is not None:
                        r.then_inc(inc[0], inc[1])
            return body
        block.tensor(make("tensor"))
        block.vector(make("vector"))
        block.scalar(make("scalar"))
        block.gpsimd(make("gpsimd"))
        block.sync(make("sync"))


def at(*sems):
    return [(s, s.val) for s in sems]


def build_nc(debug=()):
    nc = bass.Bass("TRN2", target_bir_lowering=False)

    def din(name, shape, dt=F32):
        return nc.dram_tensor(name, shape, dt, kind="ExternalInput").ap()

    x_h = din("x", [TT, 128, D])
    qkvw_h = din("qkv_w", [D, 3 * D])
    qkvb_h = din("qkv_b", [3 * D])
    gatew_h = din("gate_w", [D, D])
    gateb_h = din("gate_b", [D])
    outw_h = din("out_w", [D, D])
    outb_h = din("out_b", [D])
    ln1g_h = din("ln1_g", [D])
    ln1b_h = din("ln1_b", [D])
    ln2g_h = din("ln2_g", [D])
    ln2b_h = din("ln2_b", [D])
    fc1w_h = din("fc1_w", [D, FF])
    fc1b_h = din("fc1_b", [FF])
    fc2w_h = din("fc2_w", [FF, D])
    fc2b_h = din("fc2_b", [D])
    wg_h = din("w_gate", [HQ, D], I32)
    wu_h = din("w_up", [HQ, D], I32)
    wd_h = din("w_down", [D, HQ], I32)
    pbw_h = din("pb_win", [16, 128, 384])
    ident_h = din("ident", [128, 128])
    pbs_h = din("pb_sp", [13, 128, 128])

    out_h = nc.dram_tensor("out", [TT, 128, D], F32, kind="ExternalOutput").ap()

    def scratch(name, shape, dt=BF16):
        if name in debug:
            return nc.dram_tensor(name, shape, dt, kind="ExternalOutput").ap()
        return nc.dram_tensor(name, shape, dt).ap()

    xn_tm_h = scratch("xn_tm", [TT, 128, D])
    xn1_fm_h = scratch("xn1_fm", [8, 128, NTOK])
    g_tm_h = scratch("g_tm", [TT, 128, D])
    attn_tm_h = scratch("attn_tm", [TT, 128, D])
    x2_tm_h = scratch("x2_tm", [TT, 128, D])
    xn2_fm_h = scratch("xn2_fm", [8, 128, NTOK])
    xh2_fm_h = scratch("xh2_fm", [8, 128, NTOK])

    ctx = ExitStack()
    with ctx:
        b = Builder(nc, ctx)
        ones_row = ctx.enter_context(nc.sbuf_tensor("ones_row", [1, 128], BF16))
        one1 = ctx.enter_context(nc.sbuf_tensor("one1", [1, 512], BF16))
        s_stats = ctx.enter_context(nc.sbuf_tensor("s_stats", [128, 16], F32))
        s_junk = ctx.enter_context(nc.sbuf_tensor("s_junk", [128, 1024], BF16))
        s_eps = ctx.enter_context(nc.sbuf_tensor("s_eps", [128, 1], F32))
        s_id = ctx.enter_context(nc.sbuf_tensor("s_id", [128, 128], BF16))

        dma_g = b.sem("dma_g")
        dma_s = b.sem("dma_s")
        dma_a = b.sem("dma_a")
        ev_t = b.sem("ev_t")
        ev_v = b.sem("ev_v")
        ev_a = b.sem("ev_a")
        ev_p = b.sem("ev_p")
        DMAS = {"gpsimd": dma_g, "sync": dma_s, "scalar": dma_a}
        b.self_sems = {"vector": ev_v, "scalar": ev_a, "gpsimd": ev_p}
        b.all_ev = [ev_t, ev_v, ev_a]
        b.all_dma = [dma_g]

        b.op("gpsimd", lambda e: e.memset(ones_row[:], 1.0), inc=ev_p)
        b.op("gpsimd", lambda e: e.memset(s_eps[:], 1e-5), inc=ev_p)
        b.dma("gpsimd", s_id[:], ident_h, inc=dma_g)
        b.op("gpsimd", lambda e: e.memset(one1[:], 1.0), inc=ev_p)

        def ln_stats(src_ap, ln1=False):
            st = s_stats
            b.op("vector", lambda e: e.reduce_sum(out=st[:, 0:1], in_=src_ap, axis=AX.X), at(ev_a), inc=ev_v)
            b.op("scalar", lambda e: e.activation(
                out=s_junk[:, 0:src_ap.shape[-1]], in_=src_ap, func=AF.Square,
                accum_out=st[:, 1:2]), at(ev_v), inc=ev_a)
            b.op("vector", lambda e: e.tensor_scalar_mul(out=st[:, 0:1], in0=st[:, 0:1], scalar1=1.0 / D), inc=ev_v)
            b.op("vector", lambda e: e.tensor_scalar_mul(out=st[:, 1:2], in0=st[:, 1:2], scalar1=1.0 / D), at(ev_a), inc=ev_v)
            b.op("vector", lambda e: e.tensor_tensor(out=st[:, 2:3], in0=st[:, 0:1], in1=st[:, 0:1], op=ALU.mult), inc=ev_v)
            b.op("vector", lambda e: e.tensor_tensor(out=st[:, 2:3], in0=st[:, 1:2], in1=st[:, 2:3], op=ALU.subtract), inc=ev_v)
            b.op("scalar", lambda e: e.activation(out=st[:, 3:4], in_=st[:, 2:3], func=AF.Sqrt, bias=s_eps[:], scale=1.0),
                 at(ev_v, ev_p), inc=ev_a)
            b.op("vector", lambda e: e.reciprocal(out=st[:, 4:5], in_=st[:, 3:4]), at(ev_a), inc=ev_v)
            b.op("vector", lambda e: e.tensor_tensor(out=st[:, 5:6], in0=st[:, 0:1], in1=st[:, 4:5], op=ALU.mult), inc=ev_v)
            if ln1:
                b.op("vector", lambda e: e.tensor_tensor(out=st[:, 6:7], in0=st[:, 4:5], in1=st[:, 4:5], op=ALU.mult), inc=ev_v)
                b.op("vector", lambda e: e.tensor_tensor(out=st[:, 6:7], in0=st[:, 2:3], in1=st[:, 6:7], op=ALU.mult), inc=ev_v)
                b.op("scalar", lambda e: e.activation(out=st[:, 7:8], in_=st[:, 6:7], func=AF.Sqrt, bias=s_eps[:], scale=1.0),
                     at(ev_v, ev_p), inc=ev_a)
                b.op("vector", lambda e: e.reciprocal(out=st[:, 8:9], in_=st[:, 7:8]), at(ev_a), inc=ev_v)
                b.op("vector", lambda e: e.tensor_tensor(out=st[:, 9:10], in0=st[:, 4:5], in1=st[:, 8:9], op=ALU.mult), inc=ev_v)
                b.op("vector", lambda e: e.tensor_tensor(out=st[:, 10:11], in0=st[:, 0:1], in1=st[:, 9:10], op=ALU.mult), inc=ev_v)

        def ln_apply(dst_ap, src_ap, rstd_col, mur_col):
            st = s_stats
            mur_view = st[:, mur_col:mur_col + 1].broadcast_to([128, dst_ap.shape[-1]])
            b.op("vector", lambda e: e.scalar_tensor_tensor(
                out=dst_ap, in0=src_ap, scalar=st[:, rstd_col:rstd_col + 1], in1=mur_view,
                op0=ALU.mult, op1=ALU.subtract), inc=ev_v)

        with ExitStack() as p1:
            sb1 = lambda name, shape, dt: p1.enter_context(nc.sbuf_tensor(name, shape, dt))
            ps1 = lambda name: p1.enter_context(nc.psum_tensor(name, [128, 512], F32))
            wgT = sb1("wgT", [128, 8, HQ], BF16)
            wuT = sb1("wuT", [128, 8, HQ], BF16)
            wdT = sb1("wdT", [128, 24, D], BF16)
            g1t = sb1("g1t", [128, 8], F32)
            b1t = sb1("b1t", [128, 8], F32)
            pg = ps1("pg")
            pu = ps1("pu")
            pd = [ps1(f"pd{i}") for i in range(4)]

            with ExitStack() as pro:
                stg = pro.enter_context(nc.sbuf_tensor("stg", [128, 24 * D], BF16))
                rowbuf1 = pro.enter_context(nc.sbuf_tensor("rowbuf1", [1, 2 * D], BF16))
                b.dma("gpsimd", rowbuf1[0:1, 0:D], ln1g_h.unsqueeze(0), at(ev_p), inc=dma_g)
                b.dma("gpsimd", rowbuf1[0:1, D:2 * D], ln1b_h.unsqueeze(0), inc=dma_g)
                gv = dma_g.val
                for i in range(8):
                    b.op("tensor", lambda e, i=i: e.matmul(
                        pg[:, i:i + 1], rowbuf1[0:1, 128 * i:128 * i + 128], one1[0:1, 0:1],
                        start=True, stop=True), [(dma_g, gv)] if i == 0 else (), inc=ev_t)
                    b.op("tensor", lambda e, i=i: e.matmul(
                        pg[:, 8 + i:9 + i], rowbuf1[0:1, D + 128 * i:D + 128 * i + 128], one1[0:1, 0:1],
                        start=True, stop=True), inc=ev_t)
                b.op("scalar", lambda e: e.copy(out=g1t[:], in_=pg[:, 0:8]), at(ev_t), inc=ev_a)
                b.op("scalar", lambda e: e.copy(out=b1t[:], in_=pg[:, 8:16]), inc=ev_a)

                ptw_t = [pro.enter_context(nc.psum_tensor(f"ptw{i}", [128, 1024], BF16)) for i in range(2)]
                ptw = [p[:, 0:128] for p in ptw_t]
                widt = dma_g.val
                for (w_h, dstT, src_tiles, dst_chunks) in (
                        (wg_h, wgT, 24, 8), (wu_h, wuT, 24, 8), (wd_h, wdT, 8, 24)):
                    inner = 128 * dst_chunks
                    stg_v = stg[:].rearrange("p (c d) -> p c d", d=inner)
                    b.dma("gpsimd", stg_v[:, 0:src_tiles, :],
                          w_h.rearrange("(c p) d -> p c d", p=128), at(ev_v, ev_a), inc=dma_g)
                    gv = dma_g.val
                    nslot = 2
                    k = 0
                    for sc in range(src_tiles):
                        for dc in range(dst_chunks):
                            sl = k % nslot
                            w = [(dma_g, max(gv, widt))]
                            if k >= nslot:
                                w += at(ev_v, ev_a)
                            b.op("tensor", lambda e, sc=sc, dc=dc, sl=sl, stg_v=stg_v: e.transpose(
                                ptw[sl], stg_v[:, sc, 128 * dc:128 * dc + 128], s_id[:]),
                                w, inc=ev_t)
                            tvw = ev_t.val
                            eng = "vector" if k % 2 == 0 else "scalar"
                            if eng == "vector":
                                b.op("vector", lambda e, sc=sc, dc=dc, sl=sl, dstT=dstT: e.tensor_copy(
                                    out=dstT[:, dc, 128 * sc:128 * sc + 128], in_=ptw[sl]),
                                    [(ev_t, tvw)], inc=ev_v)
                            else:
                                b.op("scalar", lambda e, sc=sc, dc=dc, sl=sl, dstT=dstT: e.copy(
                                    out=dstT[:, dc, 128 * sc:128 * sc + 128], in_=ptw[sl]),
                                    [(ev_t, tvw)], inc=ev_a)
                            k += 1
            tr_done = at(ev_v, ev_a)

            s_x = sb1("s_x", [128, 2, 2, D], F32)
            s_xn = sb1("s_xn", [128, 2, 2, D], BF16)
            s_xh1 = sb1("s_xh1", [128, 2, 2, D], BF16)
            s_xnfm = sb1("s_xnfm", [128, 2, 8, CH], BF16)
            s_xh1fm = sb1("s_xh1fm", [128, 2, 8, CH], BF16)
            s_xn1fm = sb1("s_xn1fm", [128, 8, CH], BF16)
            s_sg1 = sb1("s_sg1", [128, 2, CH], BF16)
            s_su = sb1("s_su", [128, 2, CH], BF16)
            s_hp = sb1("s_hp", [128, 2, CH], BF16)
            s_g = sb1("s_g", [128, 2, D], BF16)

            ev_v2 = b.sem("p1v2")
            ev_a2 = b.sem("p1a2")
            xn_ready = {}
            xload = {}

            ptr1_t = [p1.enter_context(nc.psum_tensor(f"ptr1{i}", [128, 1024], BF16)) for i in range(2)]
            ptr1 = [p[:, 0:128] for p in ptr1_t]
            lnv = {}

            def p1_A(ci):
                pp = ci % 2
                if ci == 0:
                    b.dma("gpsimd", s_x[:, 0, :, :],
                          x_h.rearrange("t p d -> p t d")[:, 0:2, :],
                          tr_done + at(ev_t), inc=dma_g)
                    xload[0] = dma_g.val
                if ci + 1 < NCH:
                    b.dma("gpsimd", s_x[:, (ci + 1) % 2, :, :],
                          x_h.rearrange("t p d -> p t d")[:, 2 * (ci + 1):2 * (ci + 1) + 2, :],
                          at(ev_v, ev_a), inc=dma_g)
                    xload[ci + 1] = dma_g.val
                b.wait("vector", [(dma_g, xload[ci])])
                b.wait("scalar", [(dma_g, xload[ci])])
                for tt in range(2):
                    src_ap = s_x[:, pp, tt, :]
                    ln_stats(src_ap, ln1=True)
                    ln_apply(s_xn[:, pp, tt, :], src_ap, 4, 5)
                    ln_apply(s_xh1[:, pp, tt, :], src_ap, 9, 10)
                lnv[ci] = ev_v.val

            def p1_A_trans(ci):
                pp = ci % 2
                k = 0
                for tt in range(2):
                    for (srcT, dstT2) in ((s_xn, s_xnfm), (s_xh1, s_xh1fm)):
                        for ft in range(8):
                            sl = k % 2
                            w = [(ev_v, lnv[ci])]
                            if k >= 2:
                                w += at(ev_v, ev_a)
                            b.op("tensor", lambda e, tt=tt, ft=ft, sl=sl, srcT=srcT, pp=pp: e.transpose(
                                ptr1[sl], srcT[:, pp, tt, 128 * ft:128 * ft + 128], s_id[:]),
                                w, inc=ev_t)
                            tvw = ev_t.val
                            if k % 2 == 0:
                                b.op("vector", lambda e, tt=tt, ft=ft, sl=sl, dstT2=dstT2, pp=pp: e.tensor_copy(
                                    out=dstT2[:, pp, ft, 128 * tt:128 * tt + 128], in_=ptr1[sl]),
                                    [(ev_t, tvw)], inc=ev_v)
                            else:
                                b.op("scalar", lambda e, tt=tt, ft=ft, sl=sl, dstT2=dstT2, pp=pp: e.copy(
                                    out=dstT2[:, pp, ft, 128 * tt:128 * tt + 128], in_=ptr1[sl]),
                                    [(ev_t, tvw)], inc=ev_a)
                            k += 1
                xn_ready[ci] = at(ev_v, ev_a)

            def p1_A_late(ci):
                pp = ci % 2
                for ft in range(8):
                    b.op("scalar", lambda e, ft=ft, pp=pp: e.activation(
                        out=s_xn1fm[:, ft, :], in_=s_xh1fm[:, pp, ft, :], func=AF.Identity,
                        bias=b1t[:, ft:ft + 1], scale=g1t[:, ft:ft + 1]),
                        (xn_ready[ci] + [(dma_g, dma_g.val)]) if ft == 0 else (), inc=ev_a)
                b.dma("gpsimd", xn_tm_h[2 * ci:2 * ci + 2].rearrange("t p d -> p t d"),
                      s_xn[:, pp, :, :], xn_ready[ci] + at(ev_v), inc=dma_g)
                b.dma("gpsimd", xn1_fm_h[:, :, CH * ci:CH * (ci + 1)].rearrange("f p t -> p f t"),
                      s_xn1fm[:], at(ev_a), inc=dma_g)

            def p1_B(ci):
                pp = ci % 2
                for hfc in range(24):
                    hp_i = hfc % 2
                    for dc in range(8):
                        w = ()
                        if dc == 0:
                            w = at(ev_a2, ev_v2) + (xn_ready[ci] if hfc == 0 else [])
                            if hfc == 0 and ci == 0:
                                w = w + tr_done
                        b.op("tensor", lambda e, dc=dc, hfc=hfc, pp=pp: e.matmul(
                            pg[:, 0:CH], wgT[:, dc, 128 * hfc:128 * hfc + 128], s_xnfm[:, pp, dc, :],
                            start=(dc == 0), stop=(dc == 7)), w, inc=ev_t)
                    for dc in range(8):
                        b.op("tensor", lambda e, dc=dc, hfc=hfc, pp=pp: e.matmul(
                            pu[:, 0:CH], wuT[:, dc, 128 * hfc:128 * hfc + 128], s_xnfm[:, pp, dc, :],
                            start=(dc == 0), stop=(dc == 7)), inc=ev_t)
                    tv = ev_t.val
                    b.op("scalar", lambda e, hp_i=hp_i: e.activation(
                        out=s_sg1[:, hp_i, :], in_=pg[:, 0:CH], func=AF.Silu, scale=QS),
                        [(ev_t, tv)] + at(ev_v2), inc=ev_a2, self_ser=False)
                    b.op("vector", lambda e, hp_i=hp_i: e.tensor_copy(
                        out=s_su[:, hp_i, :], in_=pu[:, 0:CH]),
                        [(ev_t, tv)] + at(ev_v2), inc=ev_v2, self_ser=False)
                    b.op("vector", lambda e, hp_i=hp_i: e.tensor_tensor(
                        out=s_hp[:, hp_i, :], in0=s_sg1[:, hp_i, :], in1=s_su[:, hp_i, :],
                        op=ALU.mult), at(ev_a2, ev_v2), inc=ev_v2, self_ser=False)
                    hv = ev_v2.val
                    for tt in range(2):
                        for half in range(2):
                            w = ()
                            if tt == 0 and half == 0:
                                w = [(ev_v2, hv)] + (at(ev_a2) if hfc == 0 else [])
                            b.op("tensor", lambda e, tt=tt, half=half, hfc=hfc, hp_i=hp_i: e.matmul(
                                pd[2 * tt + half][:, 0:512],
                                s_hp[:, hp_i, 128 * tt:128 * tt + 128],
                                wdT[:, hfc, 512 * half:512 * half + 512],
                                start=(hfc == 0), stop=(hfc == 23)), w, inc=ev_t)
                tv = ev_t.val
                for tt in range(2):
                    for half in range(2):
                        b.op("scalar", lambda e, tt=tt, half=half: e.activation(
                            out=s_g[:, tt, 512 * half:512 * half + 512],
                            in_=pd[2 * tt + half][:, 0:512], func=AF.Silu, scale=QS * QS),
                            [(ev_t, tv), (dma_g, dma_g.val)] + at(ev_v2), inc=ev_a2, self_ser=False)
                b.dma("gpsimd", g_tm_h[2 * ci:2 * ci + 2].rearrange("t p d -> p t d"), s_g[:],
                      at(ev_a2), inc=dma_g)

            p1_A(0)
            p1_A_trans(0)
            for ci in range(NCH):
                if ci + 1 < NCH:
                    p1_A(ci + 1)
                p1_B(ci)
                if ci + 1 < NCH:
                    p1_A_trans(ci + 1)
                p1_A_late(ci)

            b.barrier(DMAS)

        with ExitStack() as p2:
            sb2 = lambda name, shape, dt: p2.enter_context(nc.sbuf_tensor(name, shape, dt))
            ps2 = lambda name: p2.enter_context(nc.psum_tensor(name, [128, 512], F32))
            s_qkvw = sb2("s_qkvw", [128, 8, 3 * D], BF16)
            qkb_t = sb2("qkb_t", [128, 16], F32)
            vb_bc = sb2("vb_bc", [128, D], F32)
            kfm = sb2("kfm", [128, 4, 8, 128], BF16)
            ktm = sb2("ktm", [128, RING, D], BF16)
            vtm = sb2("vtm", [128, RING, H, 65], BF16)
            pb8w = sb2("pb8w", [128, 16, 384], F32)
            pb8e = sb2("pb8e", [128, 128], F32)
            pb8s = sb2("pb8s", [128, 13, 128], F32)
            s_xn1 = sb2("s_xn1", [128, 2, 8, CH], BF16)
            s_qfm = sb2("s_qfm", [128, 8, CH], BF16)
            s_qtm = sb2("s_qtm", [128, 2, D], BF16)
            s_pt = sb2("s_pt", [128, 4, 384], BF16)
            s_acc = sb2("s_acc", [128, H, HD], F32)
            s_den = sb2("s_den", [128, 16], F32)
            s_sprod = sb2("s_sprod", [128, D], BF16)
            s_ssc = sb2("s_ssc", [128, 128], F32)
            s_psp = sb2("s_psp", [128, 128], BF16)
            s_attn = sb2("s_attn", [128, 2, D], BF16)
            psA = [ps2("psA0"), ps2("psA1")]
            psB = [ps2("psB0"), ps2("psB1")]
            pav = ps2("pav")
            pq = ps2("pq")
            pv0 = ps2("pv0")
            pv1 = ps2("pv1")

            b.dma("gpsimd", s_qkvw[:], qkvw_h.rearrange("(c p) f -> p c f", p=128), inc=dma_g)
            with ExitStack() as pro:
                rowbuf2v = pro.enter_context(nc.sbuf_tensor("rowbuf2", [1, 3 * D], BF16))
                b.dma("gpsimd", rowbuf2v[0:1, :], qkvb_h.unsqueeze(0), inc=dma_g)
                gv = dma_g.val
                for i in range(16):
                    b.op("tensor", lambda e, i=i: e.matmul(
                        pq[:, i:i + 1], rowbuf2v[0:1, 128 * i:128 * i + 128], one1[0:1, 0:1],
                        start=True, stop=True), [(dma_g, gv)] if i == 0 else (), inc=ev_t)
                b.op("scalar", lambda e: e.copy(out=qkb_t[:], in_=pq[:, 0:16]), at(ev_t), inc=ev_a)
                for half in range(2):
                    b.op("tensor", lambda e, half=half: e.matmul(
                        pv0[:, 0:512], ones_row[0:1, :],
                        rowbuf2v[0:1, 2 * D + 512 * half:2 * D + 512 * half + 512],
                        start=True, stop=True), at(ev_a), inc=ev_t)
                    b.op("scalar", lambda e, half=half: e.copy(
                        out=vb_bc[:, 512 * half:512 * half + 512], in_=pv0[:, 0:512]),
                        at(ev_t), inc=ev_a)
            b.dma("gpsimd", pb8w[:], pbw_h.rearrange("h p w -> p h w"), inc=dma_g)
            b.dma("gpsimd", pb8s[:], pbs_h.rearrange("c p w -> p c w"), inc=dma_g)
            gv = dma_g.val
            b.op("vector", lambda e: e.tensor_scalar_mul(out=pb8w[:], in0=pb8w[:], scalar1=8.0),
                 [(dma_g, gv)], inc=ev_v)
            b.op("vector", lambda e: e.tensor_scalar_mul(out=pb8s[:], in0=pb8s[:], scalar1=8.0), inc=ev_v)
            b.op("gpsimd", lambda e: e.memset(pb8e[:], -8e9), inc=ev_p)
            b.op("gpsimd", lambda e: e.memset(s_ssc[:], 0.0), inc=ev_p)
            b.op("gpsimd", lambda e: e.memset(
                vtm[:].rearrange("p r h w -> p (r h) w")[:, :, 64:65], 1.0), inc=ev_p)
            for par in range(2):
                b.op("vector", lambda e, par=par: e.memset(psA[par][:, 0:384], 0.0), inc=ev_v)
                b.op("vector", lambda e, par=par: e.memset(psB[par][:, 0:384], 0.0), inc=ev_v)
            p2pro = at(dma_g, ev_v, ev_a, ev_p)

            for ci in range(NCH):
                pp = ci % 2
                b.dma("gpsimd", s_xn1[:, pp, :, :],
                      xn1_fm_h[:, :, CH * ci:CH * (ci + 1)].rearrange("f p t -> p f t"),
                      (p2pro if ci == 0 else []) + at(ev_t, ev_v, ev_a, dma_s, dma_a), inc=dma_g)
                xld = dma_g.val
                b.wait("vector", [(dma_g, xld)] + at(dma_s, dma_a))
                b.wait("scalar", [(dma_g, xld)] + at(dma_s, dma_a))
                for qf in range(8):
                    for dc in range(8):
                        b.op("tensor", lambda e, qf=qf, dc=dc, pp=pp: e.matmul(
                            pq[:, 0:CH], s_qkvw[:, dc, 128 * qf:128 * qf + 128], s_xn1[:, pp, dc, :],
                            start=(dc == 0), stop=(dc == 7)),
                            ([(dma_g, xld)] + at(ev_a)) if dc == 0 else (), inc=ev_t)
                    tv = ev_t.val
                    b.op("scalar", lambda e, qf=qf: e.activation(
                        out=s_qfm[:, qf, :], in_=pq[:, 0:CH], func=AF.Identity,
                        bias=qkb_t[:, qf:qf + 1], scale=1.0), [(ev_t, tv)], inc=ev_a)
                slot0 = (2 * ci) % 4
                for kf in range(8):
                    for dc in range(8):
                        b.op("tensor", lambda e, kf=kf, dc=dc, pp=pp: e.matmul(
                            pq[:, 0:CH], s_qkvw[:, dc, D + 128 * kf:D + 128 * kf + 128],
                            s_xn1[:, pp, dc, :],
                            start=(dc == 0), stop=(dc == 7)), at(ev_a) if dc == 0 else (), inc=ev_t)
                    tv = ev_t.val
                    b.op("scalar", lambda e, kf=kf, slot0=slot0: e.activation(
                        out=kfm[:, slot0:slot0 + 2, kf, :],
                        in_=pq[:, 0:CH].rearrange("p (s t) -> p s t", s=2), func=AF.Identity,
                        bias=qkb_t[:, 8 + kf:9 + kf], scale=1.0), [(ev_t, tv)], inc=ev_a)
                for tt in range(2):
                    pvp = pv0 if tt == 0 else pv1
                    slot = (2 * ci + tt) % RING
                    for half in range(2):
                        for dc in range(8):
                            b.op("tensor", lambda e, tt=tt, half=half, dc=dc, pp=pp, pvp=pvp: e.matmul(
                                pvp[:, 0:512],
                                s_xn1[:, pp, dc, 128 * tt:128 * tt + 128],
                                s_qkvw[:, dc, 2 * D + 512 * half:2 * D + 512 * half + 512],
                                start=(dc == 0), stop=(dc == 7)),
                                at(ev_v) if dc == 0 else (), inc=ev_t)
                        tv = ev_t.val
                        b.op("vector", lambda e, slot=slot, half=half, pvp=pvp: e.scalar_tensor_tensor(
                            out=vtm[:, slot, 8 * half:8 * half + 8, 0:64],
                            in0=pvp[:, 0:512].rearrange("p (h w) -> p h w", w=64),
                            scalar=1.0,
                            in1=vb_bc[:, 512 * half:512 * half + 512].rearrange("p (h w) -> p h w", w=64),
                            op0=ALU.mult, op1=ALU.add), [(ev_t, tv)] + at(ev_p), inc=ev_v)
                av = ev_a.val
                for tt in range(2):
                    slot = (2 * ci + tt) % RING
                    for ft in range(8):
                        q = "sync"
                        sem = dma_s
                        b.trans(q, s_qtm[:, tt, 128 * ft:128 * ft + 128],
                                s_qfm[:, ft, 128 * tt:128 * tt + 128],
                                [(ev_a, av)] + at(ev_v, ev_t), inc=sem)
                        b.trans(q, ktm[:, slot, 128 * ft:128 * ft + 128],
                                kfm[:, slot % 4, ft, :],
                                [(ev_a, av)], inc=sem)
                trv_s, trv_a = dma_s.val, dma_a.val

                for tt in range(2):
                    c = 2 * ci + tt
                    sp_wait = [(dma_s, trv_s), (dma_a, trv_a)]
                    for bi, m in enumerate(SP_M):
                        if m > c:
                            continue
                        slot_s = (c - m) % RING
                        b.op("vector", lambda e, tt=tt, slot_s=slot_s: e.tensor_tensor(
                            out=s_sprod[:], in0=s_qtm[:, tt, :], in1=ktm[:, slot_s, :],
                            op=ALU.mult), sp_wait + at(ev_p), inc=ev_v)
                        b.op("vector", lambda e, bi=bi: e.tensor_reduce(
                            out=s_ssc[:].rearrange("p (h w) -> p h w", w=8)[:, :, bi:bi + 1],
                            in_=s_sprod[:].rearrange("p (h w) -> p h w", w=64),
                            axis=AX.X, op=ALU.add), inc=ev_v)
                    cv = min(c, 12)
                    b.op("vector", lambda e, cv=cv: e.scalar_tensor_tensor(
                        out=s_ssc[:], in0=s_ssc[:], scalar=CLAMP, in1=pb8s[:, cv, :],
                        op0=ALU.min, op1=ALU.add), inc=ev_v)
                    vv = ev_v.val
                    b.op("scalar", lambda e: e.activation(
                        out=s_psp[:], in_=s_ssc[:], func=AF.Exp, scale=0.125),
                        [(ev_v, vv)], inc=ev_a)
                    spv = ev_a.val

                    qk_done = {}

                    def qk(hp):
                        par = hp % 2
                        for wt in range(3):
                            if c - wt < 0:
                                continue
                            slot_w = (c - wt) % 4
                            wA = at(ev_v, ev_a)
                            if hp == 0 and wt == 0:
                                wA = [(dma_s, trv_s), (dma_a, trv_a)] + at(ev_v, ev_a)
                            elif wt != 0:
                                wA = ()
                            b.op("tensor", lambda e, hp=hp, wt=wt, slot_w=slot_w, tt=tt, par=par: e.matmul(
                                psA[par][:, 128 * wt:128 * wt + 128],
                                kfm[0:64, slot_w, hp, :], s_qfm[0:64, hp, 128 * tt:128 * tt + 128],
                                start=True, stop=True, tile_position=(0, 0)), wA, inc=ev_t)
                            b.op("tensor", lambda e, hp=hp, wt=wt, slot_w=slot_w, tt=tt, par=par: e.matmul(
                                psB[par][:, 128 * wt:128 * wt + 128],
                                kfm[64:128, slot_w, hp, :], s_qfm[64:128, hp, 128 * tt:128 * tt + 128],
                                start=True, stop=True, tile_position=(64, 0)), inc=ev_t)
                        qk_done[hp] = ev_t.val

                    qk(0)
                    for hp in range(8):
                        if hp + 1 < 8:
                            qk(hp + 1)
                        par = hp % 2
                        tv = qk_done[hp]
                        for (pst, h) in ((psA[par], 2 * hp), (psB[par], 2 * hp + 1)):
                            if c >= 2:
                                b.op("vector", lambda e, pst=pst, h=h: e.scalar_tensor_tensor(
                                    out=pst[:, 0:384], in0=pst[:, 0:384], scalar=CLAMP,
                                    in1=pb8w[:, h, :], op0=ALU.min, op1=ALU.add),
                                    [(ev_t, tv)], inc=ev_v)
                            else:
                                for wt in range(3):
                                    srcw = pb8w[:, h, 128 * wt:128 * wt + 128] if wt <= c else pb8e[:]
                                    b.op("vector", lambda e, pst=pst, wt=wt, srcw=srcw: e.scalar_tensor_tensor(
                                        out=pst[:, 128 * wt:128 * wt + 128],
                                        in0=pst[:, 128 * wt:128 * wt + 128], scalar=CLAMP,
                                        in1=srcw, op0=ALU.min, op1=ALU.add),
                                        [(ev_t, tv)] if wt == 0 else (), inc=ev_v)
                        vv = ev_v.val
                        b.op("scalar", lambda e, par=par: e.activation(
                            out=s_pt[:, 2 * par, :], in_=psA[par][:, 0:384], func=AF.Exp, scale=0.125),
                            [(ev_v, vv)], inc=ev_a)
                        av2 = ev_a.val
                        h = 2 * hp
                        for wt in range(3):
                            if c - wt < 0:
                                continue
                            slot_w = (c - wt) % RING
                            b.op("tensor", lambda e, wt=wt, slot_w=slot_w, par=par, h=h: e.matmul(
                                pav[:, 128 * par:128 * par + 65],
                                s_pt[:, 2 * par, 128 * wt:128 * wt + 128],
                                vtm[:, slot_w, h, 0:65],
                                start=(wt == 0), stop=(wt == min(2, c))),
                                ([(ev_a, av2)] + at(ev_v)) if wt == 0 else (), inc=ev_t)
                        tv2 = ev_t.val
                        b.op("vector", lambda e, h=h, par=par: e.tensor_copy(
                            out=s_acc[:, h, :], in_=pav[:, 128 * par:128 * par + 64]),
                            [(ev_t, tv2)], inc=ev_v)
                        b.op("vector", lambda e, h=h, par=par: e.tensor_copy(
                            out=s_den[:, h:h + 1], in_=pav[:, 128 * par + 64:128 * par + 65]), inc=ev_v)
                        accv = ev_v.val
                        b.op("scalar", lambda e, par=par: e.activation(
                            out=s_pt[:, 2 * par + 1, :], in_=psB[par][:, 0:384], func=AF.Exp, scale=0.125),
                            inc=ev_a)
                        av2 = ev_a.val
                        h = 2 * hp + 1
                        for wt in range(3):
                            if c - wt < 0:
                                continue
                            slot_w = (c - wt) % RING
                            b.op("tensor", lambda e, wt=wt, slot_w=slot_w, par=par, h=h: e.matmul(
                                pav[:, 256 + 128 * par:256 + 128 * par + 65],
                                s_pt[:, 2 * par + 1, 128 * wt:128 * wt + 128],
                                vtm[:, slot_w, h, 0:65],
                                start=(wt == 0), stop=(wt == min(2, c))),
                                [(ev_a, av2), (ev_v, accv)] if wt == 0 else (), inc=ev_t)
                        tv2 = ev_t.val
                        b.op("vector", lambda e, h=h, par=par: e.tensor_copy(
                            out=s_acc[:, h, :], in_=pav[:, 256 + 128 * par:256 + 128 * par + 64]),
                            [(ev_t, tv2)], inc=ev_v)
                        b.op("vector", lambda e, h=h, par=par: e.tensor_copy(
                            out=s_den[:, h:h + 1], in_=pav[:, 256 + 128 * par + 64:256 + 128 * par + 65]),
                            inc=ev_v)

                    accv = ev_v.val
                    for bi, m in enumerate(SP_M):
                        if m > c:
                            continue
                        slot_s = (c - m) % RING
                        pview = s_psp[:].rearrange("p (h w) -> p h w", w=8)[:, :, bi:bi + 1] \
                            .broadcast_to([128, 16, 64])
                        b.op("vector", lambda e, slot_s=slot_s, pview=pview: e.tensor_tensor(
                            out=s_sprod[:].rearrange("p (h w) -> p h w", w=64),
                            in0=vtm[:, slot_s, :, 0:64], in1=pview, op=ALU.mult),
                            [(ev_a, spv), (ev_p, ev_p.val), (ev_v, accv)], inc=ev_v)
                        vv = ev_v.val
                        b.op("gpsimd", lambda e: e.tensor_tensor(
                            out=s_acc[:].rearrange("p h w -> p (h w)"),
                            in0=s_acc[:].rearrange("p h w -> p (h w)"), in1=s_sprod[:],
                            op=ALU.add), [(ev_v, vv)], inc=ev_p)
                    b.op("vector", lambda e: e.tensor_reduce(
                        out=s_stats[:, 0:16],
                        in_=s_psp[:].rearrange("p (h w) -> p h w", w=8),
                        axis=AX.X, op=ALU.add), [(ev_a, spv)], inc=ev_v)
                    b.op("vector", lambda e: e.tensor_tensor(
                        out=s_den[:], in0=s_den[:], in1=s_stats[:, 0:16], op=ALU.add),
                        at(ev_v), inc=ev_v)
                    b.op("vector", lambda e: e.reciprocal(out=s_den[:], in_=s_den[:]), inc=ev_v)
                    pvv = ev_p.val
                    b.op("vector", lambda e, tt=tt: e.tensor_tensor(
                        out=s_attn[:, tt, :].rearrange("p (h w) -> p h w", w=64),
                        in0=s_acc[:], in1=s_den[:].unsqueeze(2).broadcast_to([128, 16, 64]),
                        op=ALU.mult), [(ev_p, pvv), (dma_g, dma_g.val)] + at(ev_v), inc=ev_v)
                b.dma("gpsimd", attn_tm_h[2 * ci:2 * ci + 2].rearrange("t p d -> p t d"), s_attn[:],
                      at(ev_v), inc=dma_g)

            b.barrier(DMAS)

        with ExitStack() as p3:
            sb3 = lambda name, shape, dt: p3.enter_context(nc.sbuf_tensor(name, shape, dt))
            ps3 = lambda name: p3.enter_context(nc.psum_tensor(name, [128, 512], F32))
            s_gatew = sb3("s_gatew", [128, 8, D], BF16)
            s_outw = sb3("s_outw", [128, 8, D], BF16)
            gb_bc = sb3("gb_bc", [128, D], F32)
            ob_bc = sb3("ob_bc", [128, D], F32)
            g2t = sb3("g2t", [128, 8], F32)
            b2t = sb3("b2t", [128, 8], F32)
            s_xn1c = sb3("s_xn1b", [128, 2, 8, CH], BF16)
            s_at = sb3("s_at", [128, 2, 2, D], BF16)
            s_xnt = sb3("s_xnt", [128, 2, 2, D], BF16)
            s_gate = sb3("s_gate", [128, 2, D], BF16)
            s_agfm = sb3("s_agfm", [128, 8, CH], BF16)
            s_x2 = sb3("s_x2", [128, 2, D], BF16)
            s_xh2 = sb3("s_xh2", [128, 2, D], BF16)
            s_xh2fm = sb3("s_xh2fm", [128, 8, CH], BF16)
            s_xn2fm = sb3("s_xn2fm", [128, 8, CH], BF16)
            s_tmp3v = sb3("s_tmp3", [128, 512], F32)
            pgt = ps3("pgt")
            po = ps3("po")
            ptr3_t = [p3.enter_context(nc.psum_tensor(f"ptr3{i}", [128, 1024], BF16)) for i in range(2)]
            ptr3 = [p[:, 0:128] for p in ptr3_t]

            b.dma("gpsimd", s_gatew[:], gatew_h.rearrange("(c p) f -> p c f", p=128), inc=dma_g)
            b.dma("gpsimd", s_outw[:], outw_h.rearrange("(c p) f -> p c f", p=128), inc=dma_g)
            with ExitStack() as pro:
                rowbuf3v = pro.enter_context(nc.sbuf_tensor("rowbuf3", [1, 4 * D], BF16))
                b.dma("gpsimd", rowbuf3v[0:1, 0:D], gateb_h.unsqueeze(0), inc=dma_g)
                b.dma("gpsimd", rowbuf3v[0:1, D:2 * D], outb_h.unsqueeze(0), inc=dma_g)
                b.dma("gpsimd", rowbuf3v[0:1, 2 * D:3 * D], ln2g_h.unsqueeze(0), inc=dma_g)
                b.dma("gpsimd", rowbuf3v[0:1, 3 * D:4 * D], ln2b_h.unsqueeze(0), inc=dma_g)
                gv = dma_g.val
                for half in range(2):
                    b.op("tensor", lambda e, half=half: e.matmul(
                        pgt[:, 0:512], ones_row[0:1, :], rowbuf3v[0:1, 512 * half:512 * half + 512],
                        start=True, stop=True), [(dma_g, gv)] + at(ev_a), inc=ev_t)
                    b.op("scalar", lambda e, half=half: e.copy(
                        out=gb_bc[:, 512 * half:512 * half + 512], in_=pgt[:, 0:512]),
                        at(ev_t), inc=ev_a)
                    b.op("tensor", lambda e, half=half: e.matmul(
                        pgt[:, 0:512], ones_row[0:1, :],
                        rowbuf3v[0:1, D + 512 * half:D + 512 * half + 512],
                        start=True, stop=True), at(ev_a), inc=ev_t)
                    b.op("scalar", lambda e, half=half: e.copy(
                        out=ob_bc[:, 512 * half:512 * half + 512], in_=pgt[:, 0:512]),
                        at(ev_t), inc=ev_a)
                for i in range(8):
                    b.op("tensor", lambda e, i=i: e.matmul(
                        pgt[:, i:i + 1], rowbuf3v[0:1, 2 * D + 128 * i:2 * D + 128 * i + 128],
                        one1[0:1, 0:1], start=True, stop=True),
                        at(ev_a) if i == 0 else (), inc=ev_t)
                    b.op("tensor", lambda e, i=i: e.matmul(
                        pgt[:, 8 + i:9 + i], rowbuf3v[0:1, 3 * D + 128 * i:3 * D + 128 * i + 128],
                        one1[0:1, 0:1], start=True, stop=True), inc=ev_t)
                b.op("scalar", lambda e: e.copy(out=g2t[:], in_=pgt[:, 0:8]), at(ev_t), inc=ev_a)
                b.op("scalar", lambda e: e.copy(out=b2t[:], in_=pgt[:, 8:16]), inc=ev_a)
            p3pro = at(dma_g, ev_a)

            for ci in range(NCH):
                pp = ci % 2
                b.dma("gpsimd", s_xn1c[:, pp, :, :],
                      xn1_fm_h[:, :, CH * ci:CH * (ci + 1)].rearrange("f p t -> p f t"),
                      (p3pro if ci == 0 else []) + at(ev_t, ev_v, ev_a, dma_s, dma_a), inc=dma_g)
                b.dma("gpsimd", s_at[:, pp, :, :],
                      attn_tm_h[2 * ci:2 * ci + 2].rearrange("t p d -> p t d"), inc=dma_g)
                b.dma("gpsimd", s_xnt[:, pp, :, :],
                      xn_tm_h[2 * ci:2 * ci + 2].rearrange("t p d -> p t d"), inc=dma_g)
                xld = dma_g.val
                b.wait("vector", [(dma_g, xld)] + at(dma_s, dma_a))
                b.wait("scalar", [(dma_g, xld)] + at(dma_s, dma_a))
                for tt in range(2):
                    for half in range(2):
                        for dc in range(8):
                            b.op("tensor", lambda e, tt=tt, half=half, dc=dc, pp=pp: e.matmul(
                                pgt[:, 0:512], s_xn1c[:, pp, dc, 128 * tt:128 * tt + 128],
                                s_gatew[:, dc, 512 * half:512 * half + 512],
                                start=(dc == 0), stop=(dc == 7)),
                                ([(dma_g, xld)] + at(ev_v, ev_a)) if dc == 0 else (), inc=ev_t)
                        tv = ev_t.val
                        b.op("vector", lambda e, half=half: e.scalar_tensor_tensor(
                            out=s_tmp3v[:], in0=pgt[:, 0:512], scalar=0.0,
                            in1=gb_bc[:, 512 * half:512 * half + 512], op0=ALU.add, op1=ALU.add),
                            [(ev_t, tv)], inc=ev_v)
                        b.op("scalar", lambda e, tt=tt, half=half: e.activation(
                            out=s_gate[:, tt, 512 * half:512 * half + 512], in_=s_tmp3v[:],
                            func=AF.Sigmoid), at(ev_v), inc=ev_a)
                    b.op("vector", lambda e, tt=tt, pp=pp: e.tensor_tensor(
                        out=s_at[:, pp, tt, :], in0=s_at[:, pp, tt, :], in1=s_gate[:, tt, :],
                        op=ALU.mult), at(ev_a), inc=ev_v)
                    vv = ev_v.val
                    for ft in range(8):
                        sl = (8 * tt + ft) % 2
                        w = [(ev_v, vv)] + (at(ev_v, ev_a) if (8 * tt + ft) >= 2 else [])
                        b.op("tensor", lambda e, tt=tt, ft=ft, sl=sl, pp=pp: e.transpose(
                            ptr3[sl], s_at[:, pp, tt, 128 * ft:128 * ft + 128], s_id[:]),
                            w, inc=ev_t)
                        tvw = ev_t.val
                        if ft % 2 == 0:
                            b.op("vector", lambda e, tt=tt, ft=ft, sl=sl: e.tensor_copy(
                                out=s_agfm[:, ft, 128 * tt:128 * tt + 128], in_=ptr3[sl]),
                                [(ev_t, tvw)], inc=ev_v)
                        else:
                            b.op("scalar", lambda e, tt=tt, ft=ft, sl=sl: e.copy(
                                out=s_agfm[:, ft, 128 * tt:128 * tt + 128], in_=ptr3[sl]),
                                [(ev_t, tvw)], inc=ev_a)
                trv = at(ev_v, ev_a)
                for tt in range(2):
                    for half in range(2):
                        for afc in range(8):
                            b.op("tensor", lambda e, tt=tt, half=half, afc=afc: e.matmul(
                                po[:, 0:512], s_agfm[:, afc, 128 * tt:128 * tt + 128],
                                s_outw[:, afc, 512 * half:512 * half + 512],
                                start=(afc == 0), stop=(afc == 7)),
                                (trv + at(ev_v)) if afc == 0 else (), inc=ev_t)
                        tv = ev_t.val
                        b.op("vector", lambda e, half=half: e.scalar_tensor_tensor(
                            out=s_tmp3v[:], in0=po[:, 0:512], scalar=0.0,
                            in1=ob_bc[:, 512 * half:512 * half + 512], op0=ALU.add, op1=ALU.add),
                            [(ev_t, tv)], inc=ev_v)
                        b.op("vector", lambda e, tt=tt, half=half, pp=pp: e.tensor_tensor(
                            out=s_x2[:, tt, 512 * half:512 * half + 512], in0=s_tmp3v[:],
                            in1=s_xnt[:, pp, tt, 512 * half:512 * half + 512], op=ALU.add), inc=ev_v)
                    ln_stats(s_x2[:, tt, :], ln1=False)
                    ln_apply(s_xh2[:, tt, :], s_x2[:, tt, :], 4, 5)
                    vv = ev_v.val
                    for ft in range(8):
                        sl = ft % 2
                        w = [(ev_v, vv)] + at(ev_v, ev_a)
                        b.op("tensor", lambda e, tt=tt, ft=ft, sl=sl: e.transpose(
                            ptr3[sl], s_xh2[:, tt, 128 * ft:128 * ft + 128], s_id[:]),
                            w, inc=ev_t)
                        tvw = ev_t.val
                        if ft % 2 == 0:
                            b.op("vector", lambda e, tt=tt, ft=ft, sl=sl: e.tensor_copy(
                                out=s_xh2fm[:, ft, 128 * tt:128 * tt + 128], in_=ptr3[sl]),
                                [(ev_t, tvw)], inc=ev_v)
                        else:
                            b.op("scalar", lambda e, tt=tt, ft=ft, sl=sl: e.copy(
                                out=s_xh2fm[:, ft, 128 * tt:128 * tt + 128], in_=ptr3[sl]),
                                [(ev_t, tvw)], inc=ev_a)
                for ft in range(8):
                    b.op("scalar", lambda e, ft=ft: e.activation(
                        out=s_xn2fm[:, ft, :], in_=s_xh2fm[:, ft, :], func=AF.Identity,
                        bias=b2t[:, ft:ft + 1], scale=g2t[:, ft:ft + 1]),
                        at(ev_v, ev_a) if ft == 0 else (), inc=ev_a)
                b.dma("gpsimd", x2_tm_h[2 * ci:2 * ci + 2].rearrange("t p d -> p t d"), s_x2[:],
                      at(ev_v), inc=dma_g)
                b.dma("gpsimd", xn2_fm_h[:, :, CH * ci:CH * (ci + 1)].rearrange("f p t -> p f t"),
                      s_xn2fm[:], at(ev_a), inc=dma_g)
                if "xh2_fm" in debug:
                    b.dma("gpsimd", xh2_fm_h[:, :, CH * ci:CH * (ci + 1)].rearrange("f p t -> p f t"),
                          s_xh2fm[:], at(dma_s, dma_a), inc=dma_g)

            b.barrier(DMAS)

        with ExitStack() as p4:
            sb4 = lambda name, shape, dt: p4.enter_context(nc.sbuf_tensor(name, shape, dt))
            ps4 = lambda name: p4.enter_context(nc.psum_tensor(name, [128, 512], F32))
            s_fc1w = sb4("s_fc1w", [128, 8, FF], BF16)
            s_fc2w = sb4("s_fc2w", [128, 32, D], BF16)
            f1b_t = sb4("f1b_t", [128, 32], F32)
            f2b_bc = sb4("f2b_bc", [128, D], F32)
            s_xn2 = sb4("s_xn2b", [128, 2, 8, CH], BF16)
            s_x2t = sb4("s_x2t", [128, 2, 2, D], BF16)
            s_xf = sb4("s_xf", [128, 2, 2, D], F32)
            s_gt = sb4("s_gt", [128, 2, 2, D], BF16)
            s_gel = sb4("s_gel", [128, 3, CH], BF16)
            s_s = sb4("s_s", [128, 2, D], BF16)
            s_sg4v = sb4("s_sg4", [128, 1024], F32)
            s_out = sb4("s_out", [128, 2, D], F32)
            s_tmp4v = sb4("s_tmp4", [128, 512], F32)
            pf1a = ps4("pf1a")
            pf1b = ps4("pf1b")
            pf2 = [ps4(f"pf2_{i}") for i in range(4)]

            b.dma("gpsimd", s_fc1w[:], fc1w_h.rearrange("(c p) f -> p c f", p=128), inc=dma_g)
            b.dma("gpsimd", s_fc2w[:], fc2w_h.rearrange("(c p) f -> p c f", p=128), inc=dma_g)
            with ExitStack() as pro:
                rowbuf4v = pro.enter_context(nc.sbuf_tensor("rowbuf4", [1, FF + D], BF16))
                b.dma("gpsimd", rowbuf4v[0:1, 0:FF], fc1b_h.unsqueeze(0), inc=dma_g)
                b.dma("gpsimd", rowbuf4v[0:1, FF:FF + D], fc2b_h.unsqueeze(0), inc=dma_g)
                gv = dma_g.val
                for i in range(32):
                    b.op("tensor", lambda e, i=i: e.matmul(
                        pf1a[:, i:i + 1], rowbuf4v[0:1, 128 * i:128 * i + 128], one1[0:1, 0:1],
                        start=True, stop=True), [(dma_g, gv)] if i == 0 else (), inc=ev_t)
                b.op("scalar", lambda e: e.copy(out=f1b_t[:], in_=pf1a[:, 0:32]), at(ev_t), inc=ev_a)
                for half in range(2):
                    b.op("tensor", lambda e, half=half: e.matmul(
                        pf1a[:, 0:512], ones_row[0:1, :],
                        rowbuf4v[0:1, FF + 512 * half:FF + 512 * half + 512],
                        start=True, stop=True), at(ev_a), inc=ev_t)
                    b.op("scalar", lambda e, half=half: e.copy(
                        out=f2b_bc[:, 512 * half:512 * half + 512], in_=pf1a[:, 0:512]),
                        at(ev_t), inc=ev_a)
            p4pro = at(dma_g, ev_a)

            ev_v4 = b.sem("p4v2")
            ev_a4 = b.sem("p4a2")
            ld_ready = {}

            def p4_A(ci):
                pp = ci % 2
                w = (p4pro if ci == 0 else []) + at(ev_t, ev_v4)
                b.dma("gpsimd", s_xn2[:, pp, :, :],
                      xn2_fm_h[:, :, CH * ci:CH * (ci + 1)].rearrange("f p t -> p f t"),
                      w, inc=dma_g)
                b.dma("gpsimd", s_x2t[:, pp, :, :],
                      x2_tm_h[2 * ci:2 * ci + 2].rearrange("t p d -> p t d"), inc=dma_g)
                b.dma("gpsimd", s_xf[:, pp, :, :],
                      x_h.rearrange("t p d -> p t d")[:, 2 * ci:2 * ci + 2, :], inc=dma_g)
                b.dma("gpsimd", s_gt[:, pp, :, :],
                      g_tm_h[2 * ci:2 * ci + 2].rearrange("t p d -> p t d"), inc=dma_g)
                ld_ready[ci] = [(dma_g, dma_g.val)]

            def p4_B(ci):
                pp = ci % 2
                for hf in range(32):
                    pf1 = pf1a if hf % 2 == 0 else pf1b
                    gl = hf % 3
                    for dc in range(8):
                        w = ()
                        if dc == 0:
                            w = at(ev_a4) + (ld_ready[ci] if hf == 0 else [])
                        b.op("tensor", lambda e, hf=hf, dc=dc, pp=pp, pf1=pf1: e.matmul(
                            pf1[:, 0:CH], s_fc1w[:, dc, 128 * hf:128 * hf + 128], s_xn2[:, pp, dc, :],
                            start=(dc == 0), stop=(dc == 7)), w, inc=ev_t)
                    tv = ev_t.val
                    b.op("scalar", lambda e, hf=hf, gl=gl, pf1=pf1: e.activation(
                        out=s_gel[:, gl, :], in_=pf1[:, 0:CH], func=AF.Gelu,
                        bias=f1b_t[:, hf:hf + 1], scale=1.0),
                        [(ev_t, tv)], inc=ev_a4, self_ser=False)
                    av = ev_a4.val
                    for tt in range(2):
                        for half in range(2):
                            w = ()
                            if tt == 0 and half == 0:
                                w = [(ev_a4, av)] + (at(ev_v4) if hf == 0 else [])
                            b.op("tensor", lambda e, tt=tt, half=half, hf=hf, gl=gl: e.matmul(
                                pf2[2 * tt + half][:, 0:512],
                                s_gel[:, gl, 128 * tt:128 * tt + 128],
                                s_fc2w[:, hf, 512 * half:512 * half + 512],
                                start=(hf == 0), stop=(hf == 31)), w, inc=ev_t)
                tv = ev_t.val
                for tt in range(2):
                    for half in range(2):
                        w = [(ev_t, tv)] + at(ev_v4)
                        if tt == 0 and half == 0:
                            w += ld_ready[ci] + [(dma_g, dma_g.val)]
                        b.op("vector", lambda e, tt=tt, half=half: e.scalar_tensor_tensor(
                            out=s_tmp4v[:], in0=pf2[2 * tt + half][:, 0:512], scalar=0.0,
                            in1=f2b_bc[:, 512 * half:512 * half + 512], op0=ALU.add, op1=ALU.add),
                            w, inc=ev_v4, self_ser=False)
                        b.op("vector", lambda e, tt=tt, half=half, pp=pp: e.tensor_tensor(
                            out=s_s[:, tt, 512 * half:512 * half + 512], in0=s_tmp4v[:],
                            in1=s_x2t[:, pp, tt, 512 * half:512 * half + 512], op=ALU.add),
                            at(ev_v4), inc=ev_v4, self_ser=False)
                    b.op("vector", lambda e, tt=tt, pp=pp: e.tensor_tensor(
                        out=s_sg4v[:], in0=s_s[:, tt, :], in1=s_gt[:, pp, tt, :], op=ALU.mult),
                        at(ev_v4), inc=ev_v4, self_ser=False)
                    b.op("vector", lambda e, tt=tt, pp=pp: e.tensor_tensor(
                        out=s_out[:, tt, :], in0=s_sg4v[:], in1=s_xf[:, pp, tt, :], op=ALU.add),
                        at(ev_v4), inc=ev_v4, self_ser=False)
                b.dma("gpsimd", out_h[2 * ci:2 * ci + 2].rearrange("t p d -> p t d"), s_out[:],
                      at(ev_v4), inc=dma_g)

            p4_A(0)
            p4_A(1)
            for ci in range(NCH):
                p4_B(ci)
                if ci + 2 < NCH:
                    p4_A(ci + 2)

        b.wait("gpsimd", at(dma_g))
        b.wait("sync", at(dma_g, dma_s))
        b.wait("scalar", at(dma_a))

        block = ctx.enter_context(nc.Block())
        b.emit(block)
    return nc


def kernel(**inputs):
    x = np.ascontiguousarray(inputs["L0_attn_x"], dtype=np.float32)
    pb = np.ascontiguousarray(inputs["L0_attn_pos_bias"], dtype=np.float32)
    consts = {"pb_win": host_pb_win(pb), "pb_sp": host_pb_sparse(pb),
              "ident": np.eye(128, dtype=np.float32)}
    base = {
        "qkv_w": inputs["L0_attn_qkv_w"], "qkv_b": inputs["L0_attn_qkv_b"],
        "gate_w": inputs["L0_attn_gate_w"], "gate_b": inputs["L0_attn_gate_b"],
        "out_w": inputs["L0_attn_out_w"], "out_b": inputs["L0_attn_out_b"],
        "ln1_g": inputs["L0_attn_ln1_g"], "ln1_b": inputs["L0_attn_ln1_b"],
        "ln2_g": inputs["L0_attn_ln2_g"], "ln2_b": inputs["L0_attn_ln2_b"],
        "fc1_w": inputs["L0_attn_fc1_w"], "fc1_b": inputs["L0_attn_fc1_b"],
        "fc2_w": inputs["L0_attn_fc2_w"], "fc2_b": inputs["L0_attn_fc2_b"],
        "w_gate": inputs["L0_ffn_w_gate"], "w_up": inputs["L0_ffn_w_up"],
        "w_down": inputs["L0_ffn_w_down"],
    }
    base = {k: np.ascontiguousarray(v) for k, v in base.items()}
    nc = build_nc(debug=DEBUG)
    in_maps = []
    for core in range(8):
        m = {"x": np.ascontiguousarray(x[core].reshape(TT, 128, D))}
        m.update(base)
        m.update(consts)
        in_maps.append(m)
    res = run_bass_kernel_spmd(nc, in_maps, core_ids=list(range(8)))
    out = np.stack([res.results[i]["out"].reshape(NTOK, D) for i in range(8)], axis=0)
    return out.astype(np.float32)


# version 04
# speedup vs baseline: 1.0930x; 1.0930x over previous
import numpy as np
import ml_dtypes
from contextlib import ExitStack

import concourse.bass as bass
import concourse.mybir as mybir
from concourse.bass_utils import run_bass_kernel_spmd

F32 = mybir.dt.float32
BF16 = mybir.dt.bfloat16
I32 = mybir.dt.int32
AF = mybir.ActivationFunctionType
ALU = mybir.AluOpType
AX = mybir.AxisListType

NTOK = 4096
D = 1024
H = 16
HD = 64
FF = 4096
HQ = 3072
TT = NTOK // 128
RING = 16
CH = 256
NCH = NTOK // CH
QS = 1.0 / 127.0

WINSET = set(range(0, 33)) | {48, 64, 96, 128, 192, 256}
SPARSE = [384, 512, 768, 1024, 1536]
SP_M = [d // 128 for d in SPARSE]
OFFSETS = tuple(sorted(set(range(0, 33)) | {48, 64, 96, 128, 192, 256, 384, 512, 768, 1024, 1536}))
CLAMP = 480.0

DEBUG = ()


def host_pb_win(pos_bias):
    idx_of = {d: i for i, d in enumerate(OFFSETS)}
    pb = np.asarray(pos_bias, dtype=np.float32)
    out = np.full((16, 128, 3 * 128), -1e9, dtype=np.float32)
    tq = np.arange(128)
    for wt in range(3):
        for j in range(128):
            dl = tq - j + 128 * wt
            for d in WINSET:
                m = dl == d
                if m.any():
                    out[:, j, wt * 128 + np.nonzero(m)[0]] = pb[idx_of[d], :][:, None]
    return out


def host_pb_sparse(pos_bias):
    idx_of = {d: i for i, d in enumerate(OFFSETS)}
    pb = np.asarray(pos_bias, dtype=np.float32)
    out = np.full((13, 128, 128), -1e9, dtype=np.float32)
    for cv in range(13):
        for bi, dlt in enumerate(SPARSE):
            lo = dlt - 128 * cv
            if lo >= 128:
                continue
            lo = max(0, lo)
            for h in range(16):
                out[cv, lo:, 8 * h + bi] = pb[idx_of[dlt], h]
    return out


class Sem:
    __slots__ = ("h", "val")


class Builder:
    def __init__(self, nc, ctx):
        self.nc = nc
        self.ctx = ctx
        self.streams = {e: [] for e in ("tensor", "vector", "scalar", "gpsimd", "sync")}
        self.self_sems = {}
        self.all_ev = []
        self.all_dma = []
        self._n = 0

    def sem(self, name=None):
        s = Sem()
        s.h = self.ctx.enter_context(self.nc.semaphore(name or f"s{self._n}"))
        s.val = 0
        self._n += 1
        return s

    def op(self, eng, fn, waits=(), inc=None, inc_n=1, self_ser=True):
        wl = [(s.h, v) for (s, v) in waits]
        ss = self.self_sems.get(eng) if self_ser else None
        if ss is not None:
            wl.append((ss.h, ss.val))
            if inc is None:
                inc = ss
                inc_n = 1
        if inc is not None:
            inc.val += inc_n
        self.streams[eng].append((wl, fn, None if inc is None else (inc.h, inc_n)))

    def wait(self, eng, waits):
        self.streams[eng].append(([(s.h, v) for (s, v) in waits], None, None))

    def dma(self, eng, out, in_, waits=(), inc=None, **kw):
        self.op(eng, lambda e: e.dma_start(out=out, in_=in_, **kw), waits, inc, 16, self_ser=False)

    def trans(self, eng, out, in_, waits=(), inc=None):
        waits = list(waits) + [(s, s.val) for s in self.all_ev] + [(s, s.val) for s in self.all_dma]
        self.op(eng, lambda e: e.dma_start_transpose(out=out, in_=in_), waits, inc, 16, self_ser=False)

    def barrier(self, dma_sems):
        for eng, s in dma_sems.items():
            self.wait(eng, [(s, s.val)])
        bsem = self.sem()
        bsem.val = 5
        for e in self.streams:
            self.streams[e].append(([], lambda eng: eng.nop(), (bsem.h, 1)))
            self.streams[e].append(([(bsem.h, 5)], None, None))

    def emit(self, block):
        def make(eng_name):
            def body(e):
                for wl, fn, inc in self.streams[eng_name]:
                    for (h, v) in wl:
                        e.wait_ge(h, v)
                    if fn is None:
                        continue
                    r = fn(e)
                    if inc is not None:
                        r.then_inc(inc[0], inc[1])
            return body
        block.tensor(make("tensor"))
        block.vector(make("vector"))
        block.scalar(make("scalar"))
        block.gpsimd(make("gpsimd"))
        block.sync(make("sync"))


def at(*sems):
    return [(s, s.val) for s in sems]


def build_nc(debug=()):
    nc = bass.Bass("TRN2", target_bir_lowering=False)

    def din(name, shape, dt=F32):
        return nc.dram_tensor(name, shape, dt, kind="ExternalInput").ap()

    x_h = din("x", [TT, 128, D])
    qkvw_h = din("qkv_w", [D, 3 * D])
    qkvb_h = din("qkv_b", [3 * D])
    gatew_h = din("gate_w", [D, D])
    gateb_h = din("gate_b", [D])
    outw_h = din("out_w", [D, D])
    outb_h = din("out_b", [D])
    ln1g_h = din("ln1_g", [D])
    ln1b_h = din("ln1_b", [D])
    ln2g_h = din("ln2_g", [D])
    ln2b_h = din("ln2_b", [D])
    fc1w_h = din("fc1_w", [D, FF])
    fc1b_h = din("fc1_b", [FF])
    fc2w_h = din("fc2_w", [FF, D])
    fc2b_h = din("fc2_b", [D])
    wg_h = din("w_gate", [HQ, D], I32)
    wu_h = din("w_up", [HQ, D], I32)
    wd_h = din("w_down", [D, HQ], I32)
    pbw_h = din("pb_win", [16, 128, 384])
    ident_h = din("ident", [128, 128])
    pbs_h = din("pb_sp", [13, 128, 128])

    out_h = nc.dram_tensor("out", [TT, 128, D], F32, kind="ExternalOutput").ap()

    def scratch(name, shape, dt=BF16):
        if name in debug:
            return nc.dram_tensor(name, shape, dt, kind="ExternalOutput").ap()
        return nc.dram_tensor(name, shape, dt).ap()

    xn_tm_h = scratch("xn_tm", [TT, 128, D])
    xn1_fm_h = scratch("xn1_fm", [8, 128, NTOK])
    g_tm_h = scratch("g_tm", [TT, 128, D])
    attn_tm_h = scratch("attn_tm", [TT, 128, D])
    x2_tm_h = scratch("x2_tm", [TT, 128, D])
    xn2_fm_h = scratch("xn2_fm", [8, 128, NTOK])
    xh2_fm_h = scratch("xh2_fm", [8, 128, NTOK])

    ctx = ExitStack()
    with ctx:
        b = Builder(nc, ctx)
        ones_row = ctx.enter_context(nc.sbuf_tensor("ones_row", [1, 128], BF16))
        one1 = ctx.enter_context(nc.sbuf_tensor("one1", [1, 512], BF16))
        s_stats = ctx.enter_context(nc.sbuf_tensor("s_stats", [128, 16], F32))
        s_junk = ctx.enter_context(nc.sbuf_tensor("s_junk", [128, 1024], BF16))
        s_eps = ctx.enter_context(nc.sbuf_tensor("s_eps", [128, 1], F32))
        s_id = ctx.enter_context(nc.sbuf_tensor("s_id", [128, 128], BF16))

        dma_g = b.sem("dma_g")
        dma_s = b.sem("dma_s")
        dma_a = b.sem("dma_a")
        ev_t = b.sem("ev_t")
        ev_v = b.sem("ev_v")
        ev_a = b.sem("ev_a")
        ev_p = b.sem("ev_p")
        DMAS = {"gpsimd": dma_g, "sync": dma_s, "scalar": dma_a}
        b.self_sems = {"vector": ev_v, "scalar": ev_a, "gpsimd": ev_p}
        b.all_ev = [ev_t, ev_v, ev_a]
        b.all_dma = [dma_g]

        b.op("gpsimd", lambda e: e.memset(ones_row[:], 1.0), inc=ev_p)
        b.op("gpsimd", lambda e: e.memset(s_eps[:], 1e-5), inc=ev_p)
        b.dma("gpsimd", s_id[:], ident_h, inc=dma_g)
        b.op("gpsimd", lambda e: e.memset(one1[:], 1.0), inc=ev_p)

        def ln_stats(src_ap, ln1=False):
            st = s_stats
            b.op("vector", lambda e: e.reduce_sum(out=st[:, 0:1], in_=src_ap, axis=AX.X), at(ev_a), inc=ev_v)
            b.op("scalar", lambda e: e.activation(
                out=s_junk[:, 0:src_ap.shape[-1]], in_=src_ap, func=AF.Square,
                accum_out=st[:, 1:2]), at(ev_v), inc=ev_a)
            b.op("vector", lambda e: e.tensor_scalar_mul(out=st[:, 0:1], in0=st[:, 0:1], scalar1=1.0 / D), inc=ev_v)
            b.op("vector", lambda e: e.tensor_scalar_mul(out=st[:, 1:2], in0=st[:, 1:2], scalar1=1.0 / D), at(ev_a), inc=ev_v)
            b.op("vector", lambda e: e.tensor_tensor(out=st[:, 2:3], in0=st[:, 0:1], in1=st[:, 0:1], op=ALU.mult), inc=ev_v)
            b.op("vector", lambda e: e.tensor_tensor(out=st[:, 2:3], in0=st[:, 1:2], in1=st[:, 2:3], op=ALU.subtract), inc=ev_v)
            b.op("scalar", lambda e: e.activation(out=st[:, 3:4], in_=st[:, 2:3], func=AF.Sqrt, bias=s_eps[:], scale=1.0),
                 at(ev_v, ev_p), inc=ev_a)
            b.op("vector", lambda e: e.reciprocal(out=st[:, 4:5], in_=st[:, 3:4]), at(ev_a), inc=ev_v)
            b.op("vector", lambda e: e.tensor_tensor(out=st[:, 5:6], in0=st[:, 0:1], in1=st[:, 4:5], op=ALU.mult), inc=ev_v)
            if ln1:
                b.op("vector", lambda e: e.tensor_tensor(out=st[:, 6:7], in0=st[:, 4:5], in1=st[:, 4:5], op=ALU.mult), inc=ev_v)
                b.op("vector", lambda e: e.tensor_tensor(out=st[:, 6:7], in0=st[:, 2:3], in1=st[:, 6:7], op=ALU.mult), inc=ev_v)
                b.op("scalar", lambda e: e.activation(out=st[:, 7:8], in_=st[:, 6:7], func=AF.Sqrt, bias=s_eps[:], scale=1.0),
                     at(ev_v, ev_p), inc=ev_a)
                b.op("vector", lambda e: e.reciprocal(out=st[:, 8:9], in_=st[:, 7:8]), at(ev_a), inc=ev_v)
                b.op("vector", lambda e: e.tensor_tensor(out=st[:, 9:10], in0=st[:, 4:5], in1=st[:, 8:9], op=ALU.mult), inc=ev_v)
                b.op("vector", lambda e: e.tensor_tensor(out=st[:, 10:11], in0=st[:, 0:1], in1=st[:, 9:10], op=ALU.mult), inc=ev_v)

        def ln_apply(dst_ap, src_ap, rstd_col, mur_col):
            st = s_stats
            mur_view = st[:, mur_col:mur_col + 1].broadcast_to([128, dst_ap.shape[-1]])
            b.op("vector", lambda e: e.scalar_tensor_tensor(
                out=dst_ap, in0=src_ap, scalar=st[:, rstd_col:rstd_col + 1], in1=mur_view,
                op0=ALU.mult, op1=ALU.subtract), inc=ev_v)

        with ExitStack() as p1:
            sb1 = lambda name, shape, dt: p1.enter_context(nc.sbuf_tensor(name, shape, dt))
            ps1 = lambda name: p1.enter_context(nc.psum_tensor(name, [128, 512], F32))
            wgT = sb1("wgT", [128, 8, HQ], BF16)
            wuT = sb1("wuT", [128, 8, HQ], BF16)
            wdT = sb1("wdT", [128, 24, D], BF16)
            g1t = sb1("g1t", [128, 8], F32)
            b1t = sb1("b1t", [128, 8], F32)
            pg = ps1("pg")
            pu = ps1("pu")
            pd = [ps1(f"pd{i}") for i in range(4)]

            with ExitStack() as pro:
                stg = pro.enter_context(nc.sbuf_tensor("stg", [128, 24 * D], BF16))
                rowbuf1 = pro.enter_context(nc.sbuf_tensor("rowbuf1", [1, 2 * D], BF16))
                b.dma("gpsimd", rowbuf1[0:1, 0:D], ln1g_h.unsqueeze(0), at(ev_p), inc=dma_g)
                b.dma("gpsimd", rowbuf1[0:1, D:2 * D], ln1b_h.unsqueeze(0), inc=dma_g)
                gv = dma_g.val
                for i in range(8):
                    b.op("tensor", lambda e, i=i: e.matmul(
                        pg[:, i:i + 1], rowbuf1[0:1, 128 * i:128 * i + 128], one1[0:1, 0:1],
                        start=True, stop=True), [(dma_g, gv)] if i == 0 else (), inc=ev_t)
                    b.op("tensor", lambda e, i=i: e.matmul(
                        pg[:, 8 + i:9 + i], rowbuf1[0:1, D + 128 * i:D + 128 * i + 128], one1[0:1, 0:1],
                        start=True, stop=True), inc=ev_t)
                b.op("scalar", lambda e: e.copy(out=g1t[:], in_=pg[:, 0:8]), at(ev_t), inc=ev_a)
                b.op("scalar", lambda e: e.copy(out=b1t[:], in_=pg[:, 8:16]), inc=ev_a)

                ptw_t = [pro.enter_context(nc.psum_tensor(f"ptw{i}", [128, 1024], BF16)) for i in range(2)]
                ptw = [p[:, 0:128] for p in ptw_t]
                widt = dma_g.val
                for (w_h, dstT, src_tiles, dst_chunks) in (
                        (wg_h, wgT, 24, 8), (wu_h, wuT, 24, 8), (wd_h, wdT, 8, 24)):
                    inner = 128 * dst_chunks
                    stg_v = stg[:].rearrange("p (c d) -> p c d", d=inner)
                    b.dma("gpsimd", stg_v[:, 0:src_tiles, :],
                          w_h.rearrange("(c p) d -> p c d", p=128), at(ev_v, ev_a), inc=dma_g)
                    gv = dma_g.val
                    nslot = 2
                    k = 0
                    for sc in range(src_tiles):
                        for dc in range(dst_chunks):
                            sl = k % nslot
                            w = [(dma_g, max(gv, widt))]
                            if k >= nslot:
                                w += at(ev_v, ev_a)
                            b.op("tensor", lambda e, sc=sc, dc=dc, sl=sl, stg_v=stg_v: e.transpose(
                                ptw[sl], stg_v[:, sc, 128 * dc:128 * dc + 128], s_id[:]),
                                w, inc=ev_t)
                            tvw = ev_t.val
                            eng = "vector" if k % 2 == 0 else "scalar"
                            if eng == "vector":
                                b.op("vector", lambda e, sc=sc, dc=dc, sl=sl, dstT=dstT: e.tensor_copy(
                                    out=dstT[:, dc, 128 * sc:128 * sc + 128], in_=ptw[sl]),
                                    [(ev_t, tvw)], inc=ev_v)
                            else:
                                b.op("scalar", lambda e, sc=sc, dc=dc, sl=sl, dstT=dstT: e.copy(
                                    out=dstT[:, dc, 128 * sc:128 * sc + 128], in_=ptw[sl]),
                                    [(ev_t, tvw)], inc=ev_a)
                            k += 1
            tr_done = at(ev_v, ev_a)

            s_x = sb1("s_x", [128, 2, 2, D], F32)
            s_xn = sb1("s_xn", [128, 2, 2, D], BF16)
            s_xh1 = sb1("s_xh1", [128, 2, 2, D], BF16)
            s_xnfm = sb1("s_xnfm", [128, 2, 8, CH], BF16)
            s_xh1fm = sb1("s_xh1fm", [128, 2, 8, CH], BF16)
            s_xn1fm = sb1("s_xn1fm", [128, 8, CH], BF16)
            s_sg1 = sb1("s_sg1", [128, 2, CH], BF16)
            s_su = sb1("s_su", [128, 2, CH], BF16)
            s_hp = sb1("s_hp", [128, 2, CH], BF16)
            s_g = sb1("s_g", [128, 2, D], BF16)

            ev_v2 = b.sem("p1v2")
            ev_a2 = b.sem("p1a2")
            xn_ready = {}
            xload = {}

            ptr1_t = [p1.enter_context(nc.psum_tensor(f"ptr1{i}", [128, 1024], BF16)) for i in range(2)]
            ptr1 = [p[:, 0:128] for p in ptr1_t]
            lnv = {}

            def p1_A(ci):
                pp = ci % 2
                if ci == 0:
                    b.dma("gpsimd", s_x[:, 0, :, :],
                          x_h.rearrange("t p d -> p t d")[:, 0:2, :],
                          tr_done + at(ev_t), inc=dma_g)
                    xload[0] = dma_g.val
                if ci + 1 < NCH:
                    b.dma("gpsimd", s_x[:, (ci + 1) % 2, :, :],
                          x_h.rearrange("t p d -> p t d")[:, 2 * (ci + 1):2 * (ci + 1) + 2, :],
                          at(ev_v, ev_a), inc=dma_g)
                    xload[ci + 1] = dma_g.val
                b.wait("vector", [(dma_g, xload[ci])])
                b.wait("scalar", [(dma_g, xload[ci])])
                for tt in range(2):
                    src_ap = s_x[:, pp, tt, :]
                    ln_stats(src_ap, ln1=True)
                    ln_apply(s_xn[:, pp, tt, :], src_ap, 4, 5)
                    ln_apply(s_xh1[:, pp, tt, :], src_ap, 9, 10)
                lnv[ci] = ev_v.val

            def p1_A_trans(ci):
                pp = ci % 2
                k = 0
                for tt in range(2):
                    for (srcT, dstT2) in ((s_xn, s_xnfm), (s_xh1, s_xh1fm)):
                        for ft in range(8):
                            sl = k % 2
                            w = [(ev_v, lnv[ci])]
                            if k >= 2:
                                w += at(ev_v, ev_a)
                            b.op("tensor", lambda e, tt=tt, ft=ft, sl=sl, srcT=srcT, pp=pp: e.transpose(
                                ptr1[sl], srcT[:, pp, tt, 128 * ft:128 * ft + 128], s_id[:]),
                                w, inc=ev_t)
                            tvw = ev_t.val
                            if k % 2 == 0:
                                b.op("vector", lambda e, tt=tt, ft=ft, sl=sl, dstT2=dstT2, pp=pp: e.tensor_copy(
                                    out=dstT2[:, pp, ft, 128 * tt:128 * tt + 128], in_=ptr1[sl]),
                                    [(ev_t, tvw)], inc=ev_v)
                            else:
                                b.op("scalar", lambda e, tt=tt, ft=ft, sl=sl, dstT2=dstT2, pp=pp: e.copy(
                                    out=dstT2[:, pp, ft, 128 * tt:128 * tt + 128], in_=ptr1[sl]),
                                    [(ev_t, tvw)], inc=ev_a)
                            k += 1
                xn_ready[ci] = at(ev_v, ev_a)

            def p1_A_late(ci):
                pp = ci % 2
                for ft in range(8):
                    b.op("scalar", lambda e, ft=ft, pp=pp: e.activation(
                        out=s_xn1fm[:, ft, :], in_=s_xh1fm[:, pp, ft, :], func=AF.Identity,
                        bias=b1t[:, ft:ft + 1], scale=g1t[:, ft:ft + 1]),
                        (xn_ready[ci] + [(dma_g, dma_g.val)]) if ft == 0 else (), inc=ev_a)
                b.dma("gpsimd", xn_tm_h[2 * ci:2 * ci + 2].rearrange("t p d -> p t d"),
                      s_xn[:, pp, :, :], xn_ready[ci] + at(ev_v), inc=dma_g)
                b.dma("gpsimd", xn1_fm_h[:, :, CH * ci:CH * (ci + 1)].rearrange("f p t -> p f t"),
                      s_xn1fm[:], at(ev_a), inc=dma_g)

            def p1_B(ci):
                pp = ci % 2
                for hfc in range(24):
                    hp_i = hfc % 2
                    for dc in range(8):
                        w = ()
                        if dc == 0:
                            w = at(ev_a2, ev_v2) + (xn_ready[ci] if hfc == 0 else [])
                            if hfc == 0 and ci == 0:
                                w = w + tr_done
                        b.op("tensor", lambda e, dc=dc, hfc=hfc, pp=pp: e.matmul(
                            pg[:, 0:CH], wgT[:, dc, 128 * hfc:128 * hfc + 128], s_xnfm[:, pp, dc, :],
                            start=(dc == 0), stop=(dc == 7)), w, inc=ev_t)
                    for dc in range(8):
                        b.op("tensor", lambda e, dc=dc, hfc=hfc, pp=pp: e.matmul(
                            pu[:, 0:CH], wuT[:, dc, 128 * hfc:128 * hfc + 128], s_xnfm[:, pp, dc, :],
                            start=(dc == 0), stop=(dc == 7)), inc=ev_t)
                    tv = ev_t.val
                    b.op("scalar", lambda e, hp_i=hp_i: e.activation(
                        out=s_sg1[:, hp_i, :], in_=pg[:, 0:CH], func=AF.Silu, scale=QS),
                        [(ev_t, tv)] + at(ev_v2), inc=ev_a2, self_ser=False)
                    b.op("vector", lambda e, hp_i=hp_i: e.tensor_copy(
                        out=s_su[:, hp_i, :], in_=pu[:, 0:CH]),
                        [(ev_t, tv)] + at(ev_v2), inc=ev_v2, self_ser=False)
                    b.op("vector", lambda e, hp_i=hp_i: e.tensor_tensor(
                        out=s_hp[:, hp_i, :], in0=s_sg1[:, hp_i, :], in1=s_su[:, hp_i, :],
                        op=ALU.mult), at(ev_a2, ev_v2), inc=ev_v2, self_ser=False)
                    hv = ev_v2.val
                    for tt in range(2):
                        for half in range(2):
                            w = ()
                            if tt == 0 and half == 0:
                                w = [(ev_v2, hv)] + (at(ev_a2) if hfc == 0 else [])
                            b.op("tensor", lambda e, tt=tt, half=half, hfc=hfc, hp_i=hp_i: e.matmul(
                                pd[2 * tt + half][:, 0:512],
                                s_hp[:, hp_i, 128 * tt:128 * tt + 128],
                                wdT[:, hfc, 512 * half:512 * half + 512],
                                start=(hfc == 0), stop=(hfc == 23)), w, inc=ev_t)
                tv = ev_t.val
                for tt in range(2):
                    for half in range(2):
                        b.op("scalar", lambda e, tt=tt, half=half: e.activation(
                            out=s_g[:, tt, 512 * half:512 * half + 512],
                            in_=pd[2 * tt + half][:, 0:512], func=AF.Silu, scale=QS * QS),
                            [(ev_t, tv), (dma_g, dma_g.val)] + at(ev_v2), inc=ev_a2, self_ser=False)
                b.dma("gpsimd", g_tm_h[2 * ci:2 * ci + 2].rearrange("t p d -> p t d"), s_g[:],
                      at(ev_a2), inc=dma_g)

            p1_A(0)
            p1_A_trans(0)
            for ci in range(NCH):
                if ci + 1 < NCH:
                    p1_A(ci + 1)
                p1_B(ci)
                if ci + 1 < NCH:
                    p1_A_trans(ci + 1)
                p1_A_late(ci)

            b.barrier(DMAS)

        with ExitStack() as p2:
            sb2 = lambda name, shape, dt: p2.enter_context(nc.sbuf_tensor(name, shape, dt))
            ps2 = lambda name: p2.enter_context(nc.psum_tensor(name, [128, 512], F32))
            s_qkvw = sb2("s_qkvw", [128, 8, 3 * D], BF16)
            qkb_t = sb2("qkb_t", [128, 16], F32)
            vb_bc = sb2("vb_bc", [128, D], F32)
            kfm = sb2("kfm", [128, 8, 8, 128], BF16)
            ktm = sb2("ktm", [128, RING, D], BF16)
            vtm = sb2("vtm", [128, RING, H, 65], BF16)
            pb8w = sb2("pb8w", [128, 16, 384], BF16)
            pb8e = sb2("pb8e", [128, 128], BF16)
            pb8s = sb2("pb8s", [128, 13, 128], BF16)
            s_xn1 = sb2("s_xn1", [128, 2, 8, CH], BF16)
            s_qfm = sb2("s_qfm", [128, 2, 8, CH], BF16)
            s_qtm = sb2("s_qtm", [128, 2, 2, D], BF16)
            s_pt = sb2("s_pt", [128, 4, 384], BF16)
            s_acc = sb2("s_acc", [128, H, HD], F32)
            s_den = sb2("s_den", [128, 16], F32)
            s_sprod = sb2("s_sprod", [128, D], BF16)
            s_ssc = sb2("s_ssc", [128, 128], F32)
            s_psp = sb2("s_psp", [128, 128], BF16)
            s_attn = sb2("s_attn", [128, 2, D], BF16)
            psA = [ps2("psA0"), ps2("psA1")]
            psB = [ps2("psB0"), ps2("psB1")]
            pav = ps2("pav")
            pq = ps2("pq")
            pv0 = ps2("pv0")
            pv1 = ps2("pv1")

            b.dma("gpsimd", s_qkvw[:], qkvw_h.rearrange("(c p) f -> p c f", p=128), inc=dma_g)
            with ExitStack() as pro:
                rowbuf2v = pro.enter_context(nc.sbuf_tensor("rowbuf2", [1, 3 * D], BF16))
                b.dma("gpsimd", rowbuf2v[0:1, :], qkvb_h.unsqueeze(0), inc=dma_g)
                gv = dma_g.val
                for i in range(16):
                    b.op("tensor", lambda e, i=i: e.matmul(
                        pq[:, i:i + 1], rowbuf2v[0:1, 128 * i:128 * i + 128], one1[0:1, 0:1],
                        start=True, stop=True), [(dma_g, gv)] if i == 0 else (), inc=ev_t)
                b.op("scalar", lambda e: e.copy(out=qkb_t[:], in_=pq[:, 0:16]), at(ev_t), inc=ev_a)
                for half in range(2):
                    b.op("tensor", lambda e, half=half: e.matmul(
                        pv0[:, 0:512], ones_row[0:1, :],
                        rowbuf2v[0:1, 2 * D + 512 * half:2 * D + 512 * half + 512],
                        start=True, stop=True), at(ev_a), inc=ev_t)
                    b.op("scalar", lambda e, half=half: e.copy(
                        out=vb_bc[:, 512 * half:512 * half + 512], in_=pv0[:, 0:512]),
                        at(ev_t), inc=ev_a)
            b.dma("gpsimd", pb8w[:], pbw_h.rearrange("h p w -> p h w"), inc=dma_g)
            b.dma("gpsimd", pb8s[:], pbs_h.rearrange("c p w -> p c w"), inc=dma_g)
            gv = dma_g.val
            b.op("vector", lambda e: e.tensor_scalar_mul(out=pb8w[:], in0=pb8w[:], scalar1=8.0),
                 [(dma_g, gv)], inc=ev_v)
            b.op("vector", lambda e: e.tensor_scalar_mul(out=pb8s[:], in0=pb8s[:], scalar1=8.0), inc=ev_v)
            b.op("gpsimd", lambda e: e.memset(pb8e[:], -8e9), inc=ev_p)
            b.op("gpsimd", lambda e: e.memset(s_ssc[:], 0.0), inc=ev_p)
            b.op("gpsimd", lambda e: e.memset(
                vtm[:].rearrange("p r h w -> p (r h) w")[:, :, 64:65], 1.0), inc=ev_p)
            for par in range(2):
                b.op("vector", lambda e, par=par: e.memset(psA[par][:, 0:384], 0.0), inc=ev_v)
                b.op("vector", lambda e, par=par: e.memset(psB[par][:, 0:384], 0.0), inc=ev_v)
            p2pro = at(dma_g, ev_v, ev_a, ev_p)

            ev_v3 = b.sem("p2v3")
            ev_a3 = b.sem("p2a3")
            ready2 = {}

            def p2_A(ci):
                pp = ci % 2
                b.dma("gpsimd", s_xn1[:, pp, :, :],
                      xn1_fm_h[:, :, CH * ci:CH * (ci + 1)].rearrange("f p t -> p f t"),
                      (p2pro if ci == 0 else []) + at(ev_t, ev_v, ev_a, dma_s), inc=dma_g)
                xld = dma_g.val
                b.wait("vector", [(dma_g, xld)])
                b.wait("scalar", [(dma_g, xld)])
                for qf in range(8):
                    pqx = pq if qf % 2 == 0 else pv0
                    for dc in range(8):
                        b.op("tensor", lambda e, qf=qf, dc=dc, pp=pp, pqx=pqx: e.matmul(
                            pqx[:, 0:CH], s_qkvw[:, dc, 128 * qf:128 * qf + 128], s_xn1[:, pp, dc, :],
                            start=(dc == 0), stop=(dc == 7)),
                            ([(dma_g, xld)] + at(ev_a)) if dc == 0 else (), inc=ev_t)
                    tv = ev_t.val
                    b.op("scalar", lambda e, qf=qf, pp=pp, pqx=pqx: e.activation(
                        out=s_qfm[:, pp, qf, :], in_=pqx[:, 0:CH], func=AF.Identity,
                        bias=qkb_t[:, qf:qf + 1], scale=1.0), [(ev_t, tv)], inc=ev_a)
                slot0 = (2 * ci) % 8
                for kf in range(8):
                    pqx = pq if kf % 2 == 0 else pv0
                    for dc in range(8):
                        b.op("tensor", lambda e, kf=kf, dc=dc, pp=pp, pqx=pqx: e.matmul(
                            pqx[:, 0:CH], s_qkvw[:, dc, D + 128 * kf:D + 128 * kf + 128],
                            s_xn1[:, pp, dc, :],
                            start=(dc == 0), stop=(dc == 7)), at(ev_a) if dc == 0 else (), inc=ev_t)
                    tv = ev_t.val
                    b.op("scalar", lambda e, kf=kf, slot0=slot0, pqx=pqx: e.activation(
                        out=kfm[:, slot0:slot0 + 2, kf, :],
                        in_=pqx[:, 0:CH].rearrange("p (s t) -> p s t", s=2), func=AF.Identity,
                        bias=qkb_t[:, 8 + kf:9 + kf], scale=1.0), [(ev_t, tv)], inc=ev_a)
                for tt in range(2):
                    pvp = pv1
                    slot = (2 * ci + tt) % RING
                    for half in range(2):
                        for dc in range(8):
                            b.op("tensor", lambda e, tt=tt, half=half, dc=dc, pp=pp, pvp=pvp: e.matmul(
                                pvp[:, 0:512],
                                s_xn1[:, pp, dc, 128 * tt:128 * tt + 128],
                                s_qkvw[:, dc, 2 * D + 512 * half:2 * D + 512 * half + 512],
                                start=(dc == 0), stop=(dc == 7)),
                                at(ev_v) if dc == 0 else (), inc=ev_t)
                        tv = ev_t.val
                        b.op("vector", lambda e, slot=slot, half=half, pvp=pvp: e.scalar_tensor_tensor(
                            out=vtm[:, slot, 8 * half:8 * half + 8, 0:64],
                            in0=pvp[:, 0:512].rearrange("p (h w) -> p h w", w=64),
                            scalar=1.0,
                            in1=vb_bc[:, 512 * half:512 * half + 512].rearrange("p (h w) -> p h w", w=64),
                            op0=ALU.mult, op1=ALU.add), [(ev_t, tv)] + at(ev_p), inc=ev_v)
                av = ev_a.val
                for tt in range(2):
                    slot = (2 * ci + tt) % RING
                    for ft in range(8):
                        b.trans("sync", s_qtm[:, pp, tt, 128 * ft:128 * ft + 128],
                                s_qfm[:, pp, ft, 128 * tt:128 * tt + 128],
                                [(ev_a, av)], inc=dma_s)
                        b.trans("sync", ktm[:, slot, 128 * ft:128 * ft + 128],
                                kfm[:, slot % 8, ft, :],
                                [(ev_a, av)], inc=dma_s)
                ready2[ci] = at(dma_s, ev_v, ev_a)

            def p2_B(ci):
                pp = ci % 2
                for tt in range(2):
                    c = 2 * ci + tt
                    for bi, m in enumerate(SP_M):
                        if m > c:
                            continue
                        slot_s = (c - m) % RING
                        b.op("vector", lambda e, tt=tt, slot_s=slot_s, pp=pp: e.tensor_tensor(
                            out=s_sprod[:], in0=s_qtm[:, pp, tt, :], in1=ktm[:, slot_s, :],
                            op=ALU.mult), ready2[ci] + at(ev_p, ev_v3), inc=ev_v3, self_ser=False)
                        b.op("vector", lambda e, bi=bi: e.tensor_reduce(
                            out=s_ssc[:].rearrange("p (h w) -> p h w", w=8)[:, :, bi:bi + 1],
                            in_=s_sprod[:].rearrange("p (h w) -> p h w", w=64),
                            axis=AX.X, op=ALU.add), at(ev_v3), inc=ev_v3, self_ser=False)
                    cv = min(c, 12)
                    b.op("vector", lambda e, cv=cv: e.scalar_tensor_tensor(
                        out=s_ssc[:], in0=s_ssc[:], scalar=CLAMP, in1=pb8s[:, cv, :],
                        op0=ALU.min, op1=ALU.add), at(ev_v3), inc=ev_v3, self_ser=False)
                    vv = ev_v3.val
                    b.op("scalar", lambda e: e.activation(
                        out=s_psp[:], in_=s_ssc[:], func=AF.Exp, scale=0.125),
                        [(ev_v3, vv)], inc=ev_a3, self_ser=False)
                    spv = ev_a3.val

                    qk_done = {}

                    def qk(hp):
                        par = hp % 2
                        for wt in range(3):
                            if c - wt < 0:
                                continue
                            slot_w = (c - wt) % 8
                            wA = ()
                            if wt == 0:
                                wA = at(ev_v3, ev_a3)
                                if hp == 0:
                                    wA = wA + ready2[ci]
                            b.op("tensor", lambda e, hp=hp, wt=wt, slot_w=slot_w, tt=tt, par=par, pp=pp: e.matmul(
                                psA[par][:, 128 * wt:128 * wt + 128],
                                kfm[0:64, slot_w, hp, :], s_qfm[0:64, pp, hp, 128 * tt:128 * tt + 128],
                                start=True, stop=True, tile_position=(0, 0)), wA, inc=ev_t)
                            b.op("tensor", lambda e, hp=hp, wt=wt, slot_w=slot_w, tt=tt, par=par, pp=pp: e.matmul(
                                psB[par][:, 128 * wt:128 * wt + 128],
                                kfm[64:128, slot_w, hp, :], s_qfm[64:128, pp, hp, 128 * tt:128 * tt + 128],
                                start=True, stop=True, tile_position=(64, 0)), inc=ev_t)
                        qk_done[hp] = ev_t.val

                    qk(0)
                    for hp in range(8):
                        if hp + 1 < 8:
                            qk(hp + 1)
                        par = hp % 2
                        tv = qk_done[hp]
                        for (pst, h) in ((psA[par], 2 * hp), (psB[par], 2 * hp + 1)):
                            if c >= 2:
                                b.op("vector", lambda e, pst=pst, h=h: e.scalar_tensor_tensor(
                                    out=pst[:, 0:384], in0=pst[:, 0:384], scalar=CLAMP,
                                    in1=pb8w[:, h, :], op0=ALU.min, op1=ALU.add),
                                    [(ev_t, tv)] + at(ev_a3), inc=ev_v3, self_ser=False)
                            else:
                                for wt in range(3):
                                    srcw = pb8w[:, h, 128 * wt:128 * wt + 128] if wt <= c else pb8e[:]
                                    b.op("vector", lambda e, pst=pst, wt=wt, srcw=srcw: e.scalar_tensor_tensor(
                                        out=pst[:, 128 * wt:128 * wt + 128],
                                        in0=pst[:, 128 * wt:128 * wt + 128], scalar=CLAMP,
                                        in1=srcw, op0=ALU.min, op1=ALU.add),
                                        ([(ev_t, tv)] + at(ev_a3)) if wt == 0 else (),
                                        inc=ev_v3, self_ser=False)
                        vv = ev_v3.val
                        b.op("scalar", lambda e, par=par: e.activation(
                            out=s_pt[:, 2 * par, :], in_=psA[par][:, 0:384], func=AF.Exp, scale=0.125),
                            [(ev_v3, vv)], inc=ev_a3, self_ser=False)
                        av2 = ev_a3.val
                        h = 2 * hp
                        for wt in range(3):
                            if c - wt < 0:
                                continue
                            slot_w = (c - wt) % RING
                            b.op("tensor", lambda e, wt=wt, slot_w=slot_w, par=par, h=h: e.matmul(
                                pav[:, 128 * par:128 * par + 65],
                                s_pt[:, 2 * par, 128 * wt:128 * wt + 128],
                                vtm[:, slot_w, h, 0:65],
                                start=(wt == 0), stop=(wt == min(2, c))),
                                ([(ev_a3, av2)] + at(ev_v3)) if wt == 0 else (), inc=ev_t)
                        tv2 = ev_t.val
                        b.op("vector", lambda e, h=h, par=par: e.tensor_copy(
                            out=s_acc[:, h, :], in_=pav[:, 128 * par:128 * par + 64]),
                            [(ev_t, tv2)], inc=ev_v3, self_ser=False)
                        b.op("vector", lambda e, h=h, par=par: e.tensor_copy(
                            out=s_den[:, h:h + 1], in_=pav[:, 128 * par + 64:128 * par + 65]),
                            at(ev_v3), inc=ev_v3, self_ser=False)
                        accv = ev_v3.val
                        b.op("scalar", lambda e, par=par: e.activation(
                            out=s_pt[:, 2 * par + 1, :], in_=psB[par][:, 0:384], func=AF.Exp, scale=0.125),
                            at(ev_v3) if False else (), inc=ev_a3, self_ser=False)
                        av2 = ev_a3.val
                        h = 2 * hp + 1
                        for wt in range(3):
                            if c - wt < 0:
                                continue
                            slot_w = (c - wt) % RING
                            b.op("tensor", lambda e, wt=wt, slot_w=slot_w, par=par, h=h: e.matmul(
                                pav[:, 256 + 128 * par:256 + 128 * par + 65],
                                s_pt[:, 2 * par + 1, 128 * wt:128 * wt + 128],
                                vtm[:, slot_w, h, 0:65],
                                start=(wt == 0), stop=(wt == min(2, c))),
                                [(ev_a3, av2), (ev_v3, accv)] if wt == 0 else (), inc=ev_t)
                        tv2 = ev_t.val
                        b.op("vector", lambda e, h=h, par=par: e.tensor_copy(
                            out=s_acc[:, h, :], in_=pav[:, 256 + 128 * par:256 + 128 * par + 64]),
                            [(ev_t, tv2)] + at(ev_v3), inc=ev_v3, self_ser=False)
                        b.op("vector", lambda e, h=h, par=par: e.tensor_copy(
                            out=s_den[:, h:h + 1], in_=pav[:, 256 + 128 * par + 64:256 + 128 * par + 65]),
                            at(ev_v3), inc=ev_v3, self_ser=False)

                    accv = ev_v3.val
                    for bi, m in enumerate(SP_M):
                        if m > c:
                            continue
                        slot_s = (c - m) % RING
                        pview = s_psp[:].rearrange("p (h w) -> p h w", w=8)[:, :, bi:bi + 1] \
                            .broadcast_to([128, 16, 64])
                        b.op("vector", lambda e, slot_s=slot_s, pview=pview: e.tensor_tensor(
                            out=s_sprod[:].rearrange("p (h w) -> p h w", w=64),
                            in0=vtm[:, slot_s, :, 0:64], in1=pview, op=ALU.mult),
                            [(ev_a3, spv), (ev_p, ev_p.val), (ev_v3, accv)], inc=ev_v3, self_ser=False)
                        vv = ev_v3.val
                        b.op("gpsimd", lambda e: e.tensor_tensor(
                            out=s_acc[:].rearrange("p h w -> p (h w)"),
                            in0=s_acc[:].rearrange("p h w -> p (h w)"), in1=s_sprod[:],
                            op=ALU.add), [(ev_v3, vv)], inc=ev_p)
                    b.op("vector", lambda e: e.tensor_reduce(
                        out=s_stats[:, 0:16],
                        in_=s_psp[:].rearrange("p (h w) -> p h w", w=8),
                        axis=AX.X, op=ALU.add), [(ev_a3, spv)] + at(ev_v3), inc=ev_v3, self_ser=False)
                    b.op("vector", lambda e: e.tensor_tensor(
                        out=s_den[:], in0=s_den[:], in1=s_stats[:, 0:16], op=ALU.add),
                        at(ev_v3), inc=ev_v3, self_ser=False)
                    b.op("vector", lambda e: e.reciprocal(out=s_den[:], in_=s_den[:]),
                        at(ev_v3), inc=ev_v3, self_ser=False)
                    pvv = ev_p.val
                    b.op("vector", lambda e, tt=tt: e.tensor_tensor(
                        out=s_attn[:, tt, :].rearrange("p (h w) -> p h w", w=64),
                        in0=s_acc[:], in1=s_den[:].unsqueeze(2).broadcast_to([128, 16, 64]),
                        op=ALU.mult), [(ev_p, pvv), (dma_g, dma_g.val)] + at(ev_v3),
                        inc=ev_v3, self_ser=False)
                b.dma("gpsimd", attn_tm_h[2 * ci:2 * ci + 2].rearrange("t p d -> p t d"), s_attn[:],
                      at(ev_v3), inc=dma_g)

            p2_A(0)
            for ci in range(NCH):
                if ci + 1 < NCH:
                    p2_A(ci + 1)
                p2_B(ci)

            b.barrier(DMAS)

        with ExitStack() as p3:
            sb3 = lambda name, shape, dt: p3.enter_context(nc.sbuf_tensor(name, shape, dt))
            ps3 = lambda name: p3.enter_context(nc.psum_tensor(name, [128, 512], F32))
            s_gatew = sb3("s_gatew", [128, 8, D], BF16)
            s_outw = sb3("s_outw", [128, 8, D], BF16)
            gb_bc = sb3("gb_bc", [128, D], F32)
            ob_bc = sb3("ob_bc", [128, D], F32)
            g2t = sb3("g2t", [128, 8], F32)
            b2t = sb3("b2t", [128, 8], F32)
            s_xn1c = sb3("s_xn1b", [128, 2, 8, CH], BF16)
            s_at = sb3("s_at", [128, 2, 2, D], BF16)
            s_xnt = sb3("s_xnt", [128, 2, 2, D], BF16)
            s_gate = sb3("s_gate", [128, 2, D], BF16)
            s_agfm = sb3("s_agfm", [128, 8, CH], BF16)
            s_x2 = sb3("s_x2", [128, 2, D], BF16)
            s_xh2 = sb3("s_xh2", [128, 2, D], BF16)
            s_xh2fm = sb3("s_xh2fm", [128, 8, CH], BF16)
            s_xn2fm = sb3("s_xn2fm", [128, 8, CH], BF16)
            s_tmp3v = sb3("s_tmp3", [128, 512], F32)
            pgt = ps3("pgt")
            po = ps3("po")
            ptr3_t = [p3.enter_context(nc.psum_tensor(f"ptr3{i}", [128, 1024], BF16)) for i in range(2)]
            ptr3 = [p[:, 0:128] for p in ptr3_t]

            b.dma("gpsimd", s_gatew[:], gatew_h.rearrange("(c p) f -> p c f", p=128), inc=dma_g)
            b.dma("gpsimd", s_outw[:], outw_h.rearrange("(c p) f -> p c f", p=128), inc=dma_g)
            with ExitStack() as pro:
                rowbuf3v = pro.enter_context(nc.sbuf_tensor("rowbuf3", [1, 4 * D], BF16))
                b.dma("gpsimd", rowbuf3v[0:1, 0:D], gateb_h.unsqueeze(0), inc=dma_g)
                b.dma("gpsimd", rowbuf3v[0:1, D:2 * D], outb_h.unsqueeze(0), inc=dma_g)
                b.dma("gpsimd", rowbuf3v[0:1, 2 * D:3 * D], ln2g_h.unsqueeze(0), inc=dma_g)
                b.dma("gpsimd", rowbuf3v[0:1, 3 * D:4 * D], ln2b_h.unsqueeze(0), inc=dma_g)
                gv = dma_g.val
                for half in range(2):
                    b.op("tensor", lambda e, half=half: e.matmul(
                        pgt[:, 0:512], ones_row[0:1, :], rowbuf3v[0:1, 512 * half:512 * half + 512],
                        start=True, stop=True), [(dma_g, gv)] + at(ev_a), inc=ev_t)
                    b.op("scalar", lambda e, half=half: e.copy(
                        out=gb_bc[:, 512 * half:512 * half + 512], in_=pgt[:, 0:512]),
                        at(ev_t), inc=ev_a)
                    b.op("tensor", lambda e, half=half: e.matmul(
                        pgt[:, 0:512], ones_row[0:1, :],
                        rowbuf3v[0:1, D + 512 * half:D + 512 * half + 512],
                        start=True, stop=True), at(ev_a), inc=ev_t)
                    b.op("scalar", lambda e, half=half: e.copy(
                        out=ob_bc[:, 512 * half:512 * half + 512], in_=pgt[:, 0:512]),
                        at(ev_t), inc=ev_a)
                for i in range(8):
                    b.op("tensor", lambda e, i=i: e.matmul(
                        pgt[:, i:i + 1], rowbuf3v[0:1, 2 * D + 128 * i:2 * D + 128 * i + 128],
                        one1[0:1, 0:1], start=True, stop=True),
                        at(ev_a) if i == 0 else (), inc=ev_t)
                    b.op("tensor", lambda e, i=i: e.matmul(
                        pgt[:, 8 + i:9 + i], rowbuf3v[0:1, 3 * D + 128 * i:3 * D + 128 * i + 128],
                        one1[0:1, 0:1], start=True, stop=True), inc=ev_t)
                b.op("scalar", lambda e: e.copy(out=g2t[:], in_=pgt[:, 0:8]), at(ev_t), inc=ev_a)
                b.op("scalar", lambda e: e.copy(out=b2t[:], in_=pgt[:, 8:16]), inc=ev_a)
            p3pro = at(dma_g, ev_a)

            for ci in range(NCH):
                pp = ci % 2
                b.dma("gpsimd", s_xn1c[:, pp, :, :],
                      xn1_fm_h[:, :, CH * ci:CH * (ci + 1)].rearrange("f p t -> p f t"),
                      (p3pro if ci == 0 else []) + at(ev_t, ev_v, ev_a, dma_s, dma_a), inc=dma_g)
                b.dma("gpsimd", s_at[:, pp, :, :],
                      attn_tm_h[2 * ci:2 * ci + 2].rearrange("t p d -> p t d"), inc=dma_g)
                b.dma("gpsimd", s_xnt[:, pp, :, :],
                      xn_tm_h[2 * ci:2 * ci + 2].rearrange("t p d -> p t d"), inc=dma_g)
                xld = dma_g.val
                b.wait("vector", [(dma_g, xld)] + at(dma_s, dma_a))
                b.wait("scalar", [(dma_g, xld)] + at(dma_s, dma_a))
                for tt in range(2):
                    for half in range(2):
                        for dc in range(8):
                            b.op("tensor", lambda e, tt=tt, half=half, dc=dc, pp=pp: e.matmul(
                                pgt[:, 0:512], s_xn1c[:, pp, dc, 128 * tt:128 * tt + 128],
                                s_gatew[:, dc, 512 * half:512 * half + 512],
                                start=(dc == 0), stop=(dc == 7)),
                                ([(dma_g, xld)] + at(ev_v, ev_a)) if dc == 0 else (), inc=ev_t)
                        tv = ev_t.val
                        b.op("vector", lambda e, half=half: e.scalar_tensor_tensor(
                            out=s_tmp3v[:], in0=pgt[:, 0:512], scalar=0.0,
                            in1=gb_bc[:, 512 * half:512 * half + 512], op0=ALU.add, op1=ALU.add),
                            [(ev_t, tv)], inc=ev_v)
                        b.op("scalar", lambda e, tt=tt, half=half: e.activation(
                            out=s_gate[:, tt, 512 * half:512 * half + 512], in_=s_tmp3v[:],
                            func=AF.Sigmoid), at(ev_v), inc=ev_a)
                    b.op("vector", lambda e, tt=tt, pp=pp: e.tensor_tensor(
                        out=s_at[:, pp, tt, :], in0=s_at[:, pp, tt, :], in1=s_gate[:, tt, :],
                        op=ALU.mult), at(ev_a), inc=ev_v)
                    vv = ev_v.val
                    for ft in range(8):
                        sl = (8 * tt + ft) % 2
                        w = [(ev_v, vv)] + (at(ev_v, ev_a) if (8 * tt + ft) >= 2 else [])
                        b.op("tensor", lambda e, tt=tt, ft=ft, sl=sl, pp=pp: e.transpose(
                            ptr3[sl], s_at[:, pp, tt, 128 * ft:128 * ft + 128], s_id[:]),
                            w, inc=ev_t)
                        tvw = ev_t.val
                        if ft % 2 == 0:
                            b.op("vector", lambda e, tt=tt, ft=ft, sl=sl: e.tensor_copy(
                                out=s_agfm[:, ft, 128 * tt:128 * tt + 128], in_=ptr3[sl]),
                                [(ev_t, tvw)], inc=ev_v)
                        else:
                            b.op("scalar", lambda e, tt=tt, ft=ft, sl=sl: e.copy(
                                out=s_agfm[:, ft, 128 * tt:128 * tt + 128], in_=ptr3[sl]),
                                [(ev_t, tvw)], inc=ev_a)
                trv = at(ev_v, ev_a)
                for tt in range(2):
                    for half in range(2):
                        for afc in range(8):
                            b.op("tensor", lambda e, tt=tt, half=half, afc=afc: e.matmul(
                                po[:, 0:512], s_agfm[:, afc, 128 * tt:128 * tt + 128],
                                s_outw[:, afc, 512 * half:512 * half + 512],
                                start=(afc == 0), stop=(afc == 7)),
                                (trv + at(ev_v)) if afc == 0 else (), inc=ev_t)
                        tv = ev_t.val
                        b.op("vector", lambda e, half=half: e.scalar_tensor_tensor(
                            out=s_tmp3v[:], in0=po[:, 0:512], scalar=0.0,
                            in1=ob_bc[:, 512 * half:512 * half + 512], op0=ALU.add, op1=ALU.add),
                            [(ev_t, tv)], inc=ev_v)
                        b.op("vector", lambda e, tt=tt, half=half, pp=pp: e.tensor_tensor(
                            out=s_x2[:, tt, 512 * half:512 * half + 512], in0=s_tmp3v[:],
                            in1=s_xnt[:, pp, tt, 512 * half:512 * half + 512], op=ALU.add), inc=ev_v)
                    ln_stats(s_x2[:, tt, :], ln1=False)
                    ln_apply(s_xh2[:, tt, :], s_x2[:, tt, :], 4, 5)
                    vv = ev_v.val
                    for ft in range(8):
                        sl = ft % 2
                        w = [(ev_v, vv)] + at(ev_v, ev_a)
                        b.op("tensor", lambda e, tt=tt, ft=ft, sl=sl: e.transpose(
                            ptr3[sl], s_xh2[:, tt, 128 * ft:128 * ft + 128], s_id[:]),
                            w, inc=ev_t)
                        tvw = ev_t.val
                        if ft % 2 == 0:
                            b.op("vector", lambda e, tt=tt, ft=ft, sl=sl: e.tensor_copy(
                                out=s_xh2fm[:, ft, 128 * tt:128 * tt + 128], in_=ptr3[sl]),
                                [(ev_t, tvw)], inc=ev_v)
                        else:
                            b.op("scalar", lambda e, tt=tt, ft=ft, sl=sl: e.copy(
                                out=s_xh2fm[:, ft, 128 * tt:128 * tt + 128], in_=ptr3[sl]),
                                [(ev_t, tvw)], inc=ev_a)
                for ft in range(8):
                    b.op("scalar", lambda e, ft=ft: e.activation(
                        out=s_xn2fm[:, ft, :], in_=s_xh2fm[:, ft, :], func=AF.Identity,
                        bias=b2t[:, ft:ft + 1], scale=g2t[:, ft:ft + 1]),
                        at(ev_v, ev_a) if ft == 0 else (), inc=ev_a)
                b.dma("gpsimd", x2_tm_h[2 * ci:2 * ci + 2].rearrange("t p d -> p t d"), s_x2[:],
                      at(ev_v), inc=dma_g)
                b.dma("gpsimd", xn2_fm_h[:, :, CH * ci:CH * (ci + 1)].rearrange("f p t -> p f t"),
                      s_xn2fm[:], at(ev_a), inc=dma_g)
                if "xh2_fm" in debug:
                    b.dma("gpsimd", xh2_fm_h[:, :, CH * ci:CH * (ci + 1)].rearrange("f p t -> p f t"),
                          s_xh2fm[:], at(dma_s, dma_a), inc=dma_g)

            b.barrier(DMAS)

        with ExitStack() as p4:
            sb4 = lambda name, shape, dt: p4.enter_context(nc.sbuf_tensor(name, shape, dt))
            ps4 = lambda name: p4.enter_context(nc.psum_tensor(name, [128, 512], F32))
            s_fc1w = sb4("s_fc1w", [128, 8, FF], BF16)
            s_fc2w = sb4("s_fc2w", [128, 32, D], BF16)
            f1b_t = sb4("f1b_t", [128, 32], F32)
            f2b_bc = sb4("f2b_bc", [128, D], F32)
            s_xn2 = sb4("s_xn2b", [128, 2, 8, CH], BF16)
            s_x2t = sb4("s_x2t", [128, 2, 2, D], BF16)
            s_xf = sb4("s_xf", [128, 2, 2, D], F32)
            s_gt = sb4("s_gt", [128, 2, 2, D], BF16)
            s_gel = sb4("s_gel", [128, 3, CH], BF16)
            s_s = sb4("s_s", [128, 2, D], BF16)
            s_sg4v = sb4("s_sg4", [128, 1024], F32)
            s_out = sb4("s_out", [128, 2, D], F32)
            s_tmp4v = sb4("s_tmp4", [128, 512], F32)
            pf1a = ps4("pf1a")
            pf1b = ps4("pf1b")
            pf2 = [ps4(f"pf2_{i}") for i in range(4)]

            b.dma("gpsimd", s_fc1w[:], fc1w_h.rearrange("(c p) f -> p c f", p=128), inc=dma_g)
            b.dma("gpsimd", s_fc2w[:], fc2w_h.rearrange("(c p) f -> p c f", p=128), inc=dma_g)
            with ExitStack() as pro:
                rowbuf4v = pro.enter_context(nc.sbuf_tensor("rowbuf4", [1, FF + D], BF16))
                b.dma("gpsimd", rowbuf4v[0:1, 0:FF], fc1b_h.unsqueeze(0), inc=dma_g)
                b.dma("gpsimd", rowbuf4v[0:1, FF:FF + D], fc2b_h.unsqueeze(0), inc=dma_g)
                gv = dma_g.val
                for i in range(32):
                    b.op("tensor", lambda e, i=i: e.matmul(
                        pf1a[:, i:i + 1], rowbuf4v[0:1, 128 * i:128 * i + 128], one1[0:1, 0:1],
                        start=True, stop=True), [(dma_g, gv)] if i == 0 else (), inc=ev_t)
                b.op("scalar", lambda e: e.copy(out=f1b_t[:], in_=pf1a[:, 0:32]), at(ev_t), inc=ev_a)
                for half in range(2):
                    b.op("tensor", lambda e, half=half: e.matmul(
                        pf1a[:, 0:512], ones_row[0:1, :],
                        rowbuf4v[0:1, FF + 512 * half:FF + 512 * half + 512],
                        start=True, stop=True), at(ev_a), inc=ev_t)
                    b.op("scalar", lambda e, half=half: e.copy(
                        out=f2b_bc[:, 512 * half:512 * half + 512], in_=pf1a[:, 0:512]),
                        at(ev_t), inc=ev_a)
            p4pro = at(dma_g, ev_a)

            ev_v4 = b.sem("p4v2")
            ev_a4 = b.sem("p4a2")
            ld_ready = {}

            def p4_A(ci):
                pp = ci % 2
                w = (p4pro if ci == 0 else []) + at(ev_t, ev_v4)
                b.dma("gpsimd", s_xn2[:, pp, :, :],
                      xn2_fm_h[:, :, CH * ci:CH * (ci + 1)].rearrange("f p t -> p f t"),
                      w, inc=dma_g)
                b.dma("gpsimd", s_x2t[:, pp, :, :],
                      x2_tm_h[2 * ci:2 * ci + 2].rearrange("t p d -> p t d"), inc=dma_g)
                b.dma("gpsimd", s_xf[:, pp, :, :],
                      x_h.rearrange("t p d -> p t d")[:, 2 * ci:2 * ci + 2, :], inc=dma_g)
                b.dma("gpsimd", s_gt[:, pp, :, :],
                      g_tm_h[2 * ci:2 * ci + 2].rearrange("t p d -> p t d"), inc=dma_g)
                ld_ready[ci] = [(dma_g, dma_g.val)]

            def p4_B(ci):
                pp = ci % 2
                for hf in range(32):
                    pf1 = pf1a if hf % 2 == 0 else pf1b
                    gl = hf % 3
                    for dc in range(8):
                        w = ()
                        if dc == 0:
                            w = at(ev_a4) + (ld_ready[ci] if hf == 0 else [])
                        b.op("tensor", lambda e, hf=hf, dc=dc, pp=pp, pf1=pf1: e.matmul(
                            pf1[:, 0:CH], s_fc1w[:, dc, 128 * hf:128 * hf + 128], s_xn2[:, pp, dc, :],
                            start=(dc == 0), stop=(dc == 7)), w, inc=ev_t)
                    tv = ev_t.val
                    b.op("scalar", lambda e, hf=hf, gl=gl, pf1=pf1: e.activation(
                        out=s_gel[:, gl, :], in_=pf1[:, 0:CH], func=AF.Gelu,
                        bias=f1b_t[:, hf:hf + 1], scale=1.0),
                        [(ev_t, tv)], inc=ev_a4, self_ser=False)
                    av = ev_a4.val
                    for tt in range(2):
                        for half in range(2):
                            w = ()
                            if tt == 0 and half == 0:
                                w = [(ev_a4, av)] + (at(ev_v4) if hf == 0 else [])
                            b.op("tensor", lambda e, tt=tt, half=half, hf=hf, gl=gl: e.matmul(
                                pf2[2 * tt + half][:, 0:512],
                                s_gel[:, gl, 128 * tt:128 * tt + 128],
                                s_fc2w[:, hf, 512 * half:512 * half + 512],
                                start=(hf == 0), stop=(hf == 31)), w, inc=ev_t)
                tv = ev_t.val
                for tt in range(2):
                    for half in range(2):
                        w = [(ev_t, tv)] + at(ev_v4)
                        if tt == 0 and half == 0:
                            w += ld_ready[ci] + [(dma_g, dma_g.val)]
                        b.op("vector", lambda e, tt=tt, half=half: e.scalar_tensor_tensor(
                            out=s_tmp4v[:], in0=pf2[2 * tt + half][:, 0:512], scalar=0.0,
                            in1=f2b_bc[:, 512 * half:512 * half + 512], op0=ALU.add, op1=ALU.add),
                            w, inc=ev_v4, self_ser=False)
                        b.op("vector", lambda e, tt=tt, half=half, pp=pp: e.tensor_tensor(
                            out=s_s[:, tt, 512 * half:512 * half + 512], in0=s_tmp4v[:],
                            in1=s_x2t[:, pp, tt, 512 * half:512 * half + 512], op=ALU.add),
                            at(ev_v4), inc=ev_v4, self_ser=False)
                    b.op("vector", lambda e, tt=tt, pp=pp: e.tensor_tensor(
                        out=s_sg4v[:], in0=s_s[:, tt, :], in1=s_gt[:, pp, tt, :], op=ALU.mult),
                        at(ev_v4), inc=ev_v4, self_ser=False)
                    b.op("vector", lambda e, tt=tt, pp=pp: e.tensor_tensor(
                        out=s_out[:, tt, :], in0=s_sg4v[:], in1=s_xf[:, pp, tt, :], op=ALU.add),
                        at(ev_v4), inc=ev_v4, self_ser=False)
                b.dma("gpsimd", out_h[2 * ci:2 * ci + 2].rearrange("t p d -> p t d"), s_out[:],
                      at(ev_v4), inc=dma_g)

            p4_A(0)
            p4_A(1)
            for ci in range(NCH):
                p4_B(ci)
                if ci + 2 < NCH:
                    p4_A(ci + 2)

        b.wait("gpsimd", at(dma_g))
        b.wait("sync", at(dma_g, dma_s))
        b.wait("scalar", at(dma_a))

        block = ctx.enter_context(nc.Block())
        b.emit(block)
    return nc


def kernel(**inputs):
    x = np.ascontiguousarray(inputs["L0_attn_x"], dtype=np.float32)
    pb = np.ascontiguousarray(inputs["L0_attn_pos_bias"], dtype=np.float32)
    consts = {"pb_win": host_pb_win(pb), "pb_sp": host_pb_sparse(pb),
              "ident": np.eye(128, dtype=np.float32)}
    base = {
        "qkv_w": inputs["L0_attn_qkv_w"], "qkv_b": inputs["L0_attn_qkv_b"],
        "gate_w": inputs["L0_attn_gate_w"], "gate_b": inputs["L0_attn_gate_b"],
        "out_w": inputs["L0_attn_out_w"], "out_b": inputs["L0_attn_out_b"],
        "ln1_g": inputs["L0_attn_ln1_g"], "ln1_b": inputs["L0_attn_ln1_b"],
        "ln2_g": inputs["L0_attn_ln2_g"], "ln2_b": inputs["L0_attn_ln2_b"],
        "fc1_w": inputs["L0_attn_fc1_w"], "fc1_b": inputs["L0_attn_fc1_b"],
        "fc2_w": inputs["L0_attn_fc2_w"], "fc2_b": inputs["L0_attn_fc2_b"],
        "w_gate": inputs["L0_ffn_w_gate"], "w_up": inputs["L0_ffn_w_up"],
        "w_down": inputs["L0_ffn_w_down"],
    }
    base = {k: np.ascontiguousarray(v) for k, v in base.items()}
    nc = build_nc(debug=DEBUG)
    in_maps = []
    for core in range(8):
        m = {"x": np.ascontiguousarray(x[core].reshape(TT, 128, D))}
        m.update(base)
        m.update(consts)
        in_maps.append(m)
    res = run_bass_kernel_spmd(nc, in_maps, core_ids=list(range(8)))
    out = np.stack([res.results[i]["out"].reshape(NTOK, D) for i in range(8)], axis=0)
    return out.astype(np.float32)


# version 06
# speedup vs baseline: 1.0962x; 1.0030x over previous
import numpy as np
import ml_dtypes
from contextlib import ExitStack

import concourse.bass as bass
import concourse.mybir as mybir
from concourse.bass_utils import run_bass_kernel_spmd

F32 = mybir.dt.float32
BF16 = mybir.dt.bfloat16
I32 = mybir.dt.int32
AF = mybir.ActivationFunctionType
ALU = mybir.AluOpType
AX = mybir.AxisListType

NTOK = 4096
D = 1024
H = 16
HD = 64
FF = 4096
HQ = 3072
TT = NTOK // 128
RING = 16
CH = 256
NCH = NTOK // CH
QS = 1.0 / 127.0

WINSET = set(range(0, 33)) | {48, 64, 96, 128, 192, 256}
SPARSE = [384, 512, 768, 1024, 1536]
SP_M = [d // 128 for d in SPARSE]
OFFSETS = tuple(sorted(set(range(0, 33)) | {48, 64, 96, 128, 192, 256, 384, 512, 768, 1024, 1536}))
CLAMP = 480.0

DEBUG = ()


def host_pb_win(pos_bias):
    idx_of = {d: i for i, d in enumerate(OFFSETS)}
    pb = np.asarray(pos_bias, dtype=np.float32)
    out = np.full((16, 128, 3 * 128), -1e9, dtype=np.float32)
    tq = np.arange(128)
    for wt in range(3):
        for j in range(128):
            dl = tq - j + 128 * wt
            for d in WINSET:
                m = dl == d
                if m.any():
                    out[:, j, wt * 128 + np.nonzero(m)[0]] = pb[idx_of[d], :][:, None]
    return out


def host_pb_sparse(pos_bias):
    idx_of = {d: i for i, d in enumerate(OFFSETS)}
    pb = np.asarray(pos_bias, dtype=np.float32)
    out = np.full((13, 128, 128), -1e9, dtype=np.float32)
    for cv in range(13):
        for bi, dlt in enumerate(SPARSE):
            lo = dlt - 128 * cv
            if lo >= 128:
                continue
            lo = max(0, lo)
            for h in range(16):
                out[cv, lo:, 8 * h + bi] = pb[idx_of[dlt], h]
    return out


class Sem:
    __slots__ = ("h", "val")


class Builder:
    def __init__(self, nc, ctx):
        self.nc = nc
        self.ctx = ctx
        self.streams = {e: [] for e in ("tensor", "vector", "scalar", "gpsimd", "sync")}
        self.self_sems = {}
        self.all_ev = []
        self.all_dma = []
        self._n = 0

    def sem(self, name=None):
        s = Sem()
        s.h = self.ctx.enter_context(self.nc.semaphore(name or f"s{self._n}"))
        s.val = 0
        self._n += 1
        return s

    def op(self, eng, fn, waits=(), inc=None, inc_n=1, self_ser=True):
        wl = [(s.h, v) for (s, v) in waits]
        ss = self.self_sems.get(eng) if self_ser else None
        if ss is not None:
            wl.append((ss.h, ss.val))
            if inc is None:
                inc = ss
                inc_n = 1
        if inc is not None:
            inc.val += inc_n
        self.streams[eng].append((wl, fn, None if inc is None else (inc.h, inc_n)))

    def wait(self, eng, waits):
        self.streams[eng].append(([(s.h, v) for (s, v) in waits], None, None))

    def dma(self, eng, out, in_, waits=(), inc=None, **kw):
        self.op(eng, lambda e: e.dma_start(out=out, in_=in_, **kw), waits, inc, 16, self_ser=False)

    def trans(self, eng, out, in_, waits=(), inc=None):
        waits = list(waits) + [(s, s.val) for s in self.all_ev] + [(s, s.val) for s in self.all_dma]
        self.op(eng, lambda e: e.dma_start_transpose(out=out, in_=in_), waits, inc, 16, self_ser=False)

    def barrier(self, dma_sems, extra=()):
        for eng, s in dma_sems.items():
            self.wait(eng, [(s, s.val)])
        if extra:
            self.wait("gpsimd", [(s, s.val) for s in extra])
        bsem = self.sem()
        bsem.val = 5
        for e in self.streams:
            self.streams[e].append(([], lambda eng: eng.nop(), (bsem.h, 1)))
            self.streams[e].append(([(bsem.h, 5)], None, None))

    def emit(self, block):
        def make(eng_name):
            def body(e):
                for wl, fn, inc in self.streams[eng_name]:
                    for (h, v) in wl:
                        e.wait_ge(h, v)
                    if fn is None:
                        continue
                    r = fn(e)
                    if inc is not None:
                        r.then_inc(inc[0], inc[1])
            return body
        block.tensor(make("tensor"))
        block.vector(make("vector"))
        block.scalar(make("scalar"))
        block.gpsimd(make("gpsimd"))
        block.sync(make("sync"))


def at(*sems):
    return [(s, s.val) for s in sems]


def build_nc(debug=()):
    nc = bass.Bass("TRN2", target_bir_lowering=False)

    def din(name, shape, dt=F32):
        return nc.dram_tensor(name, shape, dt, kind="ExternalInput").ap()

    x_h = din("x", [TT, 128, D])
    qkvw_h = din("qkv_w", [D, 3 * D])
    qkvb_h = din("qkv_b", [3 * D])
    gatew_h = din("gate_w", [D, D])
    gateb_h = din("gate_b", [D])
    outw_h = din("out_w", [D, D])
    outb_h = din("out_b", [D])
    ln1g_h = din("ln1_g", [D])
    ln1b_h = din("ln1_b", [D])
    ln2g_h = din("ln2_g", [D])
    ln2b_h = din("ln2_b", [D])
    fc1w_h = din("fc1_w", [D, FF])
    fc1b_h = din("fc1_b", [FF])
    fc2w_h = din("fc2_w", [FF, D])
    fc2b_h = din("fc2_b", [D])
    wg_h = din("w_gate", [HQ, D], I32)
    wu_h = din("w_up", [HQ, D], I32)
    wd_h = din("w_down", [D, HQ], I32)
    pbw_h = din("pb_win", [16, 128, 384])
    ident_h = din("ident", [128, 128])
    pbs_h = din("pb_sp", [13, 128, 128])

    out_h = nc.dram_tensor("out", [TT, 128, D], F32, kind="ExternalOutput").ap()

    def scratch(name, shape, dt=BF16):
        if name in debug:
            return nc.dram_tensor(name, shape, dt, kind="ExternalOutput").ap()
        return nc.dram_tensor(name, shape, dt).ap()

    xn_tm_h = scratch("xn_tm", [TT, 128, D])
    xn1_fm_h = scratch("xn1_fm", [8, 128, NTOK])
    g_tm_h = scratch("g_tm", [TT, 128, D])
    attn_tm_h = scratch("attn_tm", [TT, 128, D])
    x2_tm_h = scratch("x2_tm", [TT, 128, D])
    xn2_fm_h = scratch("xn2_fm", [8, 128, NTOK])
    xh2_fm_h = scratch("xh2_fm", [8, 128, NTOK])

    ctx = ExitStack()
    with ctx:
        b = Builder(nc, ctx)
        ones_row = ctx.enter_context(nc.sbuf_tensor("ones_row", [1, 128], BF16))
        one1 = ctx.enter_context(nc.sbuf_tensor("one1", [1, 512], BF16))
        s_stats = ctx.enter_context(nc.sbuf_tensor("s_stats", [128, 16], F32))
        s_junk = ctx.enter_context(nc.sbuf_tensor("s_junk", [128, 1024], BF16))
        s_eps = ctx.enter_context(nc.sbuf_tensor("s_eps", [128, 1], F32))
        s_id = ctx.enter_context(nc.sbuf_tensor("s_id", [128, 128], BF16))

        dma_g = b.sem("dma_g")
        dma_s = b.sem("dma_s")
        dma_a = b.sem("dma_a")
        ev_t = b.sem("ev_t")
        ev_v = b.sem("ev_v")
        ev_a = b.sem("ev_a")
        ev_p = b.sem("ev_p")
        DMAS = {"gpsimd": dma_g, "sync": dma_s, "scalar": dma_a}
        xl1 = [b.sem("xl1a"), b.sem("xl1b")]
        st1 = [b.sem("st1a"), b.sem("st1b")]
        xl2 = [b.sem("xl2a"), b.sem("xl2b")]
        st2 = [b.sem("st2a"), b.sem("st2b")]
        xl3 = [b.sem("xl3a"), b.sem("xl3b")]
        st3 = [b.sem("st3a"), b.sem("st3b")]
        st4 = [b.sem("st4a"), b.sem("st4b")]
        ALLGP = [xl1[0], xl1[1], st1[0], st1[1], xl2[0], xl2[1], st2[0], st2[1],
                 xl3[0], xl3[1], st3[0], st3[1], st4[0], st4[1]]
        b.self_sems = {"vector": ev_v, "scalar": ev_a, "gpsimd": ev_p}
        b.all_ev = [ev_t, ev_v, ev_a]
        b.all_dma = [dma_g]

        b.op("gpsimd", lambda e: e.memset(ones_row[:], 1.0), inc=ev_p)
        b.op("gpsimd", lambda e: e.memset(s_eps[:], 1e-5), inc=ev_p)
        b.dma("gpsimd", s_id[:], ident_h, inc=dma_g)
        b.op("gpsimd", lambda e: e.memset(one1[:], 1.0), inc=ev_p)

        def ln_stats(src_ap, ln1=False):
            st = s_stats
            b.op("vector", lambda e: e.reduce_sum(out=st[:, 0:1], in_=src_ap, axis=AX.X), at(ev_a), inc=ev_v)
            b.op("scalar", lambda e: e.activation(
                out=s_junk[:, 0:src_ap.shape[-1]], in_=src_ap, func=AF.Square,
                accum_out=st[:, 1:2]), at(ev_v), inc=ev_a)
            b.op("vector", lambda e: e.tensor_scalar_mul(out=st[:, 0:1], in0=st[:, 0:1], scalar1=1.0 / D), inc=ev_v)
            b.op("vector", lambda e: e.tensor_scalar_mul(out=st[:, 1:2], in0=st[:, 1:2], scalar1=1.0 / D), at(ev_a), inc=ev_v)
            b.op("vector", lambda e: e.tensor_tensor(out=st[:, 2:3], in0=st[:, 0:1], in1=st[:, 0:1], op=ALU.mult), inc=ev_v)
            b.op("vector", lambda e: e.tensor_tensor(out=st[:, 2:3], in0=st[:, 1:2], in1=st[:, 2:3], op=ALU.subtract), inc=ev_v)
            b.op("scalar", lambda e: e.activation(out=st[:, 3:4], in_=st[:, 2:3], func=AF.Sqrt, bias=s_eps[:], scale=1.0),
                 at(ev_v, ev_p), inc=ev_a)
            b.op("vector", lambda e: e.reciprocal(out=st[:, 4:5], in_=st[:, 3:4]), at(ev_a), inc=ev_v)
            b.op("vector", lambda e: e.tensor_tensor(out=st[:, 5:6], in0=st[:, 0:1], in1=st[:, 4:5], op=ALU.mult), inc=ev_v)
            if ln1:
                b.op("vector", lambda e: e.tensor_tensor(out=st[:, 6:7], in0=st[:, 4:5], in1=st[:, 4:5], op=ALU.mult), inc=ev_v)
                b.op("vector", lambda e: e.tensor_tensor(out=st[:, 6:7], in0=st[:, 2:3], in1=st[:, 6:7], op=ALU.mult), inc=ev_v)
                b.op("scalar", lambda e: e.activation(out=st[:, 7:8], in_=st[:, 6:7], func=AF.Sqrt, bias=s_eps[:], scale=1.0),
                     at(ev_v, ev_p), inc=ev_a)
                b.op("vector", lambda e: e.reciprocal(out=st[:, 8:9], in_=st[:, 7:8]), at(ev_a), inc=ev_v)
                b.op("vector", lambda e: e.tensor_tensor(out=st[:, 9:10], in0=st[:, 4:5], in1=st[:, 8:9], op=ALU.mult), inc=ev_v)
                b.op("vector", lambda e: e.tensor_tensor(out=st[:, 10:11], in0=st[:, 0:1], in1=st[:, 9:10], op=ALU.mult), inc=ev_v)

        def ln_apply(dst_ap, src_ap, rstd_col, mur_col):
            st = s_stats
            mur_view = st[:, mur_col:mur_col + 1].broadcast_to([128, dst_ap.shape[-1]])
            b.op("vector", lambda e: e.scalar_tensor_tensor(
                out=dst_ap, in0=src_ap, scalar=st[:, rstd_col:rstd_col + 1], in1=mur_view,
                op0=ALU.mult, op1=ALU.subtract), inc=ev_v)

        with ExitStack() as p1:
            sb1 = lambda name, shape, dt: p1.enter_context(nc.sbuf_tensor(name, shape, dt))
            ps1 = lambda name: p1.enter_context(nc.psum_tensor(name, [128, 512], F32))
            wgT = sb1("wgT", [128, 8, HQ], BF16)
            wuT = sb1("wuT", [128, 8, HQ], BF16)
            wdT = sb1("wdT", [128, 24, D], BF16)
            g1t = sb1("g1t", [128, 8], F32)
            b1t = sb1("b1t", [128, 8], F32)
            pg = ps1("pg")
            pu = ps1("pu")
            pd = [ps1(f"pd{i}") for i in range(4)]

            with ExitStack() as pro:
                stg = pro.enter_context(nc.sbuf_tensor("stg", [128, 24 * D], BF16))
                rowbuf1 = pro.enter_context(nc.sbuf_tensor("rowbuf1", [1, 2 * D], BF16))
                b.dma("gpsimd", rowbuf1[0:1, 0:D], ln1g_h.unsqueeze(0), at(ev_p), inc=dma_g)
                b.dma("gpsimd", rowbuf1[0:1, D:2 * D], ln1b_h.unsqueeze(0), inc=dma_g)
                gv = dma_g.val
                for i in range(8):
                    b.op("tensor", lambda e, i=i: e.matmul(
                        pg[:, i:i + 1], rowbuf1[0:1, 128 * i:128 * i + 128], one1[0:1, 0:1],
                        start=True, stop=True), [(dma_g, gv)] if i == 0 else (), inc=ev_t)
                    b.op("tensor", lambda e, i=i: e.matmul(
                        pg[:, 8 + i:9 + i], rowbuf1[0:1, D + 128 * i:D + 128 * i + 128], one1[0:1, 0:1],
                        start=True, stop=True), inc=ev_t)
                b.op("scalar", lambda e: e.copy(out=g1t[:], in_=pg[:, 0:8]), at(ev_t), inc=ev_a)
                b.op("scalar", lambda e: e.copy(out=b1t[:], in_=pg[:, 8:16]), inc=ev_a)

                ptw_t = [pro.enter_context(nc.psum_tensor(f"ptw{i}", [128, 1024], BF16)) for i in range(2)]
                ptw = [p[:, 0:128] for p in ptw_t]
                widt = dma_g.val
                for (w_h, dstT, src_tiles, dst_chunks) in (
                        (wg_h, wgT, 24, 8), (wu_h, wuT, 24, 8), (wd_h, wdT, 8, 24)):
                    inner = 128 * dst_chunks
                    stg_v = stg[:].rearrange("p (c d) -> p c d", d=inner)
                    b.dma("gpsimd", stg_v[:, 0:src_tiles, :],
                          w_h.rearrange("(c p) d -> p c d", p=128), at(ev_v, ev_a), inc=dma_g)
                    gv = dma_g.val
                    nslot = 2
                    k = 0
                    for sc in range(src_tiles):
                        for dc in range(dst_chunks):
                            sl = k % nslot
                            w = [(dma_g, max(gv, widt))]
                            if k >= nslot:
                                w += at(ev_v, ev_a)
                            b.op("tensor", lambda e, sc=sc, dc=dc, sl=sl, stg_v=stg_v: e.transpose(
                                ptw[sl], stg_v[:, sc, 128 * dc:128 * dc + 128], s_id[:]),
                                w, inc=ev_t)
                            tvw = ev_t.val
                            eng = "vector" if k % 2 == 0 else "scalar"
                            if eng == "vector":
                                b.op("vector", lambda e, sc=sc, dc=dc, sl=sl, dstT=dstT: e.tensor_copy(
                                    out=dstT[:, dc, 128 * sc:128 * sc + 128], in_=ptw[sl]),
                                    [(ev_t, tvw)], inc=ev_v)
                            else:
                                b.op("scalar", lambda e, sc=sc, dc=dc, sl=sl, dstT=dstT: e.copy(
                                    out=dstT[:, dc, 128 * sc:128 * sc + 128], in_=ptw[sl]),
                                    [(ev_t, tvw)], inc=ev_a)
                            k += 1
            tr_done = at(ev_v, ev_a)

            s_x = sb1("s_x", [128, 2, 2, D], F32)
            s_xn = sb1("s_xn", [128, 2, 2, D], BF16)
            s_xh1 = sb1("s_xh1", [128, 2, 2, D], BF16)
            s_xnfm = sb1("s_xnfm", [128, 2, 8, CH], BF16)
            s_xh1fm = sb1("s_xh1fm", [128, 2, 8, CH], BF16)
            s_xn1fm = sb1("s_xn1fm", [128, 8, CH], BF16)
            s_sg1 = sb1("s_sg1", [128, 2, CH], BF16)
            s_su = sb1("s_su", [128, 2, CH], BF16)
            s_hp = sb1("s_hp", [128, 2, CH], BF16)
            s_g = sb1("s_g", [128, 2, D], BF16)

            ev_v2 = b.sem("p1v2")
            ev_a2 = b.sem("p1a2")
            xn_ready = {}
            xload = {}

            ptr1_t = [p1.enter_context(nc.psum_tensor(f"ptr1{i}", [128, 1024], BF16)) for i in range(2)]
            ptr1 = [p[:, 0:128] for p in ptr1_t]
            lnv = {}

            def p1_A(ci):
                pp = ci % 2
                if ci == 0:
                    b.dma("gpsimd", s_x[:, 0, :, :],
                          x_h.rearrange("t p d -> p t d")[:, 0:2, :],
                          tr_done + at(ev_t), inc=xl1[0])
                if ci + 1 < NCH:
                    b.dma("gpsimd", s_x[:, (ci + 1) % 2, :, :],
                          x_h.rearrange("t p d -> p t d")[:, 2 * (ci + 1):2 * (ci + 1) + 2, :],
                          at(ev_v, ev_a), inc=xl1[(ci + 1) % 2])
                b.wait("vector", at(xl1[pp]))
                b.wait("scalar", at(xl1[pp]))
                for tt in range(2):
                    src_ap = s_x[:, pp, tt, :]
                    ln_stats(src_ap, ln1=True)
                    ln_apply(s_xn[:, pp, tt, :], src_ap, 4, 5)
                    ln_apply(s_xh1[:, pp, tt, :], src_ap, 9, 10)
                lnv[ci] = ev_v.val

            def p1_A_trans(ci):
                pp = ci % 2
                k = 0
                for tt in range(2):
                    for (srcT, dstT2) in ((s_xn, s_xnfm), (s_xh1, s_xh1fm)):
                        for ft in range(8):
                            sl = k % 2
                            w = [(ev_v, lnv[ci])]
                            if k >= 2:
                                w += at(ev_v, ev_a)
                            b.op("tensor", lambda e, tt=tt, ft=ft, sl=sl, srcT=srcT, pp=pp: e.transpose(
                                ptr1[sl], srcT[:, pp, tt, 128 * ft:128 * ft + 128], s_id[:]),
                                w, inc=ev_t)
                            tvw = ev_t.val
                            if k % 2 == 0:
                                b.op("vector", lambda e, tt=tt, ft=ft, sl=sl, dstT2=dstT2, pp=pp: e.tensor_copy(
                                    out=dstT2[:, pp, ft, 128 * tt:128 * tt + 128], in_=ptr1[sl]),
                                    [(ev_t, tvw)], inc=ev_v)
                            else:
                                b.op("scalar", lambda e, tt=tt, ft=ft, sl=sl, dstT2=dstT2, pp=pp: e.copy(
                                    out=dstT2[:, pp, ft, 128 * tt:128 * tt + 128], in_=ptr1[sl]),
                                    [(ev_t, tvw)], inc=ev_a)
                            k += 1
                xn_ready[ci] = at(ev_v, ev_a)

            def p1_A_late(ci):
                pp = ci % 2
                for ft in range(8):
                    b.op("scalar", lambda e, ft=ft, pp=pp: e.activation(
                        out=s_xn1fm[:, ft, :], in_=s_xh1fm[:, pp, ft, :], func=AF.Identity,
                        bias=b1t[:, ft:ft + 1], scale=g1t[:, ft:ft + 1]),
                        (xn_ready[ci] + at(st1[0], st1[1])) if ft == 0 else (), inc=ev_a)
                b.dma("gpsimd", xn_tm_h[2 * ci:2 * ci + 2].rearrange("t p d -> p t d"),
                      s_xn[:, pp, :, :], xn_ready[ci] + at(ev_v), inc=st1[pp])
                b.dma("gpsimd", xn1_fm_h[:, :, CH * ci:CH * (ci + 1)].rearrange("f p t -> p f t"),
                      s_xn1fm[:], at(ev_a), inc=st1[pp])

            def p1_B(ci):
                pp = ci % 2
                for hfc in range(24):
                    hp_i = hfc % 2
                    for dc in range(8):
                        w = ()
                        if dc == 0:
                            w = at(ev_a2, ev_v2) + (xn_ready[ci] if hfc == 0 else [])
                            if hfc == 0 and ci == 0:
                                w = w + tr_done
                        b.op("tensor", lambda e, dc=dc, hfc=hfc, pp=pp: e.matmul(
                            pg[:, 0:CH], wgT[:, dc, 128 * hfc:128 * hfc + 128], s_xnfm[:, pp, dc, :],
                            start=(dc == 0), stop=(dc == 7)), w, inc=ev_t)
                    for dc in range(8):
                        b.op("tensor", lambda e, dc=dc, hfc=hfc, pp=pp: e.matmul(
                            pu[:, 0:CH], wuT[:, dc, 128 * hfc:128 * hfc + 128], s_xnfm[:, pp, dc, :],
                            start=(dc == 0), stop=(dc == 7)), inc=ev_t)
                    tv = ev_t.val
                    b.op("scalar", lambda e, hp_i=hp_i: e.activation(
                        out=s_sg1[:, hp_i, :], in_=pg[:, 0:CH], func=AF.Silu, scale=QS),
                        [(ev_t, tv)] + at(ev_v2), inc=ev_a2, self_ser=False)
                    b.op("vector", lambda e, hp_i=hp_i: e.tensor_copy(
                        out=s_su[:, hp_i, :], in_=pu[:, 0:CH]),
                        [(ev_t, tv)] + at(ev_v2), inc=ev_v2, self_ser=False)
                    b.op("vector", lambda e, hp_i=hp_i: e.tensor_tensor(
                        out=s_hp[:, hp_i, :], in0=s_sg1[:, hp_i, :], in1=s_su[:, hp_i, :],
                        op=ALU.mult), at(ev_a2, ev_v2), inc=ev_v2, self_ser=False)
                    hv = ev_v2.val
                    for tt in range(2):
                        for half in range(2):
                            w = ()
                            if tt == 0 and half == 0:
                                w = [(ev_v2, hv)] + (at(ev_a2) if hfc == 0 else [])
                            b.op("tensor", lambda e, tt=tt, half=half, hfc=hfc, hp_i=hp_i: e.matmul(
                                pd[2 * tt + half][:, 0:512],
                                s_hp[:, hp_i, 128 * tt:128 * tt + 128],
                                wdT[:, hfc, 512 * half:512 * half + 512],
                                start=(hfc == 0), stop=(hfc == 23)), w, inc=ev_t)
                tv = ev_t.val
                for tt in range(2):
                    for half in range(2):
                        b.op("scalar", lambda e, tt=tt, half=half: e.activation(
                            out=s_g[:, tt, 512 * half:512 * half + 512],
                            in_=pd[2 * tt + half][:, 0:512], func=AF.Silu, scale=QS * QS),
                            [(ev_t, tv)] + at(st1[0], st1[1], ev_v2), inc=ev_a2, self_ser=False)
                b.dma("gpsimd", g_tm_h[2 * ci:2 * ci + 2].rearrange("t p d -> p t d"), s_g[:],
                      at(ev_a2), inc=st1[pp])

            p1_A(0)
            p1_A_trans(0)
            for ci in range(NCH):
                if ci + 1 < NCH:
                    p1_A(ci + 1)
                p1_B(ci)
                if ci + 1 < NCH:
                    p1_A_trans(ci + 1)
                p1_A_late(ci)

            b.barrier(DMAS, extra=ALLGP)

        with ExitStack() as p2:
            sb2 = lambda name, shape, dt: p2.enter_context(nc.sbuf_tensor(name, shape, dt))
            ps2 = lambda name: p2.enter_context(nc.psum_tensor(name, [128, 512], F32))
            s_qkvw = sb2("s_qkvw", [128, 8, 3 * D], BF16)
            qkb_t = sb2("qkb_t", [128, 16], F32)
            vb_bc = sb2("vb_bc", [128, D], F32)
            kfm = sb2("kfm", [128, 8, 8, 128], BF16)
            ktm = sb2("ktm", [128, RING, D], BF16)
            vtm = sb2("vtm", [128, RING, H, 65], BF16)
            pb8w = sb2("pb8w", [128, 16, 384], BF16)
            pb8e = sb2("pb8e", [128, 128], BF16)
            pb8s = sb2("pb8s", [128, 13, 128], BF16)
            s_xn1 = sb2("s_xn1", [128, 2, 8, CH], BF16)
            s_qfm = sb2("s_qfm", [128, 2, 8, CH], BF16)
            s_qtm = sb2("s_qtm", [128, 2, 2, D], BF16)
            s_pt = sb2("s_pt", [128, 4, 384], BF16)
            s_acc = sb2("s_acc", [128, H, HD], F32)
            s_den = sb2("s_den", [128, 16], F32)
            s_sprod = sb2("s_sprod", [128, D], BF16)
            s_ssc = sb2("s_ssc", [128, 128], F32)
            s_psp = sb2("s_psp", [128, 128], BF16)
            s_attn = sb2("s_attn", [128, 2, D], BF16)
            psA = [ps2("psA0"), ps2("psA1")]
            psB = [ps2("psB0"), ps2("psB1")]
            pav = ps2("pav")
            pq = ps2("pq")
            pv0 = ps2("pv0")
            pv1 = ps2("pv1")

            b.dma("gpsimd", s_qkvw[:], qkvw_h.rearrange("(c p) f -> p c f", p=128), inc=dma_g)
            with ExitStack() as pro:
                rowbuf2v = pro.enter_context(nc.sbuf_tensor("rowbuf2", [1, 3 * D], BF16))
                b.dma("gpsimd", rowbuf2v[0:1, :], qkvb_h.unsqueeze(0), inc=dma_g)
                gv = dma_g.val
                for i in range(16):
                    b.op("tensor", lambda e, i=i: e.matmul(
                        pq[:, i:i + 1], rowbuf2v[0:1, 128 * i:128 * i + 128], one1[0:1, 0:1],
                        start=True, stop=True), [(dma_g, gv)] if i == 0 else (), inc=ev_t)
                b.op("scalar", lambda e: e.copy(out=qkb_t[:], in_=pq[:, 0:16]), at(ev_t), inc=ev_a)
                for half in range(2):
                    b.op("tensor", lambda e, half=half: e.matmul(
                        pv0[:, 0:512], ones_row[0:1, :],
                        rowbuf2v[0:1, 2 * D + 512 * half:2 * D + 512 * half + 512],
                        start=True, stop=True), at(ev_a), inc=ev_t)
                    b.op("scalar", lambda e, half=half: e.copy(
                        out=vb_bc[:, 512 * half:512 * half + 512], in_=pv0[:, 0:512]),
                        at(ev_t), inc=ev_a)
            b.dma("gpsimd", pb8w[:], pbw_h.rearrange("h p w -> p h w"), inc=dma_g)
            b.dma("gpsimd", pb8s[:], pbs_h.rearrange("c p w -> p c w"), inc=dma_g)
            gv = dma_g.val
            b.op("vector", lambda e: e.tensor_scalar_mul(out=pb8w[:], in0=pb8w[:], scalar1=8.0),
                 [(dma_g, gv)], inc=ev_v)
            b.op("vector", lambda e: e.tensor_scalar_mul(out=pb8s[:], in0=pb8s[:], scalar1=8.0), inc=ev_v)
            b.op("gpsimd", lambda e: e.memset(pb8e[:], -8e9), inc=ev_p)
            b.op("gpsimd", lambda e: e.memset(s_ssc[:], 0.0), inc=ev_p)
            b.op("gpsimd", lambda e: e.memset(
                vtm[:].rearrange("p r h w -> p (r h) w")[:, :, 64:65], 1.0), inc=ev_p)
            for par in range(2):
                b.op("vector", lambda e, par=par: e.memset(psA[par][:, 0:384], 0.0), inc=ev_v)
                b.op("vector", lambda e, par=par: e.memset(psB[par][:, 0:384], 0.0), inc=ev_v)
            p2pro = at(dma_g, ev_v, ev_a, ev_p)

            ev_v3 = b.sem("p2v3")
            ev_a3 = b.sem("p2a3")
            dma_sp = [b.sem("p2s0"), b.sem("p2s1")]
            ready2 = {}

            def p2_A(ci):
                pp = ci % 2
                b.dma("gpsimd", s_xn1[:, pp, :, :],
                      xn1_fm_h[:, :, CH * ci:CH * (ci + 1)].rearrange("f p t -> p f t"),
                      (p2pro if ci == 0 else []) + at(ev_t, ev_v, ev_a), inc=xl2[pp])
                xldw = at(xl2[pp])
                b.wait("vector", xldw)
                b.wait("scalar", xldw)
                for qf in range(8):
                    pqx = pq if qf % 2 == 0 else pv0
                    for dc in range(8):
                        b.op("tensor", lambda e, qf=qf, dc=dc, pp=pp, pqx=pqx: e.matmul(
                            pqx[:, 0:CH], s_qkvw[:, dc, 128 * qf:128 * qf + 128], s_xn1[:, pp, dc, :],
                            start=(dc == 0), stop=(dc == 7)),
                            (xldw + at(ev_a)) if dc == 0 else (), inc=ev_t)
                    tv = ev_t.val
                    b.op("scalar", lambda e, qf=qf, pp=pp, pqx=pqx: e.activation(
                        out=s_qfm[:, pp, qf, :], in_=pqx[:, 0:CH], func=AF.Identity,
                        bias=qkb_t[:, qf:qf + 1], scale=1.0), [(ev_t, tv)], inc=ev_a)
                slot0 = (2 * ci) % 8
                for kf in range(8):
                    pqx = pq if kf % 2 == 0 else pv0
                    for dc in range(8):
                        b.op("tensor", lambda e, kf=kf, dc=dc, pp=pp, pqx=pqx: e.matmul(
                            pqx[:, 0:CH], s_qkvw[:, dc, D + 128 * kf:D + 128 * kf + 128],
                            s_xn1[:, pp, dc, :],
                            start=(dc == 0), stop=(dc == 7)), at(ev_a) if dc == 0 else (), inc=ev_t)
                    tv = ev_t.val
                    b.op("scalar", lambda e, kf=kf, slot0=slot0, pqx=pqx: e.activation(
                        out=kfm[:, slot0:slot0 + 2, kf, :],
                        in_=pqx[:, 0:CH].rearrange("p (s t) -> p s t", s=2), func=AF.Identity,
                        bias=qkb_t[:, 8 + kf:9 + kf], scale=1.0), [(ev_t, tv)], inc=ev_a)
                for tt in range(2):
                    pvp = pv1
                    slot = (2 * ci + tt) % RING
                    for half in range(2):
                        for dc in range(8):
                            b.op("tensor", lambda e, tt=tt, half=half, dc=dc, pp=pp, pvp=pvp: e.matmul(
                                pvp[:, 0:512],
                                s_xn1[:, pp, dc, 128 * tt:128 * tt + 128],
                                s_qkvw[:, dc, 2 * D + 512 * half:2 * D + 512 * half + 512],
                                start=(dc == 0), stop=(dc == 7)),
                                at(ev_v) if dc == 0 else (), inc=ev_t)
                        tv = ev_t.val
                        b.op("vector", lambda e, slot=slot, half=half, pvp=pvp: e.scalar_tensor_tensor(
                            out=vtm[:, slot, 8 * half:8 * half + 8, 0:64],
                            in0=pvp[:, 0:512].rearrange("p (h w) -> p h w", w=64),
                            scalar=1.0,
                            in1=vb_bc[:, 512 * half:512 * half + 512].rearrange("p (h w) -> p h w", w=64),
                            op0=ALU.mult, op1=ALU.add), [(ev_t, tv)] + at(ev_p), inc=ev_v)
                av = ev_a.val
                for tt in range(2):
                    slot = (2 * ci + tt) % RING
                    for ft in range(8):
                        b.trans("sync", s_qtm[:, pp, tt, 128 * ft:128 * ft + 128],
                                s_qfm[:, pp, ft, 128 * tt:128 * tt + 128],
                                [(ev_a, av)], inc=dma_sp[pp])
                        b.trans("sync", ktm[:, slot, 128 * ft:128 * ft + 128],
                                kfm[:, slot % 8, ft, :],
                                [(ev_a, av)], inc=dma_sp[pp])
                ready2[ci] = at(dma_sp[pp], ev_v, ev_a)

            def p2_B(ci):
                pp = ci % 2
                for tt in range(2):
                    c = 2 * ci + tt
                    for bi, m in enumerate(SP_M):
                        if m > c:
                            continue
                        slot_s = (c - m) % RING
                        b.op("vector", lambda e, tt=tt, slot_s=slot_s, pp=pp: e.tensor_tensor(
                            out=s_sprod[:], in0=s_qtm[:, pp, tt, :], in1=ktm[:, slot_s, :],
                            op=ALU.mult), ready2[ci] + at(ev_p, ev_v3), inc=ev_v3, self_ser=False)
                        b.op("vector", lambda e, bi=bi: e.tensor_reduce(
                            out=s_ssc[:].rearrange("p (h w) -> p h w", w=8)[:, :, bi:bi + 1],
                            in_=s_sprod[:].rearrange("p (h w) -> p h w", w=64),
                            axis=AX.X, op=ALU.add), at(ev_v3), inc=ev_v3, self_ser=False)
                    cv = min(c, 12)
                    b.op("vector", lambda e, cv=cv: e.scalar_tensor_tensor(
                        out=s_ssc[:], in0=s_ssc[:], scalar=CLAMP, in1=pb8s[:, cv, :],
                        op0=ALU.min, op1=ALU.add), at(ev_v3), inc=ev_v3, self_ser=False)
                    vv = ev_v3.val
                    b.op("scalar", lambda e: e.activation(
                        out=s_psp[:], in_=s_ssc[:], func=AF.Exp, scale=0.125),
                        [(ev_v3, vv)], inc=ev_a3, self_ser=False)
                    spv = ev_a3.val

                    qk_done = {}

                    def qk(hp):
                        par = hp % 2
                        for wt in range(3):
                            if c - wt < 0:
                                continue
                            slot_w = (c - wt) % 8
                            wA = ()
                            if wt == 0:
                                wA = at(ev_v3, ev_a3)
                                if hp == 0:
                                    wA = wA + ready2[ci]
                            b.op("tensor", lambda e, hp=hp, wt=wt, slot_w=slot_w, tt=tt, par=par, pp=pp: e.matmul(
                                psA[par][:, 128 * wt:128 * wt + 128],
                                kfm[0:64, slot_w, hp, :], s_qfm[0:64, pp, hp, 128 * tt:128 * tt + 128],
                                start=True, stop=True, tile_position=(0, 0)), wA, inc=ev_t)
                            b.op("tensor", lambda e, hp=hp, wt=wt, slot_w=slot_w, tt=tt, par=par, pp=pp: e.matmul(
                                psB[par][:, 128 * wt:128 * wt + 128],
                                kfm[64:128, slot_w, hp, :], s_qfm[64:128, pp, hp, 128 * tt:128 * tt + 128],
                                start=True, stop=True, tile_position=(64, 0)), inc=ev_t)
                        qk_done[hp] = ev_t.val

                    qk(0)
                    for hp in range(8):
                        if hp + 1 < 8:
                            qk(hp + 1)
                        par = hp % 2
                        tv = qk_done[hp]
                        for (pst, h) in ((psA[par], 2 * hp), (psB[par], 2 * hp + 1)):
                            if c >= 2:
                                b.op("vector", lambda e, pst=pst, h=h: e.scalar_tensor_tensor(
                                    out=pst[:, 0:384], in0=pst[:, 0:384], scalar=CLAMP,
                                    in1=pb8w[:, h, :], op0=ALU.min, op1=ALU.add),
                                    [(ev_t, tv)] + at(ev_a3), inc=ev_v3, self_ser=False)
                            else:
                                for wt in range(3):
                                    srcw = pb8w[:, h, 128 * wt:128 * wt + 128] if wt <= c else pb8e[:]
                                    b.op("vector", lambda e, pst=pst, wt=wt, srcw=srcw: e.scalar_tensor_tensor(
                                        out=pst[:, 128 * wt:128 * wt + 128],
                                        in0=pst[:, 128 * wt:128 * wt + 128], scalar=CLAMP,
                                        in1=srcw, op0=ALU.min, op1=ALU.add),
                                        ([(ev_t, tv)] + at(ev_a3)) if wt == 0 else (),
                                        inc=ev_v3, self_ser=False)
                        vv = ev_v3.val
                        b.op("scalar", lambda e, par=par: e.activation(
                            out=s_pt[:, 2 * par, :], in_=psA[par][:, 0:384], func=AF.Exp, scale=0.125),
                            [(ev_v3, vv)], inc=ev_a3, self_ser=False)
                        av2 = ev_a3.val
                        h = 2 * hp
                        for wt in range(3):
                            if c - wt < 0:
                                continue
                            slot_w = (c - wt) % RING
                            b.op("tensor", lambda e, wt=wt, slot_w=slot_w, par=par, h=h: e.matmul(
                                pav[:, 128 * par:128 * par + 65],
                                s_pt[:, 2 * par, 128 * wt:128 * wt + 128],
                                vtm[:, slot_w, h, 0:65],
                                start=(wt == 0), stop=(wt == min(2, c))),
                                ([(ev_a3, av2)] + at(ev_v3)) if wt == 0 else (), inc=ev_t)
                        tv2 = ev_t.val
                        b.op("vector", lambda e, h=h, par=par: e.tensor_copy(
                            out=s_acc[:, h, :], in_=pav[:, 128 * par:128 * par + 64]),
                            [(ev_t, tv2)], inc=ev_v3, self_ser=False)
                        b.op("vector", lambda e, h=h, par=par: e.tensor_copy(
                            out=s_den[:, h:h + 1], in_=pav[:, 128 * par + 64:128 * par + 65]),
                            at(ev_v3), inc=ev_v3, self_ser=False)
                        accv = ev_v3.val
                        b.op("scalar", lambda e, par=par: e.activation(
                            out=s_pt[:, 2 * par + 1, :], in_=psB[par][:, 0:384], func=AF.Exp, scale=0.125),
                            at(ev_v3) if False else (), inc=ev_a3, self_ser=False)
                        av2 = ev_a3.val
                        h = 2 * hp + 1
                        for wt in range(3):
                            if c - wt < 0:
                                continue
                            slot_w = (c - wt) % RING
                            b.op("tensor", lambda e, wt=wt, slot_w=slot_w, par=par, h=h: e.matmul(
                                pav[:, 256 + 128 * par:256 + 128 * par + 65],
                                s_pt[:, 2 * par + 1, 128 * wt:128 * wt + 128],
                                vtm[:, slot_w, h, 0:65],
                                start=(wt == 0), stop=(wt == min(2, c))),
                                [(ev_a3, av2), (ev_v3, accv)] if wt == 0 else (), inc=ev_t)
                        tv2 = ev_t.val
                        b.op("vector", lambda e, h=h, par=par: e.tensor_copy(
                            out=s_acc[:, h, :], in_=pav[:, 256 + 128 * par:256 + 128 * par + 64]),
                            [(ev_t, tv2)] + at(ev_v3), inc=ev_v3, self_ser=False)
                        b.op("vector", lambda e, h=h, par=par: e.tensor_copy(
                            out=s_den[:, h:h + 1], in_=pav[:, 256 + 128 * par + 64:256 + 128 * par + 65]),
                            at(ev_v3), inc=ev_v3, self_ser=False)

                    accv = ev_v3.val
                    for bi, m in enumerate(SP_M):
                        if m > c:
                            continue
                        slot_s = (c - m) % RING
                        pview = s_psp[:].rearrange("p (h w) -> p h w", w=8)[:, :, bi:bi + 1] \
                            .broadcast_to([128, 16, 64])
                        b.op("vector", lambda e, slot_s=slot_s, pview=pview: e.tensor_tensor(
                            out=s_sprod[:].rearrange("p (h w) -> p h w", w=64),
                            in0=vtm[:, slot_s, :, 0:64], in1=pview, op=ALU.mult),
                            [(ev_a3, spv), (ev_p, ev_p.val), (ev_v3, accv)], inc=ev_v3, self_ser=False)
                        vv = ev_v3.val
                        b.op("gpsimd", lambda e: e.tensor_tensor(
                            out=s_acc[:].rearrange("p h w -> p (h w)"),
                            in0=s_acc[:].rearrange("p h w -> p (h w)"), in1=s_sprod[:],
                            op=ALU.add), [(ev_v3, vv)], inc=ev_p)
                    b.op("vector", lambda e: e.tensor_reduce(
                        out=s_stats[:, 0:16],
                        in_=s_psp[:].rearrange("p (h w) -> p h w", w=8),
                        axis=AX.X, op=ALU.add), [(ev_a3, spv)] + at(ev_v3), inc=ev_v3, self_ser=False)
                    b.op("vector", lambda e: e.tensor_tensor(
                        out=s_den[:], in0=s_den[:], in1=s_stats[:, 0:16], op=ALU.add),
                        at(ev_v3), inc=ev_v3, self_ser=False)
                    b.op("vector", lambda e: e.reciprocal(out=s_den[:], in_=s_den[:]),
                        at(ev_v3), inc=ev_v3, self_ser=False)
                    pvv = ev_p.val
                    b.op("vector", lambda e, tt=tt: e.tensor_tensor(
                        out=s_attn[:, tt, :].rearrange("p (h w) -> p h w", w=64),
                        in0=s_acc[:], in1=s_den[:].unsqueeze(2).broadcast_to([128, 16, 64]),
                        op=ALU.mult), [(ev_p, pvv)] + at(st2[0], st2[1], ev_v3),
                        inc=ev_v3, self_ser=False)
                b.dma("gpsimd", attn_tm_h[2 * ci:2 * ci + 2].rearrange("t p d -> p t d"), s_attn[:],
                      at(ev_v3), inc=st2[pp])

            p2_A(0)
            for ci in range(NCH):
                if ci + 1 < NCH:
                    p2_A(ci + 1)
                p2_B(ci)

            b.wait("sync", at(dma_sp[0], dma_sp[1]))
            b.barrier(DMAS, extra=ALLGP)

        with ExitStack() as p3:
            sb3 = lambda name, shape, dt: p3.enter_context(nc.sbuf_tensor(name, shape, dt))
            ps3 = lambda name: p3.enter_context(nc.psum_tensor(name, [128, 512], F32))
            s_gatew = sb3("s_gatew", [128, 8, D], BF16)
            s_outw = sb3("s_outw", [128, 8, D], BF16)
            gb_bc = sb3("gb_bc", [128, D], F32)
            ob_bc = sb3("ob_bc", [128, D], F32)
            g2t = sb3("g2t", [128, 8], F32)
            b2t = sb3("b2t", [128, 8], F32)
            s_xn1c = sb3("s_xn1b", [128, 2, 8, CH], BF16)
            s_at = sb3("s_at", [128, 2, 2, D], BF16)
            s_xnt = sb3("s_xnt", [128, 2, 2, D], BF16)
            s_gate = sb3("s_gate", [128, 2, D], BF16)
            s_agfm = sb3("s_agfm", [128, 8, CH], BF16)
            s_x2 = sb3("s_x2", [128, 2, D], BF16)
            s_xh2 = sb3("s_xh2", [128, 2, D], BF16)
            s_xh2fm = sb3("s_xh2fm", [128, 8, CH], BF16)
            s_xn2fm = sb3("s_xn2fm", [128, 8, CH], BF16)
            s_tmp3v = sb3("s_tmp3", [128, 512], F32)
            pgt = ps3("pgt")
            po = ps3("po")
            ptr3_t = [p3.enter_context(nc.psum_tensor(f"ptr3{i}", [128, 1024], BF16)) for i in range(2)]
            ptr3 = [p[:, 0:128] for p in ptr3_t]

            b.dma("gpsimd", s_gatew[:], gatew_h.rearrange("(c p) f -> p c f", p=128), inc=dma_g)
            b.dma("gpsimd", s_outw[:], outw_h.rearrange("(c p) f -> p c f", p=128), inc=dma_g)
            with ExitStack() as pro:
                rowbuf3v = pro.enter_context(nc.sbuf_tensor("rowbuf3", [1, 4 * D], BF16))
                b.dma("gpsimd", rowbuf3v[0:1, 0:D], gateb_h.unsqueeze(0), inc=dma_g)
                b.dma("gpsimd", rowbuf3v[0:1, D:2 * D], outb_h.unsqueeze(0), inc=dma_g)
                b.dma("gpsimd", rowbuf3v[0:1, 2 * D:3 * D], ln2g_h.unsqueeze(0), inc=dma_g)
                b.dma("gpsimd", rowbuf3v[0:1, 3 * D:4 * D], ln2b_h.unsqueeze(0), inc=dma_g)
                gv = dma_g.val
                for half in range(2):
                    b.op("tensor", lambda e, half=half: e.matmul(
                        pgt[:, 0:512], ones_row[0:1, :], rowbuf3v[0:1, 512 * half:512 * half + 512],
                        start=True, stop=True), [(dma_g, gv)] + at(ev_a), inc=ev_t)
                    b.op("scalar", lambda e, half=half: e.copy(
                        out=gb_bc[:, 512 * half:512 * half + 512], in_=pgt[:, 0:512]),
                        at(ev_t), inc=ev_a)
                    b.op("tensor", lambda e, half=half: e.matmul(
                        pgt[:, 0:512], ones_row[0:1, :],
                        rowbuf3v[0:1, D + 512 * half:D + 512 * half + 512],
                        start=True, stop=True), at(ev_a), inc=ev_t)
                    b.op("scalar", lambda e, half=half: e.copy(
                        out=ob_bc[:, 512 * half:512 * half + 512], in_=pgt[:, 0:512]),
                        at(ev_t), inc=ev_a)
                for i in range(8):
                    b.op("tensor", lambda e, i=i: e.matmul(
                        pgt[:, i:i + 1], rowbuf3v[0:1, 2 * D + 128 * i:2 * D + 128 * i + 128],
                        one1[0:1, 0:1], start=True, stop=True),
                        at(ev_a) if i == 0 else (), inc=ev_t)
                    b.op("tensor", lambda e, i=i: e.matmul(
                        pgt[:, 8 + i:9 + i], rowbuf3v[0:1, 3 * D + 128 * i:3 * D + 128 * i + 128],
                        one1[0:1, 0:1], start=True, stop=True), inc=ev_t)
                b.op("scalar", lambda e: e.copy(out=g2t[:], in_=pgt[:, 0:8]), at(ev_t), inc=ev_a)
                b.op("scalar", lambda e: e.copy(out=b2t[:], in_=pgt[:, 8:16]), inc=ev_a)
            p3pro = at(dma_g, ev_a)

            for ci in range(NCH):
                pp = ci % 2
                b.dma("gpsimd", s_xn1c[:, pp, :, :],
                      xn1_fm_h[:, :, CH * ci:CH * (ci + 1)].rearrange("f p t -> p f t"),
                      (p3pro if ci == 0 else []) + at(ev_t, ev_v, ev_a), inc=xl3[pp])
                b.dma("gpsimd", s_at[:, pp, :, :],
                      attn_tm_h[2 * ci:2 * ci + 2].rearrange("t p d -> p t d"), inc=xl3[pp])
                b.dma("gpsimd", s_xnt[:, pp, :, :],
                      xn_tm_h[2 * ci:2 * ci + 2].rearrange("t p d -> p t d"), inc=xl3[pp])
                xldw = at(xl3[pp]) + at(st3[0], st3[1])
                b.wait("vector", xldw)
                b.wait("scalar", xldw)
                for tt in range(2):
                    for half in range(2):
                        for dc in range(8):
                            b.op("tensor", lambda e, tt=tt, half=half, dc=dc, pp=pp: e.matmul(
                                pgt[:, 0:512], s_xn1c[:, pp, dc, 128 * tt:128 * tt + 128],
                                s_gatew[:, dc, 512 * half:512 * half + 512],
                                start=(dc == 0), stop=(dc == 7)),
                                (xldw + at(ev_v, ev_a)) if dc == 0 else (), inc=ev_t)
                        tv = ev_t.val
                        b.op("vector", lambda e, half=half: e.scalar_tensor_tensor(
                            out=s_tmp3v[:], in0=pgt[:, 0:512], scalar=0.0,
                            in1=gb_bc[:, 512 * half:512 * half + 512], op0=ALU.add, op1=ALU.add),
                            [(ev_t, tv)], inc=ev_v)
                        b.op("scalar", lambda e, tt=tt, half=half: e.activation(
                            out=s_gate[:, tt, 512 * half:512 * half + 512], in_=s_tmp3v[:],
                            func=AF.Sigmoid), at(ev_v), inc=ev_a)
                    b.op("vector", lambda e, tt=tt, pp=pp: e.tensor_tensor(
                        out=s_at[:, pp, tt, :], in0=s_at[:, pp, tt, :], in1=s_gate[:, tt, :],
                        op=ALU.mult), at(ev_a), inc=ev_v)
                    vv = ev_v.val
                    for ft in range(8):
                        sl = (8 * tt + ft) % 2
                        w = [(ev_v, vv)] + (at(ev_v, ev_a) if (8 * tt + ft) >= 2 else [])
                        b.op("tensor", lambda e, tt=tt, ft=ft, sl=sl, pp=pp: e.transpose(
                            ptr3[sl], s_at[:, pp, tt, 128 * ft:128 * ft + 128], s_id[:]),
                            w, inc=ev_t)
                        tvw = ev_t.val
                        if ft % 2 == 0:
                            b.op("vector", lambda e, tt=tt, ft=ft, sl=sl: e.tensor_copy(
                                out=s_agfm[:, ft, 128 * tt:128 * tt + 128], in_=ptr3[sl]),
                                [(ev_t, tvw)], inc=ev_v)
                        else:
                            b.op("scalar", lambda e, tt=tt, ft=ft, sl=sl: e.copy(
                                out=s_agfm[:, ft, 128 * tt:128 * tt + 128], in_=ptr3[sl]),
                                [(ev_t, tvw)], inc=ev_a)
                trv = at(ev_v, ev_a)
                for tt in range(2):
                    for half in range(2):
                        for afc in range(8):
                            b.op("tensor", lambda e, tt=tt, half=half, afc=afc: e.matmul(
                                po[:, 0:512], s_agfm[:, afc, 128 * tt:128 * tt + 128],
                                s_outw[:, afc, 512 * half:512 * half + 512],
                                start=(afc == 0), stop=(afc == 7)),
                                (trv + at(ev_v)) if afc == 0 else (), inc=ev_t)
                        tv = ev_t.val
                        b.op("vector", lambda e, half=half: e.scalar_tensor_tensor(
                            out=s_tmp3v[:], in0=po[:, 0:512], scalar=0.0,
                            in1=ob_bc[:, 512 * half:512 * half + 512], op0=ALU.add, op1=ALU.add),
                            [(ev_t, tv)], inc=ev_v)
                        b.op("vector", lambda e, tt=tt, half=half, pp=pp: e.tensor_tensor(
                            out=s_x2[:, tt, 512 * half:512 * half + 512], in0=s_tmp3v[:],
                            in1=s_xnt[:, pp, tt, 512 * half:512 * half + 512], op=ALU.add), inc=ev_v)
                    ln_stats(s_x2[:, tt, :], ln1=False)
                    ln_apply(s_xh2[:, tt, :], s_x2[:, tt, :], 4, 5)
                    vv = ev_v.val
                    for ft in range(8):
                        sl = ft % 2
                        w = [(ev_v, vv)] + at(ev_v, ev_a)
                        b.op("tensor", lambda e, tt=tt, ft=ft, sl=sl: e.transpose(
                            ptr3[sl], s_xh2[:, tt, 128 * ft:128 * ft + 128], s_id[:]),
                            w, inc=ev_t)
                        tvw = ev_t.val
                        if ft % 2 == 0:
                            b.op("vector", lambda e, tt=tt, ft=ft, sl=sl: e.tensor_copy(
                                out=s_xh2fm[:, ft, 128 * tt:128 * tt + 128], in_=ptr3[sl]),
                                [(ev_t, tvw)], inc=ev_v)
                        else:
                            b.op("scalar", lambda e, tt=tt, ft=ft, sl=sl: e.copy(
                                out=s_xh2fm[:, ft, 128 * tt:128 * tt + 128], in_=ptr3[sl]),
                                [(ev_t, tvw)], inc=ev_a)
                for ft in range(8):
                    b.op("scalar", lambda e, ft=ft: e.activation(
                        out=s_xn2fm[:, ft, :], in_=s_xh2fm[:, ft, :], func=AF.Identity,
                        bias=b2t[:, ft:ft + 1], scale=g2t[:, ft:ft + 1]),
                        at(ev_v, ev_a) if ft == 0 else (), inc=ev_a)
                b.dma("gpsimd", x2_tm_h[2 * ci:2 * ci + 2].rearrange("t p d -> p t d"), s_x2[:],
                      at(ev_v), inc=st3[pp])
                b.dma("gpsimd", xn2_fm_h[:, :, CH * ci:CH * (ci + 1)].rearrange("f p t -> p f t"),
                      s_xn2fm[:], at(ev_a), inc=st3[pp])
                if "xh2_fm" in debug:
                    b.dma("gpsimd", xh2_fm_h[:, :, CH * ci:CH * (ci + 1)].rearrange("f p t -> p f t"),
                          s_xh2fm[:], at(dma_s, dma_a), inc=dma_g)

            b.barrier(DMAS, extra=ALLGP)

        with ExitStack() as p4:
            sb4 = lambda name, shape, dt: p4.enter_context(nc.sbuf_tensor(name, shape, dt))
            ps4 = lambda name: p4.enter_context(nc.psum_tensor(name, [128, 512], F32))
            s_fc1w = sb4("s_fc1w", [128, 8, FF], BF16)
            s_fc2w = sb4("s_fc2w", [128, 32, D], BF16)
            f1b_t = sb4("f1b_t", [128, 32], F32)
            f2b_bc = sb4("f2b_bc", [128, D], F32)
            s_xn2 = sb4("s_xn2b", [128, 2, 8, CH], BF16)
            s_x2t = sb4("s_x2t", [128, 2, 2, D], BF16)
            s_xf = sb4("s_xf", [128, 2, 2, D], F32)
            s_gt = sb4("s_gt", [128, 2, 2, D], BF16)
            s_gel = sb4("s_gel", [128, 3, CH], BF16)
            s_s = sb4("s_s", [128, 2, D], BF16)
            s_sg4v = sb4("s_sg4", [128, 1024], F32)
            s_out = sb4("s_out", [128, 2, D], F32)
            s_tmp4v = sb4("s_tmp4", [128, 512], F32)
            pf1a = ps4("pf1a")
            pf1b = ps4("pf1b")
            pf2 = [ps4(f"pf2_{i}") for i in range(4)]

            b.dma("gpsimd", s_fc1w[:], fc1w_h.rearrange("(c p) f -> p c f", p=128), inc=dma_g)
            b.dma("gpsimd", s_fc2w[:], fc2w_h.rearrange("(c p) f -> p c f", p=128), inc=dma_g)
            with ExitStack() as pro:
                rowbuf4v = pro.enter_context(nc.sbuf_tensor("rowbuf4", [1, FF + D], BF16))
                b.dma("gpsimd", rowbuf4v[0:1, 0:FF], fc1b_h.unsqueeze(0), inc=dma_g)
                b.dma("gpsimd", rowbuf4v[0:1, FF:FF + D], fc2b_h.unsqueeze(0), inc=dma_g)
                gv = dma_g.val
                for i in range(32):
                    b.op("tensor", lambda e, i=i: e.matmul(
                        pf1a[:, i:i + 1], rowbuf4v[0:1, 128 * i:128 * i + 128], one1[0:1, 0:1],
                        start=True, stop=True), [(dma_g, gv)] if i == 0 else (), inc=ev_t)
                b.op("scalar", lambda e: e.copy(out=f1b_t[:], in_=pf1a[:, 0:32]), at(ev_t), inc=ev_a)
                for half in range(2):
                    b.op("tensor", lambda e, half=half: e.matmul(
                        pf1a[:, 0:512], ones_row[0:1, :],
                        rowbuf4v[0:1, FF + 512 * half:FF + 512 * half + 512],
                        start=True, stop=True), at(ev_a), inc=ev_t)
                    b.op("scalar", lambda e, half=half: e.copy(
                        out=f2b_bc[:, 512 * half:512 * half + 512], in_=pf1a[:, 0:512]),
                        at(ev_t), inc=ev_a)
            p4pro = at(dma_g, ev_a)

            ev_v4 = b.sem("p4v2")
            ev_a4 = b.sem("p4a2")
            ld4 = [b.sem("p4l0"), b.sem("p4l1")]
            ld_ready = {}

            def p4_A(ci):
                pp = ci % 2
                w = (p4pro if ci == 0 else []) + at(ev_t, ev_v4)
                b.dma("gpsimd", s_xn2[:, pp, :, :],
                      xn2_fm_h[:, :, CH * ci:CH * (ci + 1)].rearrange("f p t -> p f t"),
                      w, inc=ld4[pp])
                b.dma("gpsimd", s_x2t[:, pp, :, :],
                      x2_tm_h[2 * ci:2 * ci + 2].rearrange("t p d -> p t d"), inc=ld4[pp])
                b.dma("gpsimd", s_xf[:, pp, :, :],
                      x_h.rearrange("t p d -> p t d")[:, 2 * ci:2 * ci + 2, :], inc=ld4[pp])
                b.dma("gpsimd", s_gt[:, pp, :, :],
                      g_tm_h[2 * ci:2 * ci + 2].rearrange("t p d -> p t d"), inc=ld4[pp])
                ld_ready[ci] = [(ld4[pp], ld4[pp].val)]

            def p4_B(ci):
                pp = ci % 2
                for hf in range(32):
                    pf1 = pf1a if hf % 2 == 0 else pf1b
                    gl = hf % 3
                    for dc in range(8):
                        w = ()
                        if dc == 0:
                            w = at(ev_a4) + (ld_ready[ci] if hf == 0 else [])
                        b.op("tensor", lambda e, hf=hf, dc=dc, pp=pp, pf1=pf1: e.matmul(
                            pf1[:, 0:CH], s_fc1w[:, dc, 128 * hf:128 * hf + 128], s_xn2[:, pp, dc, :],
                            start=(dc == 0), stop=(dc == 7)), w, inc=ev_t)
                    tv = ev_t.val
                    b.op("scalar", lambda e, hf=hf, gl=gl, pf1=pf1: e.activation(
                        out=s_gel[:, gl, :], in_=pf1[:, 0:CH], func=AF.Gelu,
                        bias=f1b_t[:, hf:hf + 1], scale=1.0),
                        [(ev_t, tv)], inc=ev_a4, self_ser=False)
                    av = ev_a4.val
                    for tt in range(2):
                        for half in range(2):
                            w = ()
                            if tt == 0 and half == 0:
                                w = [(ev_a4, av)] + (at(ev_v4) if hf == 0 else [])
                            b.op("tensor", lambda e, tt=tt, half=half, hf=hf, gl=gl: e.matmul(
                                pf2[2 * tt + half][:, 0:512],
                                s_gel[:, gl, 128 * tt:128 * tt + 128],
                                s_fc2w[:, hf, 512 * half:512 * half + 512],
                                start=(hf == 0), stop=(hf == 31)), w, inc=ev_t)
                tv = ev_t.val
                for tt in range(2):
                    for half in range(2):
                        w = [(ev_t, tv)] + at(ev_v4)
                        if tt == 0 and half == 0:
                            w += ld_ready[ci] + at(st4[0], st4[1])
                        b.op("vector", lambda e, tt=tt, half=half: e.scalar_tensor_tensor(
                            out=s_tmp4v[:], in0=pf2[2 * tt + half][:, 0:512], scalar=0.0,
                            in1=f2b_bc[:, 512 * half:512 * half + 512], op0=ALU.add, op1=ALU.add),
                            w, inc=ev_v4, self_ser=False)
                        b.op("vector", lambda e, tt=tt, half=half, pp=pp: e.tensor_tensor(
                            out=s_s[:, tt, 512 * half:512 * half + 512], in0=s_tmp4v[:],
                            in1=s_x2t[:, pp, tt, 512 * half:512 * half + 512], op=ALU.add),
                            at(ev_v4), inc=ev_v4, self_ser=False)
                    b.op("vector", lambda e, tt=tt, pp=pp: e.tensor_tensor(
                        out=s_sg4v[:], in0=s_s[:, tt, :], in1=s_gt[:, pp, tt, :], op=ALU.mult),
                        at(ev_v4), inc=ev_v4, self_ser=False)
                    b.op("vector", lambda e, tt=tt, pp=pp: e.tensor_tensor(
                        out=s_out[:, tt, :], in0=s_sg4v[:], in1=s_xf[:, pp, tt, :], op=ALU.add),
                        at(ev_v4), inc=ev_v4, self_ser=False)
                b.dma("gpsimd", out_h[2 * ci:2 * ci + 2].rearrange("t p d -> p t d"), s_out[:],
                      at(ev_v4), inc=st4[pp])

            p4_A(0)
            p4_A(1)
            for ci in range(NCH):
                p4_B(ci)
                if ci + 2 < NCH:
                    p4_A(ci + 2)

        b.wait("gpsimd", at(dma_g, ld4[0], ld4[1], *ALLGP))
        b.wait("sync", at(dma_g, dma_s))
        b.wait("scalar", at(dma_a))

        block = ctx.enter_context(nc.Block())
        b.emit(block)
    return nc


def kernel(**inputs):
    x = np.ascontiguousarray(inputs["L0_attn_x"], dtype=np.float32)
    pb = np.ascontiguousarray(inputs["L0_attn_pos_bias"], dtype=np.float32)
    consts = {"pb_win": host_pb_win(pb), "pb_sp": host_pb_sparse(pb),
              "ident": np.eye(128, dtype=np.float32)}
    base = {
        "qkv_w": inputs["L0_attn_qkv_w"], "qkv_b": inputs["L0_attn_qkv_b"],
        "gate_w": inputs["L0_attn_gate_w"], "gate_b": inputs["L0_attn_gate_b"],
        "out_w": inputs["L0_attn_out_w"], "out_b": inputs["L0_attn_out_b"],
        "ln1_g": inputs["L0_attn_ln1_g"], "ln1_b": inputs["L0_attn_ln1_b"],
        "ln2_g": inputs["L0_attn_ln2_g"], "ln2_b": inputs["L0_attn_ln2_b"],
        "fc1_w": inputs["L0_attn_fc1_w"], "fc1_b": inputs["L0_attn_fc1_b"],
        "fc2_w": inputs["L0_attn_fc2_w"], "fc2_b": inputs["L0_attn_fc2_b"],
        "w_gate": inputs["L0_ffn_w_gate"], "w_up": inputs["L0_ffn_w_up"],
        "w_down": inputs["L0_ffn_w_down"],
    }
    base = {k: np.ascontiguousarray(v) for k, v in base.items()}
    nc = build_nc(debug=DEBUG)
    in_maps = []
    for core in range(8):
        m = {"x": np.ascontiguousarray(x[core].reshape(TT, 128, D))}
        m.update(base)
        m.update(consts)
        in_maps.append(m)
    res = run_bass_kernel_spmd(nc, in_maps, core_ids=list(range(8)))
    out = np.stack([res.results[i]["out"].reshape(NTOK, D) for i in range(8)], axis=0)
    return out.astype(np.float32)


# version 07 (final)
# speedup vs baseline: 1.0994x; 1.0029x over previous
import numpy as np
import ml_dtypes
from contextlib import ExitStack

import concourse.bass as bass
import concourse.mybir as mybir
from concourse.bass_utils import run_bass_kernel_spmd

F32 = mybir.dt.float32
BF16 = mybir.dt.bfloat16
I32 = mybir.dt.int32
AF = mybir.ActivationFunctionType
ALU = mybir.AluOpType
AX = mybir.AxisListType

NTOK = 4096
D = 1024
H = 16
HD = 64
FF = 4096
HQ = 3072
TT = NTOK // 128
RING = 16
CH = 256
NCH = NTOK // CH
QS = 1.0 / 127.0

WINSET = set(range(0, 33)) | {48, 64, 96, 128, 192, 256}
SPARSE = [384, 512, 768, 1024, 1536]
SP_M = [d // 128 for d in SPARSE]
OFFSETS = tuple(sorted(set(range(0, 33)) | {48, 64, 96, 128, 192, 256, 384, 512, 768, 1024, 1536}))
CLAMP = 480.0

DEBUG = ()


def host_pb_win(pos_bias):
    idx_of = {d: i for i, d in enumerate(OFFSETS)}
    pb = np.asarray(pos_bias, dtype=np.float32)
    out = np.full((16, 128, 3 * 128), -1e9, dtype=np.float32)
    tq = np.arange(128)
    for wt in range(3):
        for j in range(128):
            dl = tq - j + 128 * wt
            for d in WINSET:
                m = dl == d
                if m.any():
                    out[:, j, wt * 128 + np.nonzero(m)[0]] = pb[idx_of[d], :][:, None]
    return out


def host_pb_sparse(pos_bias):
    idx_of = {d: i for i, d in enumerate(OFFSETS)}
    pb = np.asarray(pos_bias, dtype=np.float32)
    out = np.full((13, 128, 128), -1e9, dtype=np.float32)
    for cv in range(13):
        for bi, dlt in enumerate(SPARSE):
            lo = dlt - 128 * cv
            if lo >= 128:
                continue
            lo = max(0, lo)
            for h in range(16):
                out[cv, lo:, 8 * h + bi] = pb[idx_of[dlt], h]
    return out


class Sem:
    __slots__ = ("h", "val")


class Builder:
    def __init__(self, nc, ctx):
        self.nc = nc
        self.ctx = ctx
        self.streams = {e: [] for e in ("tensor", "vector", "scalar", "gpsimd", "sync")}
        self.self_sems = {}
        self.all_ev = []
        self.all_dma = []
        self._n = 0

    def sem(self, name=None):
        s = Sem()
        s.h = self.ctx.enter_context(self.nc.semaphore(name or f"s{self._n}"))
        s.val = 0
        self._n += 1
        return s

    def op(self, eng, fn, waits=(), inc=None, inc_n=1, self_ser=True):
        wl = [(s.h, v) for (s, v) in waits]
        ss = self.self_sems.get(eng) if self_ser else None
        if ss is not None:
            wl.append((ss.h, ss.val))
            if inc is None:
                inc = ss
                inc_n = 1
        if inc is not None:
            inc.val += inc_n
        self.streams[eng].append((wl, fn, None if inc is None else (inc.h, inc_n)))

    def wait(self, eng, waits):
        self.streams[eng].append(([(s.h, v) for (s, v) in waits], None, None))

    def dma(self, eng, out, in_, waits=(), inc=None, **kw):
        self.op(eng, lambda e: e.dma_start(out=out, in_=in_, **kw), waits, inc, 16, self_ser=False)

    def trans(self, eng, out, in_, waits=(), inc=None):
        waits = list(waits) + [(s, s.val) for s in self.all_ev] + [(s, s.val) for s in self.all_dma]
        self.op(eng, lambda e: e.dma_start_transpose(out=out, in_=in_), waits, inc, 16, self_ser=False)

    def barrier(self, dma_sems, extra=()):
        for eng, s in dma_sems.items():
            self.wait(eng, [(s, s.val)])
        if extra:
            self.wait("gpsimd", [(s, s.val) for s in extra])
        bsem = self.sem()
        bsem.val = 5
        for e in self.streams:
            self.streams[e].append(([], lambda eng: eng.nop(), (bsem.h, 1)))
            self.streams[e].append(([(bsem.h, 5)], None, None))

    def emit(self, block):
        def make(eng_name):
            def body(e):
                for wl, fn, inc in self.streams[eng_name]:
                    for (h, v) in wl:
                        e.wait_ge(h, v)
                    if fn is None:
                        continue
                    r = fn(e)
                    if inc is not None:
                        r.then_inc(inc[0], inc[1])
            return body
        block.tensor(make("tensor"))
        block.vector(make("vector"))
        block.scalar(make("scalar"))
        block.gpsimd(make("gpsimd"))
        block.sync(make("sync"))


def at(*sems):
    return [(s, s.val) for s in sems]


def build_nc(debug=()):
    nc = bass.Bass("TRN2", target_bir_lowering=False)

    def din(name, shape, dt=F32):
        return nc.dram_tensor(name, shape, dt, kind="ExternalInput").ap()

    x_h = din("x", [TT, 128, D])
    qkvw_h = din("qkv_w", [D, 3 * D])
    qkvb_h = din("qkv_b", [3 * D])
    gatew_h = din("gate_w", [D, D])
    gateb_h = din("gate_b", [D])
    outw_h = din("out_w", [D, D])
    outb_h = din("out_b", [D])
    ln1g_h = din("ln1_g", [D])
    ln1b_h = din("ln1_b", [D])
    ln2g_h = din("ln2_g", [D])
    ln2b_h = din("ln2_b", [D])
    fc1w_h = din("fc1_w", [D, FF])
    fc1b_h = din("fc1_b", [FF])
    fc2w_h = din("fc2_w", [FF, D])
    fc2b_h = din("fc2_b", [D])
    wg_h = din("w_gate", [HQ, D], I32)
    wu_h = din("w_up", [HQ, D], I32)
    wd_h = din("w_down", [D, HQ], I32)
    pbw_h = din("pb_win", [16, 128, 384])
    ident_h = din("ident", [128, 128])
    pbs_h = din("pb_sp", [13, 128, 128])

    out_h = nc.dram_tensor("out", [TT, 128, D], F32, kind="ExternalOutput").ap()

    def scratch(name, shape, dt=BF16):
        if name in debug:
            return nc.dram_tensor(name, shape, dt, kind="ExternalOutput").ap()
        return nc.dram_tensor(name, shape, dt).ap()

    xn_tm_h = scratch("xn_tm", [TT, 128, D])
    xn1_fm_h = scratch("xn1_fm", [8, 128, NTOK])
    g_tm_h = scratch("g_tm", [TT, 128, D])
    attn_tm_h = scratch("attn_tm", [TT, 128, D])
    x2_tm_h = scratch("x2_tm", [TT, 128, D])
    xn2_fm_h = scratch("xn2_fm", [8, 128, NTOK])
    xh2_fm_h = scratch("xh2_fm", [8, 128, NTOK])

    ctx = ExitStack()
    with ctx:
        b = Builder(nc, ctx)
        ones_row = ctx.enter_context(nc.sbuf_tensor("ones_row", [1, 128], BF16))
        one1 = ctx.enter_context(nc.sbuf_tensor("one1", [1, 512], BF16))
        s_stats = ctx.enter_context(nc.sbuf_tensor("s_stats", [128, 16], F32))
        s_junk = ctx.enter_context(nc.sbuf_tensor("s_junk", [128, 1024], BF16))
        s_eps = ctx.enter_context(nc.sbuf_tensor("s_eps", [128, 1], F32))
        s_id = ctx.enter_context(nc.sbuf_tensor("s_id", [128, 128], BF16))

        dma_g = b.sem("dma_g")
        dma_s = b.sem("dma_s")
        dma_a = b.sem("dma_a")
        ev_t = b.sem("ev_t")
        ev_v = b.sem("ev_v")
        ev_a = b.sem("ev_a")
        ev_p = b.sem("ev_p")
        DMAS = {"gpsimd": dma_g, "sync": dma_s, "scalar": dma_a}
        xl1 = [b.sem("xl1a"), b.sem("xl1b")]
        st1x = [b.sem("st1xa"), b.sem("st1xb")]
        st1n = [b.sem("st1na"), b.sem("st1nb")]
        st1g = [b.sem("st1ga"), b.sem("st1gb")]
        xl2 = [b.sem("xl2a"), b.sem("xl2b")]
        st2 = [b.sem("st2a"), b.sem("st2b")]
        xl3 = [b.sem("xl3a"), b.sem("xl3b")]
        st3 = [b.sem("st3a"), b.sem("st3b")]
        st4 = [b.sem("st4a"), b.sem("st4b")]
        ALLGP = [xl1[0], xl1[1], st1x[0], st1x[1], st1n[0], st1n[1], st1g[0], st1g[1],
                 xl2[0], xl2[1], st2[0], st2[1],
                 xl3[0], xl3[1], st3[0], st3[1], st4[0], st4[1]]
        b.self_sems = {"vector": ev_v, "scalar": ev_a, "gpsimd": ev_p}
        b.all_ev = [ev_t, ev_v, ev_a]
        b.all_dma = [dma_g]

        b.op("gpsimd", lambda e: e.memset(ones_row[:], 1.0), inc=ev_p)
        b.op("gpsimd", lambda e: e.memset(s_eps[:], 1e-5), inc=ev_p)
        b.dma("gpsimd", s_id[:], ident_h, inc=dma_g)
        b.op("gpsimd", lambda e: e.memset(one1[:], 1.0), inc=ev_p)

        def ln_stats(src_ap, ln1=False):
            st = s_stats
            b.op("vector", lambda e: e.reduce_sum(out=st[:, 0:1], in_=src_ap, axis=AX.X), at(ev_a), inc=ev_v)
            b.op("scalar", lambda e: e.activation(
                out=s_junk[:, 0:src_ap.shape[-1]], in_=src_ap, func=AF.Square,
                accum_out=st[:, 1:2]), at(ev_v), inc=ev_a)
            b.op("vector", lambda e: e.tensor_scalar_mul(out=st[:, 0:1], in0=st[:, 0:1], scalar1=1.0 / D), inc=ev_v)
            b.op("vector", lambda e: e.tensor_scalar_mul(out=st[:, 1:2], in0=st[:, 1:2], scalar1=1.0 / D), at(ev_a), inc=ev_v)
            b.op("vector", lambda e: e.tensor_tensor(out=st[:, 2:3], in0=st[:, 0:1], in1=st[:, 0:1], op=ALU.mult), inc=ev_v)
            b.op("vector", lambda e: e.tensor_tensor(out=st[:, 2:3], in0=st[:, 1:2], in1=st[:, 2:3], op=ALU.subtract), inc=ev_v)
            b.op("scalar", lambda e: e.activation(out=st[:, 3:4], in_=st[:, 2:3], func=AF.Sqrt, bias=s_eps[:], scale=1.0),
                 at(ev_v, ev_p), inc=ev_a)
            b.op("vector", lambda e: e.reciprocal(out=st[:, 4:5], in_=st[:, 3:4]), at(ev_a), inc=ev_v)
            b.op("vector", lambda e: e.tensor_tensor(out=st[:, 5:6], in0=st[:, 0:1], in1=st[:, 4:5], op=ALU.mult), inc=ev_v)
            if ln1:
                b.op("vector", lambda e: e.tensor_tensor(out=st[:, 6:7], in0=st[:, 4:5], in1=st[:, 4:5], op=ALU.mult), inc=ev_v)
                b.op("vector", lambda e: e.tensor_tensor(out=st[:, 6:7], in0=st[:, 2:3], in1=st[:, 6:7], op=ALU.mult), inc=ev_v)
                b.op("scalar", lambda e: e.activation(out=st[:, 7:8], in_=st[:, 6:7], func=AF.Sqrt, bias=s_eps[:], scale=1.0),
                     at(ev_v, ev_p), inc=ev_a)
                b.op("vector", lambda e: e.reciprocal(out=st[:, 8:9], in_=st[:, 7:8]), at(ev_a), inc=ev_v)
                b.op("vector", lambda e: e.tensor_tensor(out=st[:, 9:10], in0=st[:, 4:5], in1=st[:, 8:9], op=ALU.mult), inc=ev_v)
                b.op("vector", lambda e: e.tensor_tensor(out=st[:, 10:11], in0=st[:, 0:1], in1=st[:, 9:10], op=ALU.mult), inc=ev_v)

        def ln_apply(dst_ap, src_ap, rstd_col, mur_col):
            st = s_stats
            mur_view = st[:, mur_col:mur_col + 1].broadcast_to([128, dst_ap.shape[-1]])
            b.op("vector", lambda e: e.scalar_tensor_tensor(
                out=dst_ap, in0=src_ap, scalar=st[:, rstd_col:rstd_col + 1], in1=mur_view,
                op0=ALU.mult, op1=ALU.subtract), inc=ev_v)

        with ExitStack() as p1:
            sb1 = lambda name, shape, dt: p1.enter_context(nc.sbuf_tensor(name, shape, dt))
            ps1 = lambda name: p1.enter_context(nc.psum_tensor(name, [128, 512], F32))
            wgT = sb1("wgT", [128, 8, HQ], BF16)
            wuT = sb1("wuT", [128, 8, HQ], BF16)
            wdT = sb1("wdT", [128, 24, D], BF16)
            g1t = sb1("g1t", [128, 8], F32)
            b1t = sb1("b1t", [128, 8], F32)
            pg = ps1("pg")
            pu = ps1("pu")
            pd = [ps1(f"pd{i}") for i in range(4)]

            with ExitStack() as pro:
                stg = pro.enter_context(nc.sbuf_tensor("stg", [128, 24 * D], BF16))
                rowbuf1 = pro.enter_context(nc.sbuf_tensor("rowbuf1", [1, 2 * D], BF16))
                b.dma("gpsimd", rowbuf1[0:1, 0:D], ln1g_h.unsqueeze(0), at(ev_p), inc=dma_g)
                b.dma("gpsimd", rowbuf1[0:1, D:2 * D], ln1b_h.unsqueeze(0), inc=dma_g)
                gv = dma_g.val
                for i in range(8):
                    b.op("tensor", lambda e, i=i: e.matmul(
                        pg[:, i:i + 1], rowbuf1[0:1, 128 * i:128 * i + 128], one1[0:1, 0:1],
                        start=True, stop=True), [(dma_g, gv)] if i == 0 else (), inc=ev_t)
                    b.op("tensor", lambda e, i=i: e.matmul(
                        pg[:, 8 + i:9 + i], rowbuf1[0:1, D + 128 * i:D + 128 * i + 128], one1[0:1, 0:1],
                        start=True, stop=True), inc=ev_t)
                b.op("scalar", lambda e: e.copy(out=g1t[:], in_=pg[:, 0:8]), at(ev_t), inc=ev_a)
                b.op("scalar", lambda e: e.copy(out=b1t[:], in_=pg[:, 8:16]), inc=ev_a)

                ptw_t = [pro.enter_context(nc.psum_tensor(f"ptw{i}", [128, 1024], BF16)) for i in range(2)]
                ptw = [p[:, 0:128] for p in ptw_t]
                widt = dma_g.val
                for (w_h, dstT, src_tiles, dst_chunks) in (
                        (wg_h, wgT, 24, 8), (wu_h, wuT, 24, 8), (wd_h, wdT, 8, 24)):
                    inner = 128 * dst_chunks
                    stg_v = stg[:].rearrange("p (c d) -> p c d", d=inner)
                    b.dma("gpsimd", stg_v[:, 0:src_tiles, :],
                          w_h.rearrange("(c p) d -> p c d", p=128), at(ev_v, ev_a), inc=dma_g)
                    gv = dma_g.val
                    nslot = 2
                    k = 0
                    for sc in range(src_tiles):
                        for dc in range(dst_chunks):
                            sl = k % nslot
                            w = [(dma_g, max(gv, widt))]
                            if k >= nslot:
                                w += at(ev_v, ev_a)
                            b.op("tensor", lambda e, sc=sc, dc=dc, sl=sl, stg_v=stg_v: e.transpose(
                                ptw[sl], stg_v[:, sc, 128 * dc:128 * dc + 128], s_id[:]),
                                w, inc=ev_t)
                            tvw = ev_t.val
                            eng = "vector" if k % 2 == 0 else "scalar"
                            if eng == "vector":
                                b.op("vector", lambda e, sc=sc, dc=dc, sl=sl, dstT=dstT: e.tensor_copy(
                                    out=dstT[:, dc, 128 * sc:128 * sc + 128], in_=ptw[sl]),
                                    [(ev_t, tvw)], inc=ev_v)
                            else:
                                b.op("scalar", lambda e, sc=sc, dc=dc, sl=sl, dstT=dstT: e.copy(
                                    out=dstT[:, dc, 128 * sc:128 * sc + 128], in_=ptw[sl]),
                                    [(ev_t, tvw)], inc=ev_a)
                            k += 1
            tr_done = at(ev_v, ev_a)

            s_x = sb1("s_x", [128, 2, 2, D], F32)
            s_xn = sb1("s_xn", [128, 2, 2, D], BF16)
            s_xh1 = sb1("s_xh1", [128, 2, 2, D], BF16)
            s_xnfm = sb1("s_xnfm", [128, 2, 8, CH], BF16)
            s_xh1fm = sb1("s_xh1fm", [128, 2, 8, CH], BF16)
            s_xn1fm = sb1("s_xn1fm", [128, 8, CH], BF16)
            s_sg1 = sb1("s_sg1", [128, 2, CH], BF16)
            s_su = sb1("s_su", [128, 2, CH], BF16)
            s_hp = sb1("s_hp", [128, 2, CH], BF16)
            s_g = sb1("s_g", [128, 2, D], BF16)

            ev_v2 = b.sem("p1v2")
            ev_a2 = b.sem("p1a2")
            xn_ready = {}
            xload = {}

            ptr1_t = [p1.enter_context(nc.psum_tensor(f"ptr1{i}", [128, 1024], BF16)) for i in range(2)]
            ptr1 = [p[:, 0:128] for p in ptr1_t]
            lnv = {}

            def p1_A(ci):
                pp = ci % 2
                if ci == 0:
                    b.dma("gpsimd", s_x[:, 0, :, :],
                          x_h.rearrange("t p d -> p t d")[:, 0:2, :],
                          tr_done + at(ev_t), inc=xl1[0])
                if ci + 1 < NCH:
                    b.dma("gpsimd", s_x[:, (ci + 1) % 2, :, :],
                          x_h.rearrange("t p d -> p t d")[:, 2 * (ci + 1):2 * (ci + 1) + 2, :],
                          at(ev_v, ev_a), inc=xl1[(ci + 1) % 2])
                b.wait("vector", at(xl1[pp], st1x[0], st1x[1]))
                b.wait("scalar", at(xl1[pp]))
                for tt in range(2):
                    src_ap = s_x[:, pp, tt, :]
                    ln_stats(src_ap, ln1=True)
                    ln_apply(s_xn[:, pp, tt, :], src_ap, 4, 5)
                    ln_apply(s_xh1[:, pp, tt, :], src_ap, 9, 10)
                lnv[ci] = ev_v.val

            def p1_A_trans(ci):
                pp = ci % 2
                k = 0
                for tt in range(2):
                    for (srcT, dstT2) in ((s_xn, s_xnfm), (s_xh1, s_xh1fm)):
                        for ft in range(8):
                            sl = k % 2
                            w = [(ev_v, lnv[ci])]
                            if k >= 2:
                                w += at(ev_v, ev_a)
                            b.op("tensor", lambda e, tt=tt, ft=ft, sl=sl, srcT=srcT, pp=pp: e.transpose(
                                ptr1[sl], srcT[:, pp, tt, 128 * ft:128 * ft + 128], s_id[:]),
                                w, inc=ev_t)
                            tvw = ev_t.val
                            if k % 2 == 0:
                                b.op("vector", lambda e, tt=tt, ft=ft, sl=sl, dstT2=dstT2, pp=pp: e.tensor_copy(
                                    out=dstT2[:, pp, ft, 128 * tt:128 * tt + 128], in_=ptr1[sl]),
                                    [(ev_t, tvw)], inc=ev_v)
                            else:
                                b.op("scalar", lambda e, tt=tt, ft=ft, sl=sl, dstT2=dstT2, pp=pp: e.copy(
                                    out=dstT2[:, pp, ft, 128 * tt:128 * tt + 128], in_=ptr1[sl]),
                                    [(ev_t, tvw)], inc=ev_a)
                            k += 1
                xn_ready[ci] = at(ev_v, ev_a)

            def p1_A_late(ci):
                pp = ci % 2
                for ft in range(8):
                    b.op("scalar", lambda e, ft=ft, pp=pp: e.activation(
                        out=s_xn1fm[:, ft, :], in_=s_xh1fm[:, pp, ft, :], func=AF.Identity,
                        bias=b1t[:, ft:ft + 1], scale=g1t[:, ft:ft + 1]),
                        (xn_ready[ci] + at(st1n[0], st1n[1])) if ft == 0 else (), inc=ev_a)
                b.dma("gpsimd", xn_tm_h[2 * ci:2 * ci + 2].rearrange("t p d -> p t d"),
                      s_xn[:, pp, :, :], xn_ready[ci] + at(ev_v), inc=st1x[pp])
                b.dma("gpsimd", xn1_fm_h[:, :, CH * ci:CH * (ci + 1)].rearrange("f p t -> p f t"),
                      s_xn1fm[:], at(ev_a), inc=st1n[pp])

            def p1_B(ci):
                pp = ci % 2
                for hfc in range(24):
                    hp_i = hfc % 2
                    for dc in range(8):
                        w = ()
                        if dc == 0:
                            w = at(ev_a2, ev_v2) + (xn_ready[ci] if hfc == 0 else [])
                            if hfc == 0 and ci == 0:
                                w = w + tr_done
                        b.op("tensor", lambda e, dc=dc, hfc=hfc, pp=pp: e.matmul(
                            pg[:, 0:CH], wgT[:, dc, 128 * hfc:128 * hfc + 128], s_xnfm[:, pp, dc, :],
                            start=(dc == 0), stop=(dc == 7)), w, inc=ev_t)
                    for dc in range(8):
                        b.op("tensor", lambda e, dc=dc, hfc=hfc, pp=pp: e.matmul(
                            pu[:, 0:CH], wuT[:, dc, 128 * hfc:128 * hfc + 128], s_xnfm[:, pp, dc, :],
                            start=(dc == 0), stop=(dc == 7)), inc=ev_t)
                    tv = ev_t.val
                    b.op("scalar", lambda e, hp_i=hp_i: e.activation(
                        out=s_sg1[:, hp_i, :], in_=pg[:, 0:CH], func=AF.Silu, scale=QS),
                        [(ev_t, tv)] + at(ev_v2), inc=ev_a2, self_ser=False)
                    b.op("vector", lambda e, hp_i=hp_i: e.tensor_copy(
                        out=s_su[:, hp_i, :], in_=pu[:, 0:CH]),
                        [(ev_t, tv)] + at(ev_v2), inc=ev_v2, self_ser=False)
                    b.op("vector", lambda e, hp_i=hp_i: e.tensor_tensor(
                        out=s_hp[:, hp_i, :], in0=s_sg1[:, hp_i, :], in1=s_su[:, hp_i, :],
                        op=ALU.mult), at(ev_a2, ev_v2), inc=ev_v2, self_ser=False)
                    hv = ev_v2.val
                    for tt in range(2):
                        for half in range(2):
                            w = ()
                            if tt == 0 and half == 0:
                                w = [(ev_v2, hv)] + (at(ev_a2) if hfc == 0 else [])
                            b.op("tensor", lambda e, tt=tt, half=half, hfc=hfc, hp_i=hp_i: e.matmul(
                                pd[2 * tt + half][:, 0:512],
                                s_hp[:, hp_i, 128 * tt:128 * tt + 128],
                                wdT[:, hfc, 512 * half:512 * half + 512],
                                start=(hfc == 0), stop=(hfc == 23)), w, inc=ev_t)
                tv = ev_t.val
                for tt in range(2):
                    for half in range(2):
                        b.op("scalar", lambda e, tt=tt, half=half: e.activation(
                            out=s_g[:, tt, 512 * half:512 * half + 512],
                            in_=pd[2 * tt + half][:, 0:512], func=AF.Silu, scale=QS * QS),
                            [(ev_t, tv)] + at(st1g[0], st1g[1], ev_v2), inc=ev_a2, self_ser=False)
                b.dma("gpsimd", g_tm_h[2 * ci:2 * ci + 2].rearrange("t p d -> p t d"), s_g[:],
                      at(ev_a2), inc=st1g[pp])

            p1_A(0)
            p1_A_trans(0)
            for ci in range(NCH):
                if ci + 1 < NCH:
                    p1_A(ci + 1)
                p1_B(ci)
                if ci + 1 < NCH:
                    p1_A_trans(ci + 1)
                p1_A_late(ci)

            b.barrier(DMAS, extra=ALLGP)

        with ExitStack() as p2:
            sb2 = lambda name, shape, dt: p2.enter_context(nc.sbuf_tensor(name, shape, dt))
            ps2 = lambda name: p2.enter_context(nc.psum_tensor(name, [128, 512], F32))
            s_qkvw = sb2("s_qkvw", [128, 8, 3 * D], BF16)
            qkb_t = sb2("qkb_t", [128, 16], F32)
            vb_bc = sb2("vb_bc", [128, D], F32)
            kfm = sb2("kfm", [128, 8, 8, 128], BF16)
            ktm = sb2("ktm", [128, RING, D], BF16)
            vtm = sb2("vtm", [128, RING, H, 65], BF16)
            pb8w = sb2("pb8w", [128, 16, 384], BF16)
            pb8e = sb2("pb8e", [128, 128], BF16)
            pb8s = sb2("pb8s", [128, 13, 128], BF16)
            s_xn1 = sb2("s_xn1", [128, 2, 8, CH], BF16)
            s_qfm = sb2("s_qfm", [128, 2, 8, CH], BF16)
            s_qtm = sb2("s_qtm", [128, 2, 2, D], BF16)
            s_pt = sb2("s_pt", [128, 4, 384], BF16)
            s_acc = sb2("s_acc", [128, H, HD], F32)
            s_den = sb2("s_den", [128, 16], F32)
            s_sprod = sb2("s_sprod", [128, D], BF16)
            s_ssc = sb2("s_ssc", [128, 128], F32)
            s_psp = sb2("s_psp", [128, 128], BF16)
            s_attn = sb2("s_attn", [128, 2, D], BF16)
            psA = [ps2("psA0"), ps2("psA1")]
            psB = [ps2("psB0"), ps2("psB1")]
            pav = ps2("pav")
            pq = ps2("pq")
            pv0 = ps2("pv0")
            pv1 = ps2("pv1")

            b.dma("gpsimd", s_qkvw[:], qkvw_h.rearrange("(c p) f -> p c f", p=128), inc=dma_g)
            with ExitStack() as pro:
                rowbuf2v = pro.enter_context(nc.sbuf_tensor("rowbuf2", [1, 3 * D], BF16))
                b.dma("gpsimd", rowbuf2v[0:1, :], qkvb_h.unsqueeze(0), inc=dma_g)
                gv = dma_g.val
                for i in range(16):
                    b.op("tensor", lambda e, i=i: e.matmul(
                        pq[:, i:i + 1], rowbuf2v[0:1, 128 * i:128 * i + 128], one1[0:1, 0:1],
                        start=True, stop=True), [(dma_g, gv)] if i == 0 else (), inc=ev_t)
                b.op("scalar", lambda e: e.copy(out=qkb_t[:], in_=pq[:, 0:16]), at(ev_t), inc=ev_a)
                for half in range(2):
                    b.op("tensor", lambda e, half=half: e.matmul(
                        pv0[:, 0:512], ones_row[0:1, :],
                        rowbuf2v[0:1, 2 * D + 512 * half:2 * D + 512 * half + 512],
                        start=True, stop=True), at(ev_a), inc=ev_t)
                    b.op("scalar", lambda e, half=half: e.copy(
                        out=vb_bc[:, 512 * half:512 * half + 512], in_=pv0[:, 0:512]),
                        at(ev_t), inc=ev_a)
            b.dma("gpsimd", pb8w[:], pbw_h.rearrange("h p w -> p h w"), inc=dma_g)
            b.dma("gpsimd", pb8s[:], pbs_h.rearrange("c p w -> p c w"), inc=dma_g)
            gv = dma_g.val
            b.op("vector", lambda e: e.tensor_scalar_mul(out=pb8w[:], in0=pb8w[:], scalar1=8.0),
                 [(dma_g, gv)], inc=ev_v)
            b.op("vector", lambda e: e.tensor_scalar_mul(out=pb8s[:], in0=pb8s[:], scalar1=8.0), inc=ev_v)
            b.op("gpsimd", lambda e: e.memset(pb8e[:], -8e9), inc=ev_p)
            b.op("gpsimd", lambda e: e.memset(s_ssc[:], 0.0), inc=ev_p)
            b.op("gpsimd", lambda e: e.memset(
                vtm[:].rearrange("p r h w -> p (r h) w")[:, :, 64:65], 1.0), inc=ev_p)
            for par in range(2):
                b.op("vector", lambda e, par=par: e.memset(psA[par][:, 0:384], 0.0), inc=ev_v)
                b.op("vector", lambda e, par=par: e.memset(psB[par][:, 0:384], 0.0), inc=ev_v)
            p2pro = at(dma_g, ev_v, ev_a, ev_p)

            ev_v3 = b.sem("p2v3")
            ev_a3 = b.sem("p2a3")
            dma_sp = [b.sem("p2s0"), b.sem("p2s1")]
            ready2 = {}

            def p2_A(ci):
                pp = ci % 2
                b.dma("gpsimd", s_xn1[:, pp, :, :],
                      xn1_fm_h[:, :, CH * ci:CH * (ci + 1)].rearrange("f p t -> p f t"),
                      (p2pro if ci == 0 else []) + at(ev_t, ev_v, ev_a), inc=xl2[pp])
                xldw = at(xl2[pp])
                b.wait("vector", xldw)
                b.wait("scalar", xldw)
                for qf in range(8):
                    pqx = pq if qf % 2 == 0 else pv0
                    for dc in range(8):
                        b.op("tensor", lambda e, qf=qf, dc=dc, pp=pp, pqx=pqx: e.matmul(
                            pqx[:, 0:CH], s_qkvw[:, dc, 128 * qf:128 * qf + 128], s_xn1[:, pp, dc, :],
                            start=(dc == 0), stop=(dc == 7)),
                            (xldw + at(ev_a)) if dc == 0 else (), inc=ev_t)
                    tv = ev_t.val
                    b.op("scalar", lambda e, qf=qf, pp=pp, pqx=pqx: e.activation(
                        out=s_qfm[:, pp, qf, :], in_=pqx[:, 0:CH], func=AF.Identity,
                        bias=qkb_t[:, qf:qf + 1], scale=1.0), [(ev_t, tv)], inc=ev_a)
                slot0 = (2 * ci) % 8
                for kf in range(8):
                    pqx = pq if kf % 2 == 0 else pv0
                    for dc in range(8):
                        b.op("tensor", lambda e, kf=kf, dc=dc, pp=pp, pqx=pqx: e.matmul(
                            pqx[:, 0:CH], s_qkvw[:, dc, D + 128 * kf:D + 128 * kf + 128],
                            s_xn1[:, pp, dc, :],
                            start=(dc == 0), stop=(dc == 7)), at(ev_a) if dc == 0 else (), inc=ev_t)
                    tv = ev_t.val
                    b.op("scalar", lambda e, kf=kf, slot0=slot0, pqx=pqx: e.activation(
                        out=kfm[:, slot0:slot0 + 2, kf, :],
                        in_=pqx[:, 0:CH].rearrange("p (s t) -> p s t", s=2), func=AF.Identity,
                        bias=qkb_t[:, 8 + kf:9 + kf], scale=1.0), [(ev_t, tv)], inc=ev_a)
                for tt in range(2):
                    pvp = pv1
                    slot = (2 * ci + tt) % RING
                    for half in range(2):
                        for dc in range(8):
                            b.op("tensor", lambda e, tt=tt, half=half, dc=dc, pp=pp, pvp=pvp: e.matmul(
                                pvp[:, 0:512],
                                s_xn1[:, pp, dc, 128 * tt:128 * tt + 128],
                                s_qkvw[:, dc, 2 * D + 512 * half:2 * D + 512 * half + 512],
                                start=(dc == 0), stop=(dc == 7)),
                                at(ev_v) if dc == 0 else (), inc=ev_t)
                        tv = ev_t.val
                        b.op("vector", lambda e, slot=slot, half=half, pvp=pvp: e.scalar_tensor_tensor(
                            out=vtm[:, slot, 8 * half:8 * half + 8, 0:64],
                            in0=pvp[:, 0:512].rearrange("p (h w) -> p h w", w=64),
                            scalar=1.0,
                            in1=vb_bc[:, 512 * half:512 * half + 512].rearrange("p (h w) -> p h w", w=64),
                            op0=ALU.mult, op1=ALU.add), [(ev_t, tv)] + at(ev_p), inc=ev_v)
                av = ev_a.val
                for tt in range(2):
                    slot = (2 * ci + tt) % RING
                    for ft in range(8):
                        b.trans("sync", s_qtm[:, pp, tt, 128 * ft:128 * ft + 128],
                                s_qfm[:, pp, ft, 128 * tt:128 * tt + 128],
                                [(ev_a, av)], inc=dma_sp[pp])
                        b.trans("sync", ktm[:, slot, 128 * ft:128 * ft + 128],
                                kfm[:, slot % 8, ft, :],
                                [(ev_a, av)], inc=dma_sp[pp])
                ready2[ci] = at(dma_sp[pp], ev_v, ev_a)

            def p2_B(ci):
                pp = ci % 2
                for tt in range(2):
                    c = 2 * ci + tt
                    for bi, m in enumerate(SP_M):
                        if m > c:
                            continue
                        slot_s = (c - m) % RING
                        b.op("vector", lambda e, tt=tt, slot_s=slot_s, pp=pp: e.tensor_tensor(
                            out=s_sprod[:], in0=s_qtm[:, pp, tt, :], in1=ktm[:, slot_s, :],
                            op=ALU.mult), ready2[ci] + at(ev_p, ev_v3), inc=ev_v3, self_ser=False)
                        b.op("vector", lambda e, bi=bi: e.tensor_reduce(
                            out=s_ssc[:].rearrange("p (h w) -> p h w", w=8)[:, :, bi:bi + 1],
                            in_=s_sprod[:].rearrange("p (h w) -> p h w", w=64),
                            axis=AX.X, op=ALU.add), at(ev_v3), inc=ev_v3, self_ser=False)
                    cv = min(c, 12)
                    b.op("vector", lambda e, cv=cv: e.scalar_tensor_tensor(
                        out=s_ssc[:], in0=s_ssc[:], scalar=CLAMP, in1=pb8s[:, cv, :],
                        op0=ALU.min, op1=ALU.add), at(ev_v3), inc=ev_v3, self_ser=False)
                    vv = ev_v3.val
                    b.op("scalar", lambda e: e.activation(
                        out=s_psp[:], in_=s_ssc[:], func=AF.Exp, scale=0.125),
                        [(ev_v3, vv)], inc=ev_a3, self_ser=False)
                    spv = ev_a3.val

                    qk_done = {}

                    def qk(hp):
                        par = hp % 2
                        for wt in range(3):
                            if c - wt < 0:
                                continue
                            slot_w = (c - wt) % 8
                            wA = ()
                            if wt == 0:
                                wA = at(ev_v3, ev_a3)
                                if hp == 0:
                                    wA = wA + ready2[ci]
                            b.op("tensor", lambda e, hp=hp, wt=wt, slot_w=slot_w, tt=tt, par=par, pp=pp: e.matmul(
                                psA[par][:, 128 * wt:128 * wt + 128],
                                kfm[0:64, slot_w, hp, :], s_qfm[0:64, pp, hp, 128 * tt:128 * tt + 128],
                                start=True, stop=True, tile_position=(0, 0)), wA, inc=ev_t)
                            b.op("tensor", lambda e, hp=hp, wt=wt, slot_w=slot_w, tt=tt, par=par, pp=pp: e.matmul(
                                psB[par][:, 128 * wt:128 * wt + 128],
                                kfm[64:128, slot_w, hp, :], s_qfm[64:128, pp, hp, 128 * tt:128 * tt + 128],
                                start=True, stop=True, tile_position=(64, 0)), inc=ev_t)
                        qk_done[hp] = ev_t.val

                    qk(0)
                    for hp in range(8):
                        if hp + 1 < 8:
                            qk(hp + 1)
                        par = hp % 2
                        tv = qk_done[hp]
                        for (pst, h) in ((psA[par], 2 * hp), (psB[par], 2 * hp + 1)):
                            if c >= 2:
                                b.op("vector", lambda e, pst=pst, h=h: e.scalar_tensor_tensor(
                                    out=pst[:, 0:384], in0=pst[:, 0:384], scalar=CLAMP,
                                    in1=pb8w[:, h, :], op0=ALU.min, op1=ALU.add),
                                    [(ev_t, tv)] + at(ev_a3), inc=ev_v3, self_ser=False)
                            else:
                                for wt in range(3):
                                    srcw = pb8w[:, h, 128 * wt:128 * wt + 128] if wt <= c else pb8e[:]
                                    b.op("vector", lambda e, pst=pst, wt=wt, srcw=srcw: e.scalar_tensor_tensor(
                                        out=pst[:, 128 * wt:128 * wt + 128],
                                        in0=pst[:, 128 * wt:128 * wt + 128], scalar=CLAMP,
                                        in1=srcw, op0=ALU.min, op1=ALU.add),
                                        ([(ev_t, tv)] + at(ev_a3)) if wt == 0 else (),
                                        inc=ev_v3, self_ser=False)
                        vv = ev_v3.val
                        b.op("scalar", lambda e, par=par: e.activation(
                            out=s_pt[:, 2 * par, :], in_=psA[par][:, 0:384], func=AF.Exp, scale=0.125),
                            [(ev_v3, vv)], inc=ev_a3, self_ser=False)
                        av2 = ev_a3.val
                        h = 2 * hp
                        for wt in range(3):
                            if c - wt < 0:
                                continue
                            slot_w = (c - wt) % RING
                            b.op("tensor", lambda e, wt=wt, slot_w=slot_w, par=par, h=h: e.matmul(
                                pav[:, 128 * par:128 * par + 65],
                                s_pt[:, 2 * par, 128 * wt:128 * wt + 128],
                                vtm[:, slot_w, h, 0:65],
                                start=(wt == 0), stop=(wt == min(2, c))),
                                ([(ev_a3, av2)] + at(ev_v3)) if wt == 0 else (), inc=ev_t)
                        tv2 = ev_t.val
                        b.op("vector", lambda e, h=h, par=par: e.tensor_copy(
                            out=s_acc[:, h, :], in_=pav[:, 128 * par:128 * par + 64]),
                            [(ev_t, tv2)], inc=ev_v3, self_ser=False)
                        b.op("vector", lambda e, h=h, par=par: e.tensor_copy(
                            out=s_den[:, h:h + 1], in_=pav[:, 128 * par + 64:128 * par + 65]),
                            at(ev_v3), inc=ev_v3, self_ser=False)
                        accv = ev_v3.val
                        b.op("scalar", lambda e, par=par: e.activation(
                            out=s_pt[:, 2 * par + 1, :], in_=psB[par][:, 0:384], func=AF.Exp, scale=0.125),
                            at(ev_v3) if False else (), inc=ev_a3, self_ser=False)
                        av2 = ev_a3.val
                        h = 2 * hp + 1
                        for wt in range(3):
                            if c - wt < 0:
                                continue
                            slot_w = (c - wt) % RING
                            b.op("tensor", lambda e, wt=wt, slot_w=slot_w, par=par, h=h: e.matmul(
                                pav[:, 256 + 128 * par:256 + 128 * par + 65],
                                s_pt[:, 2 * par + 1, 128 * wt:128 * wt + 128],
                                vtm[:, slot_w, h, 0:65],
                                start=(wt == 0), stop=(wt == min(2, c))),
                                [(ev_a3, av2), (ev_v3, accv)] if wt == 0 else (), inc=ev_t)
                        tv2 = ev_t.val
                        b.op("vector", lambda e, h=h, par=par: e.tensor_copy(
                            out=s_acc[:, h, :], in_=pav[:, 256 + 128 * par:256 + 128 * par + 64]),
                            [(ev_t, tv2)] + at(ev_v3), inc=ev_v3, self_ser=False)
                        b.op("vector", lambda e, h=h, par=par: e.tensor_copy(
                            out=s_den[:, h:h + 1], in_=pav[:, 256 + 128 * par + 64:256 + 128 * par + 65]),
                            at(ev_v3), inc=ev_v3, self_ser=False)

                    accv = ev_v3.val
                    for bi, m in enumerate(SP_M):
                        if m > c:
                            continue
                        slot_s = (c - m) % RING
                        pview = s_psp[:].rearrange("p (h w) -> p h w", w=8)[:, :, bi:bi + 1] \
                            .broadcast_to([128, 16, 64])
                        b.op("vector", lambda e, slot_s=slot_s, pview=pview: e.tensor_tensor(
                            out=s_sprod[:].rearrange("p (h w) -> p h w", w=64),
                            in0=vtm[:, slot_s, :, 0:64], in1=pview, op=ALU.mult),
                            [(ev_a3, spv), (ev_p, ev_p.val), (ev_v3, accv)], inc=ev_v3, self_ser=False)
                        vv = ev_v3.val
                        b.op("gpsimd", lambda e: e.tensor_tensor(
                            out=s_acc[:].rearrange("p h w -> p (h w)"),
                            in0=s_acc[:].rearrange("p h w -> p (h w)"), in1=s_sprod[:],
                            op=ALU.add), [(ev_v3, vv)], inc=ev_p)
                    b.op("vector", lambda e: e.tensor_reduce(
                        out=s_stats[:, 0:16],
                        in_=s_psp[:].rearrange("p (h w) -> p h w", w=8),
                        axis=AX.X, op=ALU.add), [(ev_a3, spv)] + at(ev_v3), inc=ev_v3, self_ser=False)
                    b.op("vector", lambda e: e.tensor_tensor(
                        out=s_den[:], in0=s_den[:], in1=s_stats[:, 0:16], op=ALU.add),
                        at(ev_v3), inc=ev_v3, self_ser=False)
                    b.op("vector", lambda e: e.reciprocal(out=s_den[:], in_=s_den[:]),
                        at(ev_v3), inc=ev_v3, self_ser=False)
                    pvv = ev_p.val
                    b.op("vector", lambda e, tt=tt: e.tensor_tensor(
                        out=s_attn[:, tt, :].rearrange("p (h w) -> p h w", w=64),
                        in0=s_acc[:], in1=s_den[:].unsqueeze(2).broadcast_to([128, 16, 64]),
                        op=ALU.mult), [(ev_p, pvv)] + at(st2[0], st2[1], ev_v3),
                        inc=ev_v3, self_ser=False)
                b.dma("gpsimd", attn_tm_h[2 * ci:2 * ci + 2].rearrange("t p d -> p t d"), s_attn[:],
                      at(ev_v3), inc=st2[pp])

            p2_A(0)
            for ci in range(NCH):
                if ci + 1 < NCH:
                    p2_A(ci + 1)
                p2_B(ci)

            b.wait("sync", at(dma_sp[0], dma_sp[1]))
            b.barrier(DMAS, extra=ALLGP)

        with ExitStack() as p3:
            sb3 = lambda name, shape, dt: p3.enter_context(nc.sbuf_tensor(name, shape, dt))
            ps3 = lambda name: p3.enter_context(nc.psum_tensor(name, [128, 512], F32))
            s_gatew = sb3("s_gatew", [128, 8, D], BF16)
            s_outw = sb3("s_outw", [128, 8, D], BF16)
            gb_bc = sb3("gb_bc", [128, D], F32)
            ob_bc = sb3("ob_bc", [128, D], F32)
            g2t = sb3("g2t", [128, 8], F32)
            b2t = sb3("b2t", [128, 8], F32)
            s_xn1c = sb3("s_xn1b", [128, 2, 8, CH], BF16)
            s_at = sb3("s_at", [128, 2, 2, D], BF16)
            s_xnt = sb3("s_xnt", [128, 2, 2, D], BF16)
            s_gate = sb3("s_gate", [128, 2, D], BF16)
            s_agfm = sb3("s_agfm", [128, 8, CH], BF16)
            s_x2 = sb3("s_x2", [128, 2, D], BF16)
            s_xh2 = sb3("s_xh2", [128, 2, D], BF16)
            s_xh2fm = sb3("s_xh2fm", [128, 8, CH], BF16)
            s_xn2fm = sb3("s_xn2fm", [128, 8, CH], BF16)
            s_tmp3v = sb3("s_tmp3", [128, 512], F32)
            pgt = ps3("pgt")
            po = ps3("po")
            ptr3_t = [p3.enter_context(nc.psum_tensor(f"ptr3{i}", [128, 1024], BF16)) for i in range(2)]
            ptr3 = [p[:, 0:128] for p in ptr3_t]

            b.dma("gpsimd", s_gatew[:], gatew_h.rearrange("(c p) f -> p c f", p=128), inc=dma_g)
            b.dma("gpsimd", s_outw[:], outw_h.rearrange("(c p) f -> p c f", p=128), inc=dma_g)
            with ExitStack() as pro:
                rowbuf3v = pro.enter_context(nc.sbuf_tensor("rowbuf3", [1, 4 * D], BF16))
                b.dma("gpsimd", rowbuf3v[0:1, 0:D], gateb_h.unsqueeze(0), inc=dma_g)
                b.dma("gpsimd", rowbuf3v[0:1, D:2 * D], outb_h.unsqueeze(0), inc=dma_g)
                b.dma("gpsimd", rowbuf3v[0:1, 2 * D:3 * D], ln2g_h.unsqueeze(0), inc=dma_g)
                b.dma("gpsimd", rowbuf3v[0:1, 3 * D:4 * D], ln2b_h.unsqueeze(0), inc=dma_g)
                gv = dma_g.val
                for half in range(2):
                    b.op("tensor", lambda e, half=half: e.matmul(
                        pgt[:, 0:512], ones_row[0:1, :], rowbuf3v[0:1, 512 * half:512 * half + 512],
                        start=True, stop=True), [(dma_g, gv)] + at(ev_a), inc=ev_t)
                    b.op("scalar", lambda e, half=half: e.copy(
                        out=gb_bc[:, 512 * half:512 * half + 512], in_=pgt[:, 0:512]),
                        at(ev_t), inc=ev_a)
                    b.op("tensor", lambda e, half=half: e.matmul(
                        pgt[:, 0:512], ones_row[0:1, :],
                        rowbuf3v[0:1, D + 512 * half:D + 512 * half + 512],
                        start=True, stop=True), at(ev_a), inc=ev_t)
                    b.op("scalar", lambda e, half=half: e.copy(
                        out=ob_bc[:, 512 * half:512 * half + 512], in_=pgt[:, 0:512]),
                        at(ev_t), inc=ev_a)
                for i in range(8):
                    b.op("tensor", lambda e, i=i: e.matmul(
                        pgt[:, i:i + 1], rowbuf3v[0:1, 2 * D + 128 * i:2 * D + 128 * i + 128],
                        one1[0:1, 0:1], start=True, stop=True),
                        at(ev_a) if i == 0 else (), inc=ev_t)
                    b.op("tensor", lambda e, i=i: e.matmul(
                        pgt[:, 8 + i:9 + i], rowbuf3v[0:1, 3 * D + 128 * i:3 * D + 128 * i + 128],
                        one1[0:1, 0:1], start=True, stop=True), inc=ev_t)
                b.op("scalar", lambda e: e.copy(out=g2t[:], in_=pgt[:, 0:8]), at(ev_t), inc=ev_a)
                b.op("scalar", lambda e: e.copy(out=b2t[:], in_=pgt[:, 8:16]), inc=ev_a)
            p3pro = at(dma_g, ev_a)

            for ci in range(NCH):
                pp = ci % 2
                b.dma("gpsimd", s_xn1c[:, pp, :, :],
                      xn1_fm_h[:, :, CH * ci:CH * (ci + 1)].rearrange("f p t -> p f t"),
                      (p3pro if ci == 0 else []) + at(ev_t, ev_v, ev_a), inc=xl3[pp])
                b.dma("gpsimd", s_at[:, pp, :, :],
                      attn_tm_h[2 * ci:2 * ci + 2].rearrange("t p d -> p t d"), inc=xl3[pp])
                b.dma("gpsimd", s_xnt[:, pp, :, :],
                      xn_tm_h[2 * ci:2 * ci + 2].rearrange("t p d -> p t d"), inc=xl3[pp])
                xldw = at(xl3[pp]) + at(st3[0], st3[1])
                b.wait("vector", xldw)
                b.wait("scalar", xldw)
                for tt in range(2):
                    for half in range(2):
                        for dc in range(8):
                            b.op("tensor", lambda e, tt=tt, half=half, dc=dc, pp=pp: e.matmul(
                                pgt[:, 0:512], s_xn1c[:, pp, dc, 128 * tt:128 * tt + 128],
                                s_gatew[:, dc, 512 * half:512 * half + 512],
                                start=(dc == 0), stop=(dc == 7)),
                                (xldw + at(ev_v, ev_a)) if dc == 0 else (), inc=ev_t)
                        tv = ev_t.val
                        b.op("vector", lambda e, half=half: e.scalar_tensor_tensor(
                            out=s_tmp3v[:], in0=pgt[:, 0:512], scalar=0.0,
                            in1=gb_bc[:, 512 * half:512 * half + 512], op0=ALU.add, op1=ALU.add),
                            [(ev_t, tv)], inc=ev_v)
                        b.op("scalar", lambda e, tt=tt, half=half: e.activation(
                            out=s_gate[:, tt, 512 * half:512 * half + 512], in_=s_tmp3v[:],
                            func=AF.Sigmoid), at(ev_v), inc=ev_a)
                    b.op("vector", lambda e, tt=tt, pp=pp: e.tensor_tensor(
                        out=s_at[:, pp, tt, :], in0=s_at[:, pp, tt, :], in1=s_gate[:, tt, :],
                        op=ALU.mult), at(ev_a), inc=ev_v)
                    vv = ev_v.val
                    for ft in range(8):
                        sl = (8 * tt + ft) % 2
                        w = [(ev_v, vv)] + (at(ev_v, ev_a) if (8 * tt + ft) >= 2 else [])
                        b.op("tensor", lambda e, tt=tt, ft=ft, sl=sl, pp=pp: e.transpose(
                            ptr3[sl], s_at[:, pp, tt, 128 * ft:128 * ft + 128], s_id[:]),
                            w, inc=ev_t)
                        tvw = ev_t.val
                        if ft % 2 == 0:
                            b.op("vector", lambda e, tt=tt, ft=ft, sl=sl: e.tensor_copy(
                                out=s_agfm[:, ft, 128 * tt:128 * tt + 128], in_=ptr3[sl]),
                                [(ev_t, tvw)], inc=ev_v)
                        else:
                            b.op("scalar", lambda e, tt=tt, ft=ft, sl=sl: e.copy(
                                out=s_agfm[:, ft, 128 * tt:128 * tt + 128], in_=ptr3[sl]),
                                [(ev_t, tvw)], inc=ev_a)
                trv = at(ev_v, ev_a)
                for tt in range(2):
                    for half in range(2):
                        for afc in range(8):
                            b.op("tensor", lambda e, tt=tt, half=half, afc=afc: e.matmul(
                                po[:, 0:512], s_agfm[:, afc, 128 * tt:128 * tt + 128],
                                s_outw[:, afc, 512 * half:512 * half + 512],
                                start=(afc == 0), stop=(afc == 7)),
                                (trv + at(ev_v)) if afc == 0 else (), inc=ev_t)
                        tv = ev_t.val
                        b.op("vector", lambda e, half=half: e.scalar_tensor_tensor(
                            out=s_tmp3v[:], in0=po[:, 0:512], scalar=0.0,
                            in1=ob_bc[:, 512 * half:512 * half + 512], op0=ALU.add, op1=ALU.add),
                            [(ev_t, tv)], inc=ev_v)
                        b.op("vector", lambda e, tt=tt, half=half, pp=pp: e.tensor_tensor(
                            out=s_x2[:, tt, 512 * half:512 * half + 512], in0=s_tmp3v[:],
                            in1=s_xnt[:, pp, tt, 512 * half:512 * half + 512], op=ALU.add), inc=ev_v)
                    ln_stats(s_x2[:, tt, :], ln1=False)
                    ln_apply(s_xh2[:, tt, :], s_x2[:, tt, :], 4, 5)
                    vv = ev_v.val
                    for ft in range(8):
                        sl = ft % 2
                        w = [(ev_v, vv)] + at(ev_v, ev_a)
                        b.op("tensor", lambda e, tt=tt, ft=ft, sl=sl: e.transpose(
                            ptr3[sl], s_xh2[:, tt, 128 * ft:128 * ft + 128], s_id[:]),
                            w, inc=ev_t)
                        tvw = ev_t.val
                        if ft % 2 == 0:
                            b.op("vector", lambda e, tt=tt, ft=ft, sl=sl: e.tensor_copy(
                                out=s_xh2fm[:, ft, 128 * tt:128 * tt + 128], in_=ptr3[sl]),
                                [(ev_t, tvw)], inc=ev_v)
                        else:
                            b.op("scalar", lambda e, tt=tt, ft=ft, sl=sl: e.copy(
                                out=s_xh2fm[:, ft, 128 * tt:128 * tt + 128], in_=ptr3[sl]),
                                [(ev_t, tvw)], inc=ev_a)
                for ft in range(8):
                    b.op("scalar", lambda e, ft=ft: e.activation(
                        out=s_xn2fm[:, ft, :], in_=s_xh2fm[:, ft, :], func=AF.Identity,
                        bias=b2t[:, ft:ft + 1], scale=g2t[:, ft:ft + 1]),
                        at(ev_v, ev_a) if ft == 0 else (), inc=ev_a)
                b.dma("gpsimd", x2_tm_h[2 * ci:2 * ci + 2].rearrange("t p d -> p t d"), s_x2[:],
                      at(ev_v), inc=st3[pp])
                b.dma("gpsimd", xn2_fm_h[:, :, CH * ci:CH * (ci + 1)].rearrange("f p t -> p f t"),
                      s_xn2fm[:], at(ev_a), inc=st3[pp])
                if "xh2_fm" in debug:
                    b.dma("gpsimd", xh2_fm_h[:, :, CH * ci:CH * (ci + 1)].rearrange("f p t -> p f t"),
                          s_xh2fm[:], at(dma_s, dma_a), inc=dma_g)

            b.barrier(DMAS, extra=ALLGP)

        with ExitStack() as p4:
            sb4 = lambda name, shape, dt: p4.enter_context(nc.sbuf_tensor(name, shape, dt))
            ps4 = lambda name: p4.enter_context(nc.psum_tensor(name, [128, 512], F32))
            s_fc1w = sb4("s_fc1w", [128, 8, FF], BF16)
            s_fc2w = sb4("s_fc2w", [128, 32, D], BF16)
            f1b_t = sb4("f1b_t", [128, 32], F32)
            f2b_bc = sb4("f2b_bc", [128, D], F32)
            s_xn2 = sb4("s_xn2b", [128, 2, 8, CH], BF16)
            s_x2t = sb4("s_x2t", [128, 2, 2, D], BF16)
            s_xf = sb4("s_xf", [128, 2, 2, D], F32)
            s_gt = sb4("s_gt", [128, 2, 2, D], BF16)
            s_gel = sb4("s_gel", [128, 3, CH], BF16)
            s_s = sb4("s_s", [128, 2, D], BF16)
            s_sg4v = sb4("s_sg4", [128, 1024], F32)
            s_out = sb4("s_out", [128, 2, D], F32)
            s_tmp4v = sb4("s_tmp4", [128, 512], F32)
            pf1a = ps4("pf1a")
            pf1b = ps4("pf1b")
            pf2 = [ps4(f"pf2_{i}") for i in range(4)]

            b.dma("gpsimd", s_fc1w[:], fc1w_h.rearrange("(c p) f -> p c f", p=128), inc=dma_g)
            b.dma("gpsimd", s_fc2w[:], fc2w_h.rearrange("(c p) f -> p c f", p=128), inc=dma_g)
            with ExitStack() as pro:
                rowbuf4v = pro.enter_context(nc.sbuf_tensor("rowbuf4", [1, FF + D], BF16))
                b.dma("gpsimd", rowbuf4v[0:1, 0:FF], fc1b_h.unsqueeze(0), inc=dma_g)
                b.dma("gpsimd", rowbuf4v[0:1, FF:FF + D], fc2b_h.unsqueeze(0), inc=dma_g)
                gv = dma_g.val
                for i in range(32):
                    b.op("tensor", lambda e, i=i: e.matmul(
                        pf1a[:, i:i + 1], rowbuf4v[0:1, 128 * i:128 * i + 128], one1[0:1, 0:1],
                        start=True, stop=True), [(dma_g, gv)] if i == 0 else (), inc=ev_t)
                b.op("scalar", lambda e: e.copy(out=f1b_t[:], in_=pf1a[:, 0:32]), at(ev_t), inc=ev_a)
                for half in range(2):
                    b.op("tensor", lambda e, half=half: e.matmul(
                        pf1a[:, 0:512], ones_row[0:1, :],
                        rowbuf4v[0:1, FF + 512 * half:FF + 512 * half + 512],
                        start=True, stop=True), at(ev_a), inc=ev_t)
                    b.op("scalar", lambda e, half=half: e.copy(
                        out=f2b_bc[:, 512 * half:512 * half + 512], in_=pf1a[:, 0:512]),
                        at(ev_t), inc=ev_a)
            p4pro = at(dma_g, ev_a)

            ev_v4 = b.sem("p4v2")
            ev_a4 = b.sem("p4a2")
            ld4 = [b.sem("p4l0"), b.sem("p4l1")]
            ld_ready = {}

            def p4_A(ci):
                pp = ci % 2
                w = (p4pro if ci == 0 else []) + at(ev_t, ev_v4)
                b.dma("gpsimd", s_xn2[:, pp, :, :],
                      xn2_fm_h[:, :, CH * ci:CH * (ci + 1)].rearrange("f p t -> p f t"),
                      w, inc=ld4[pp])
                b.dma("gpsimd", s_x2t[:, pp, :, :],
                      x2_tm_h[2 * ci:2 * ci + 2].rearrange("t p d -> p t d"), inc=ld4[pp])
                b.dma("gpsimd", s_xf[:, pp, :, :],
                      x_h.rearrange("t p d -> p t d")[:, 2 * ci:2 * ci + 2, :], inc=ld4[pp])
                b.dma("gpsimd", s_gt[:, pp, :, :],
                      g_tm_h[2 * ci:2 * ci + 2].rearrange("t p d -> p t d"), inc=ld4[pp])
                ld_ready[ci] = [(ld4[pp], ld4[pp].val)]

            def p4_B(ci):
                pp = ci % 2
                for hf in range(32):
                    pf1 = pf1a if hf % 2 == 0 else pf1b
                    gl = hf % 3
                    for dc in range(8):
                        w = ()
                        if dc == 0:
                            w = at(ev_a4) + (ld_ready[ci] if hf == 0 else [])
                        b.op("tensor", lambda e, hf=hf, dc=dc, pp=pp, pf1=pf1: e.matmul(
                            pf1[:, 0:CH], s_fc1w[:, dc, 128 * hf:128 * hf + 128], s_xn2[:, pp, dc, :],
                            start=(dc == 0), stop=(dc == 7)), w, inc=ev_t)
                    tv = ev_t.val
                    b.op("scalar", lambda e, hf=hf, gl=gl, pf1=pf1: e.activation(
                        out=s_gel[:, gl, :], in_=pf1[:, 0:CH], func=AF.Gelu,
                        bias=f1b_t[:, hf:hf + 1], scale=1.0),
                        [(ev_t, tv)], inc=ev_a4, self_ser=False)
                    av = ev_a4.val
                    for tt in range(2):
                        for half in range(2):
                            w = ()
                            if tt == 0 and half == 0:
                                w = [(ev_a4, av)] + (at(ev_v4) if hf == 0 else [])
                            b.op("tensor", lambda e, tt=tt, half=half, hf=hf, gl=gl: e.matmul(
                                pf2[2 * tt + half][:, 0:512],
                                s_gel[:, gl, 128 * tt:128 * tt + 128],
                                s_fc2w[:, hf, 512 * half:512 * half + 512],
                                start=(hf == 0), stop=(hf == 31)), w, inc=ev_t)
                tv = ev_t.val
                for tt in range(2):
                    for half in range(2):
                        w = [(ev_t, tv)] + at(ev_v4)
                        if tt == 0 and half == 0:
                            w += ld_ready[ci] + at(st4[0], st4[1])
                        b.op("vector", lambda e, tt=tt, half=half: e.scalar_tensor_tensor(
                            out=s_tmp4v[:], in0=pf2[2 * tt + half][:, 0:512], scalar=0.0,
                            in1=f2b_bc[:, 512 * half:512 * half + 512], op0=ALU.add, op1=ALU.add),
                            w, inc=ev_v4, self_ser=False)
                        b.op("vector", lambda e, tt=tt, half=half, pp=pp: e.tensor_tensor(
                            out=s_s[:, tt, 512 * half:512 * half + 512], in0=s_tmp4v[:],
                            in1=s_x2t[:, pp, tt, 512 * half:512 * half + 512], op=ALU.add),
                            at(ev_v4), inc=ev_v4, self_ser=False)
                    b.op("vector", lambda e, tt=tt, pp=pp: e.tensor_tensor(
                        out=s_sg4v[:], in0=s_s[:, tt, :], in1=s_gt[:, pp, tt, :], op=ALU.mult),
                        at(ev_v4), inc=ev_v4, self_ser=False)
                    b.op("vector", lambda e, tt=tt, pp=pp: e.tensor_tensor(
                        out=s_out[:, tt, :], in0=s_sg4v[:], in1=s_xf[:, pp, tt, :], op=ALU.add),
                        at(ev_v4), inc=ev_v4, self_ser=False)
                b.dma("gpsimd", out_h[2 * ci:2 * ci + 2].rearrange("t p d -> p t d"), s_out[:],
                      at(ev_v4), inc=st4[pp])

            p4_A(0)
            p4_A(1)
            for ci in range(NCH):
                p4_B(ci)
                if ci + 2 < NCH:
                    p4_A(ci + 2)

        b.wait("gpsimd", at(dma_g, ld4[0], ld4[1], *ALLGP))
        b.wait("sync", at(dma_g, dma_s))
        b.wait("scalar", at(dma_a))

        block = ctx.enter_context(nc.Block())
        b.emit(block)
    return nc


def kernel(**inputs):
    x = np.ascontiguousarray(inputs["L0_attn_x"], dtype=np.float32)
    pb = np.ascontiguousarray(inputs["L0_attn_pos_bias"], dtype=np.float32)
    consts = {"pb_win": host_pb_win(pb), "pb_sp": host_pb_sparse(pb),
              "ident": np.eye(128, dtype=np.float32)}
    base = {
        "qkv_w": inputs["L0_attn_qkv_w"], "qkv_b": inputs["L0_attn_qkv_b"],
        "gate_w": inputs["L0_attn_gate_w"], "gate_b": inputs["L0_attn_gate_b"],
        "out_w": inputs["L0_attn_out_w"], "out_b": inputs["L0_attn_out_b"],
        "ln1_g": inputs["L0_attn_ln1_g"], "ln1_b": inputs["L0_attn_ln1_b"],
        "ln2_g": inputs["L0_attn_ln2_g"], "ln2_b": inputs["L0_attn_ln2_b"],
        "fc1_w": inputs["L0_attn_fc1_w"], "fc1_b": inputs["L0_attn_fc1_b"],
        "fc2_w": inputs["L0_attn_fc2_w"], "fc2_b": inputs["L0_attn_fc2_b"],
        "w_gate": inputs["L0_ffn_w_gate"], "w_up": inputs["L0_ffn_w_up"],
        "w_down": inputs["L0_ffn_w_down"],
    }
    base = {k: np.ascontiguousarray(v) for k, v in base.items()}
    nc = build_nc(debug=DEBUG)
    in_maps = []
    for core in range(8):
        m = {"x": np.ascontiguousarray(x[core].reshape(TT, 128, D))}
        m.update(base)
        m.update(consts)
        in_maps.append(m)
    res = run_bass_kernel_spmd(nc, in_maps, core_ids=list(range(8)))
    out = np.stack([res.results[i]["out"].reshape(NTOK, D) for i in range(8)], axis=0)
    return out.astype(np.float32)
